# version 12
# baseline (speedup 1.0000x reference)
import numpy as np
import ml_dtypes
from contextlib import ExitStack
import concourse.bass as bass
import concourse.mybir as mybir
from concourse.bass_utils import run_bass_kernel_spmd

F32 = mybir.dt.float32
BF16 = mybir.dt.bfloat16
AX = mybir.AxisListType.X
ALU = mybir.AluOpType
ACT = mybir.ActivationFunctionType

D = 2048
NH = 16
QLORA = 682
KVL = 256
ROPE = 64
SCALE = float(1.0 / np.sqrt(128.0))
THETA = 128000.0
DI = 8192
CC = 512
EPS_LN = 1e-5
EPS_RMS = 1e-6
NEG = -1e9

ENGS = ("sync", "gpsimd", "scalar", "vector", "tensor")


class Pr:

    NP = 16

    def __init__(self, nc, es):
        self.nc = nc
        self.th = {e: [] for e in ENGS}
        self.sems = {}
        self.cnt = {}
        self.waited = {e: {} for e in ENGS}
        self.dma_idx = 0
        names = ["t", "v", "s"] + [f"d{i}" for i in range(self.NP)]
        for name in names:
            h = es.enter_context(nc.semaphore(name))
            self.sems[name] = h
            self.cnt[name] = 0
            for e in ENGS:
                self.waited[e][name] = 0

    def tok(self, name):
        return (name, self.cnt[name])

    def op(self, eng, fn, inc=(), wait=()):
        for (s, v) in wait:
            if v > self.waited[eng][s]:
                self.waited[eng][s] = v
                h = self.sems[s]
                self.th[eng].append(lambda e, h=h, v=v: e.wait_ge(h, v))
        incs = []
        for (s, k) in inc:
            self.cnt[s] += k
            incs.append((self.sems[s], k))

        def thunk(e, fn=fn, incs=incs):
            r = fn(e)
            for (h, k) in incs:
                r = r.then_inc(h, k)
        self.th[eng].append(thunk)
        return self.tok(inc[0][0]) if inc else None

    def dma(self, eng, fn, wait=()):
        s = f"d{self.dma_idx % self.NP}"
        self.dma_idx += 1
        self.raw_wait(eng, (s, self.cnt[s]))
        return self.op(eng, fn, inc=[(s, 16)], wait=wait)

    def raw_wait(self, eng, tok):
        s, v = tok
        if v > self.waited[eng][s]:
            self.waited[eng][s] = v
            h = self.sems[s]
            self.th[eng].append(lambda e, h=h, v=v: e.wait_ge(h, v))

    def flush(self):
        nonempty = [e for e in ENGS if self.th[e]]
        if not nonempty:
            return
        with self.nc.Block() as blk:
            for e in nonempty:
                getattr(blk, e)(lambda eng, L=list(self.th[e]): [t(eng) for t in L])
        self.th = {e: [] for e in ENGS}


def _split(pr, src, hi, lo, inc=(), wait=()):
    pr.op("vector", lambda e, s=src, h=hi: e.tensor_copy(h, s), wait=wait)
    return pr.op("vector", lambda e, s=src, h=hi, l=lo: e.tensor_tensor(l, s, h, ALU.subtract),
                 inc=inc)


def _mm3(pr, ps, lh, ll, rh, rl, start, stop, inc=(), wait=()):
    pr.op("tensor", lambda e: e.matmul(ps, lh, rh, start=start, stop=False), wait=wait)
    pr.op("tensor", lambda e: e.matmul(ps, lh, rl, start=False, stop=False))
    return pr.op("tensor", lambda e: e.matmul(ps, ll, rh, start=False, stop=stop),
                 inc=inc)


def build(S=8192):
    import os
    PHASES = int(os.environ.get("K_PHASES", "5"))
    NHALF = int(os.environ.get("K_HALF", "0"))
    NCH = int(os.environ.get("K_CHUNKS", "0"))
    TAP = os.environ.get("K_TAP", "")
    TT = S // 128
    SLOTS = TT // 8
    NTOK = SLOTS * 128
    HALVES = 2 if SLOTS >= 2 else 1
    HSLOTS = SLOTS // HALVES
    THW = HSLOTS * 128
    NCHUNK = DI // CC

    nc = bass.Bass("TRN2", target_bir_lowering=False, detect_race_conditions=False)
    es = ExitStack()
    A = es.enter_context

    x_hbm = nc.dram_tensor("x", [TT, 128, D], F32, kind="ExternalInput").ap()
    xo_hbm = nc.dram_tensor("x_own", [SLOTS, 128, D], F32, kind="ExternalInput").ap()
    cosf_hbm = nc.dram_tensor("cos_f", [TT, 128, 32], F32, kind="ExternalInput").ap()
    sinf_hbm = nc.dram_tensor("sin_f", [TT, 128, 32], F32, kind="ExternalInput").ap()
    cso_hbm = nc.dram_tensor("cs_o", [128, 2 * SLOTS * 32], F32, kind="ExternalInput").ap()
    mask_hbm = nc.dram_tensor("mask", [128, SLOTS * 8 * 128], BF16, kind="ExternalInput").ap()
    qa_hbm = nc.dram_tensor("qa_w", [D, QLORA], F32, kind="ExternalInput").ap()
    qaln_hbm = nc.dram_tensor("qa_ln", [128, 6], F32, kind="ExternalInput").ap()
    qb_hbm = nc.dram_tensor("qb_w", [QLORA, D], F32, kind="ExternalInput").ap()
    kva_hbm = nc.dram_tensor("kva_w", [D, KVL + ROPE], F32, kind="ExternalInput").ap()
    kvaln_hbm = nc.dram_tensor("kva_ln", [128, 2], F32, kind="ExternalInput").ap()
    kvb_hbm = nc.dram_tensor("kvb_w", [KVL, NH * 192], F32, kind="ExternalInput").ap()
    ow_hbm = nc.dram_tensor("o_w", [D, D], F32, kind="ExternalInput").ap()
    wg_hbm = nc.dram_tensor("w_gate", [D, DI], F32, kind="ExternalInput").ap()
    wu_hbm = nc.dram_tensor("w_up", [D, DI], F32, kind="ExternalInput").ap()
    wd_hbm = nc.dram_tensor("w_down", [DI, D], F32, kind="ExternalInput").ap()
    out_hbm = nc.dram_tensor("out", [SLOTS, 128, D], F32, kind="ExternalOutput").ap()
    spk = "ExternalOutput" if TAP else "Internal"
    outsp_hbm = nc.dram_tensor("outT_sp", [NH, 2, 128, NTOK], BF16, kind=spk).ap()
    x1sp_hbm = nc.dram_tensor("x1_sp", [SLOTS, 128, D], F32, kind=spk).ap()
    t2sp_hbm = nc.dram_tensor("t2_sp", [SLOTS, 128, D], F32, kind=spk).ap()

    pr = Pr(nc, es)

    ident = A(nc.sbuf_tensor("ident", [128, 128], F32)).ap()
    qaln_sb = A(nc.sbuf_tensor("qaln_sb", [128, 6], F32)).ap()
    kvaln_sb = A(nc.sbuf_tensor("kvaln_sb", [128, 2], F32)).ap()
    s_sum = A(nc.sbuf_tensor("s_sum", [128, 1], F32)).ap()
    s_mean = A(nc.sbuf_tensor("s_mean", [128, 1], F32)).ap()
    s_ssq = A(nc.sbuf_tensor("s_ssq", [128, 1], F32)).ap()
    s_sd = A(nc.sbuf_tensor("s_sd", [128, 1], F32)).ap()
    s_rs = A(nc.sbuf_tensor("s_rs", [128, 1], F32)).ap()
    c_epsln = A(nc.sbuf_tensor("c_epsln", [128, 1], F32)).ap()
    c_epsrms = A(nc.sbuf_tensor("c_epsrms", [128, 1], F32)).ap()
    es_attn = ExitStack()
    AA = es_attn.enter_context
    cnT_h = AA(nc.sbuf_tensor("cnT_h", [128, 2 * S], BF16)).ap()
    cnT_l = AA(nc.sbuf_tensor("cnT_l", [128, 2 * S], BF16)).ap()
    kfT_h = AA(nc.sbuf_tensor("kfT_h", [128, S], BF16)).ap()
    kfT_l = AA(nc.sbuf_tensor("kfT_l", [128, S], BF16)).ap()
    qnT_h = AA(nc.sbuf_tensor("qnT_h", [128, 6 * NTOK], BF16)).ap()
    qnT_l = AA(nc.sbuf_tensor("qnT_l", [128, 6 * NTOK], BF16)).ap()
    cs_own = AA(nc.sbuf_tensor("cs_own", [128, 2 * SLOTS * 32], F32)).ap()

    EPS_AP = {}

    def emit_taps():
        if not TAP:
            return
        taps = {"cnT_h": (cnT_h, BF16, 2 * S), "cnT_l": (cnT_l, BF16, 2 * S),
                "kfT_h": (kfT_h, BF16, S), "kfT_l": (kfT_l, BF16, S),
                "qnT_h": (qnT_h, BF16, 6 * NTOK), "qnT_l": (qnT_l, BF16, 6 * NTOK)}
        for nm in TAP.split(","):
            if nm not in taps:
                continue
            ap, dt, w = taps[nm]
            th = nc.dram_tensor("tap_" + nm, [128, w], dt, kind="ExternalOutput").ap()
            tk = pr.dma("sync", lambda e, ap=ap, th=th: e.dma_start(th, ap))
            pr.raw_wait("sync", tk)

    def ln_inplace(x_ap, sqbuf, width, eps):
        pr.raw_wait("vector", pr.tok("s"))
        t1 = pr.op("vector", lambda e: e.reduce_sum(s_sum, x_ap, AX), inc=[("v", 1)])
        pr.raw_wait("vector", t1)
        t2 = pr.op("vector", lambda e: e.tensor_scalar_mul(s_mean, s_sum, 1.0 / width),
                   inc=[("v", 1)])
        pr.raw_wait("vector", t2)
        pr.op("vector", lambda e: e.tensor_scalar(x_ap, x_ap, s_mean, None, ALU.subtract))
        pr.op("vector", lambda e: e.tensor_mul(sqbuf, x_ap, x_ap))
        tv = pr.op("vector", lambda e: e.reduce_sum(s_ssq, sqbuf, AX), inc=[("v", 1)])
        eps_ap = EPS_AP[eps]
        ts = pr.op("scalar", lambda e: e.activation(s_sd, s_ssq, ACT.Sqrt,
                                                    bias=eps_ap, scale=1.0 / width),
                   inc=[("s", 1)], wait=[tv])
        tr = pr.op("vector", lambda e: e.reciprocal(s_rs, s_sd), inc=[("v", 1)], wait=[ts])
        return pr.op("vector", lambda e: e.tensor_scalar_mul(x_ap, x_ap, s_rs),
                     inc=[("v", 1)], wait=[tr])

    EPS_AP[EPS_LN] = c_epsln
    EPS_AP[EPS_RMS] = c_epsrms
    pr.op("vector", lambda e: e.memset(c_epsln, EPS_LN))
    pr.op("vector", lambda e: e.memset(c_epsrms, EPS_RMS))
    tmz = pr.op("vector", lambda e: e.memset(ident, 0.0), inc=[("v", 1)])
    pr.op("gpsimd", lambda e: e.affine_select(
        out=ident, in_=ident, compare_op=ALU.not_equal, fill=1.0, base=0,
        pattern=[[-1, 128]], channel_multiplier=1), wait=[tmz])
    pr.dma("sync", lambda e: e.dma_start(cs_own, cso_hbm))
    pr.dma("sync", lambda e: e.dma_start(qaln_sb, qaln_hbm))
    pr.dma("sync", lambda e: e.dma_start(kvaln_sb, kvaln_hbm))
    pr.flush()

    with ExitStack() as e1:
        B = e1.enter_context
        kva_st = [B(nc.sbuf_tensor(f"kva_st{j}", [128, 320], F32)).ap() for j in range(2)]
        kva_h = B(nc.sbuf_tensor("kva_h", [128, 16 * 320], BF16)).ap()
        kva_l = B(nc.sbuf_tensor("kva_l", [128, 16 * 320], BF16)).ap()
        xs = [B(nc.sbuf_tensor(f"xs{j}", [128, D], F32)).ap() for j in range(2)]
        sqb = B(nc.sbuf_tensor("sqb", [128, D], F32)).ap()
        tT_h = B(nc.sbuf_tensor("tT_h", [128, 16 * 128], BF16)).ap()
        tT_l = B(nc.sbuf_tensor("tT_l", [128, 16 * 128], BF16)).ap()
        csb = [B(nc.sbuf_tensor(f"csb{j}", [128, 64], F32)).ap() for j in range(2)]
        ctmp = B(nc.sbuf_tensor("ctmp", [128, KVL], F32)).ap()
        ckv_s = B(nc.sbuf_tensor("ckv_s", [128, KVL], F32)).ap()
        sq2 = B(nc.sbuf_tensor("sq2_b1", [128, KVL], F32)).ap()
        kp = B(nc.sbuf_tensor("kp", [128, 64], F32)).ap()
        tp_in = B(nc.sbuf_tensor("tp_in", [128, 128], F32)).ap()
        rtmp = B(nc.sbuf_tensor("rtmp", [128, 32], F32)).ap()
        ps_t = [B(nc.psum_tensor(f"ps_t{j}", [128, 512], F32)).ap() for j in range(2)]
        ps_kv = B(nc.psum_tensor("ps_kv", [128, 320], F32)).ap()
        ps_ct = B(nc.psum_tensor("ps_ct", [128, 384], F32)).ap()

        pr.op("vector", lambda e: e.memset(tp_in[:, 0:64], 0.0))
        kst_war = [pr.tok("v"), pr.tok("v")]
        for kc in range(16):
            wb = kc % 2
            tw = pr.dma("sync", lambda e, kc=kc, wb=wb: e.dma_start(
                kva_st[wb], kva_hbm[kc * 128:(kc + 1) * 128, :]), wait=[kst_war[wb]])
            pr.raw_wait("vector", tw)
            pr.op("vector", lambda e, kc=kc, wb=wb: e.tensor_copy(
                kva_h[:, kc * 320:(kc + 1) * 320], kva_st[wb]))
            kst_war[wb] = pr.op("vector", lambda e, kc=kc, wb=wb: e.tensor_tensor(
                kva_l[:, kc * 320:(kc + 1) * 320], kva_st[wb],
                kva_h[:, kc * 320:(kc + 1) * 320], ALU.subtract), inc=[("v", 1)])
        tkva = pr.tok("v")

        xfree = [pr.tok("t"), pr.tok("t")]
        csfree = [pr.tok("v"), pr.tok("v")]
        psfree_t = [pr.tok("v"), pr.tok("v")]
        psfree_kv = pr.tok("v")
        psfree_ct = pr.tok("v")
        tT_war = pr.tok("t")
        for i in range(TT):
            b = i % 2
            tx = pr.dma("sync", lambda e, i=i, b=b: e.dma_start(xs[b], x_hbm[i]),
                        wait=[xfree[b]])
            tc1 = pr.dma("sync", lambda e, i=i, b=b: e.dma_start(csb[b][:, 0:32], cosf_hbm[i]),
                         wait=[csfree[b]])
            tc2 = pr.dma("sync", lambda e, i=i, b=b: e.dma_start(csb[b][:, 32:64], sinf_hbm[i]))
            tcs = [tc1, tc2]
            pr.raw_wait("vector", tx)
            tln = ln_inplace(xs[b], sqb, D, EPS_LN)
            tev = None
            for g in range(4):
                pj = g % 2
                for q in range(4):
                    kc = g * 4 + q
                    w = [tln, psfree_t[pj], tT_war] if q == 0 else ()
                    tt = pr.op("tensor", lambda e, b=b, kc=kc, q=q, pj=pj: e.transpose(
                        ps_t[pj][:, q * 128:(q + 1) * 128],
                        xs[b][:, kc * 128:(kc + 1) * 128], ident),
                        inc=[("t", 1)] if q == 3 else (), wait=w)
                c0 = g * 512
                ths = pr.op("scalar", lambda e, c0=c0, pj=pj: e.activation(
                    tT_h[:, c0:c0 + 512], ps_t[pj], ACT.Copy), inc=[("s", 1)], wait=[tt, tkva])
                pr.raw_wait("vector", ths)
                tev = pr.op("vector", lambda e, c0=c0, pj=pj: e.tensor_tensor(
                    tT_l[:, c0:c0 + 512], ps_t[pj], tT_h[:, c0:c0 + 512], ALU.subtract),
                    inc=[("v", 1)])
                psfree_t[pj] = tev
            xfree[b] = pr.tok("t")
            for kc in range(16):
                tmm = _mm3(pr, ps_kv,
                           tT_h[:, kc * 128:(kc + 1) * 128], tT_l[:, kc * 128:(kc + 1) * 128],
                           kva_h[:, kc * 320:(kc + 1) * 320], kva_l[:, kc * 320:(kc + 1) * 320],
                           start=(kc == 0), stop=(kc == 15),
                           inc=[("t", 1)], wait=[tev, psfree_kv] if kc == 0 else ())
            tT_war = tmm
            pr.raw_wait("vector", tmm)
            pr.raw_wait("vector", pr.tok("s"))
            pr.op("vector", lambda e: e.tensor_copy(ckv_s, ps_kv[:, 0:KVL]))
            pr.op("vector", lambda e: e.tensor_copy(kp, ps_kv[:, KVL:KVL + 64]))
            psfree_kv = pr.op("vector", lambda e: e.engine_nop(), inc=[("v", 1)])
            pr.op("vector", lambda e: e.tensor_mul(sq2[:, 0:KVL], ckv_s, ckv_s))
            tv = pr.op("vector", lambda e: e.reduce_sum(s_ssq, sq2[:, 0:KVL], AX), inc=[("v", 1)])
            ts = pr.op("scalar", lambda e: e.activation(s_sd, s_ssq, ACT.Sqrt,
                                                        bias=c_epsrms, scale=1.0 / KVL),
                       inc=[("s", 1)], wait=[tv])
            tr = pr.op("vector", lambda e: e.reciprocal(s_rs, s_sd), inc=[("v", 1)], wait=[ts])
            pr.raw_wait("vector", tr)
            pr.op("vector", lambda e: e.tensor_scalar_mul(ctmp, ckv_s, s_rs),
                  wait=[psfree_ct])
            for _t in tcs:
                pr.raw_wait("vector", _t)
            pr.op("vector", lambda e, b=b: e.tensor_mul(tp_in[:, 64:96], kp[:, 0:64:2], csb[b][:, 0:32]))
            pr.op("vector", lambda e, b=b: e.tensor_mul(rtmp, kp[:, 1:64:2], csb[b][:, 32:64]))
            pr.op("vector", lambda e: e.tensor_sub(tp_in[:, 64:96], tp_in[:, 64:96], rtmp))
            pr.op("vector", lambda e, b=b: e.tensor_mul(tp_in[:, 96:128], kp[:, 1:64:2], csb[b][:, 0:32]))
            pr.op("vector", lambda e, b=b: e.tensor_mul(rtmp, kp[:, 0:64:2], csb[b][:, 32:64]))
            tro = pr.op("vector", lambda e: e.tensor_add(tp_in[:, 96:128], tp_in[:, 96:128], rtmp),
                        inc=[("v", 1)])
            csfree[b] = tro
            for j in range(2):
                pr.op("tensor", lambda e, j=j: e.transpose(
                    ps_ct[:, j * 128:(j + 1) * 128], ctmp[:, j * 128:(j + 1) * 128], ident),
                    wait=[tro] if j == 0 else ())
            tct = pr.op("tensor", lambda e: e.transpose(ps_ct[:, 256:384], tp_in, ident),
                        inc=[("t", 1)])
            pr.raw_wait("vector", tct)
            pr.raw_wait("scalar", tct)
            for ch in range(2):
                ths = pr.op("scalar", lambda e, ch=ch, i=i: e.activation(
                    cnT_h[:, ch * S + i * 128: ch * S + (i + 1) * 128],
                    ps_ct[:, ch * 128:(ch + 1) * 128], ACT.Copy), inc=[("s", 1)])
                pr.raw_wait("vector", ths)
                pr.op("vector", lambda e, ch=ch, i=i: e.tensor_tensor(
                    cnT_l[:, ch * S + i * 128: ch * S + (i + 1) * 128],
                    ps_ct[:, ch * 128:(ch + 1) * 128],
                    cnT_h[:, ch * S + i * 128: ch * S + (i + 1) * 128], ALU.subtract))
            pr.raw_wait("vector", pr.tok("s"))
            pr.op("vector", lambda e, i=i: e.tensor_copy(
                kfT_h[64:128, i * 128:(i + 1) * 128], ps_ct[64:128, 256:384]))
            psfree_ct = pr.op("vector", lambda e, i=i: e.tensor_tensor(
                kfT_l[64:128, i * 128:(i + 1) * 128], ps_ct[64:128, 256:384],
                kfT_h[64:128, i * 128:(i + 1) * 128], ALU.subtract), inc=[("v", 1)])
        pr.flush()

    if PHASES < 2:
        emit_taps()
        pr.flush()
        return nc, es
    with ExitStack() as e2:
        B = e2.enter_context
        qa_st = [B(nc.sbuf_tensor(f"qa_st{j}", [128, QLORA], F32)).ap() for j in range(2)]
        qa_h = B(nc.sbuf_tensor("qa_h", [128, 16 * QLORA], BF16)).ap()
        qa_l = B(nc.sbuf_tensor("qa_l", [128, 16 * QLORA], BF16)).ap()
        xq = B(nc.sbuf_tensor("xq", [128, D], F32)).ap()
        sqb = B(nc.sbuf_tensor("sqb2", [128, D], F32)).ap()
        tT_h = B(nc.sbuf_tensor("tT2_h", [128, 16 * 128], BF16)).ap()
        tT_l = B(nc.sbuf_tensor("tT2_l", [128, 16 * 128], BF16)).ap()
        qn = B(nc.sbuf_tensor("qn", [128, QLORA], F32)).ap()
        sq2 = B(nc.sbuf_tensor("sq2_b2", [128, QLORA], F32)).ap()
        ps_t = [B(nc.psum_tensor(f"ps2_t{j}", [128, 512], F32)).ap() for j in range(2)]
        ps_qa = B(nc.psum_tensor("ps_qa", [128, 512], F32)).ap()
        ps_qb = B(nc.psum_tensor("ps_qb", [128, 170], F32)).ap()

        qst_war = [pr.tok("v"), pr.tok("v")]
        for kc in range(16):
            wb = kc % 2
            tw = pr.dma("sync", lambda e, kc=kc, wb=wb: e.dma_start(
                qa_st[wb], qa_hbm[kc * 128:(kc + 1) * 128, :]), wait=[qst_war[wb]])
            pr.raw_wait("vector", tw)
            pr.op("vector", lambda e, kc=kc, wb=wb: e.tensor_copy(
                qa_h[:, kc * QLORA:(kc + 1) * QLORA], qa_st[wb]))
            qst_war[wb] = pr.op("vector", lambda e, kc=kc, wb=wb: e.tensor_tensor(
                qa_l[:, kc * QLORA:(kc + 1) * QLORA], qa_st[wb],
                qa_h[:, kc * QLORA:(kc + 1) * QLORA], ALU.subtract), inc=[("v", 1)])
        tqa = pr.tok("v")

        psfree_t = [pr.tok("v"), pr.tok("v")]
        psfree_q = pr.tok("v")
        tT_war = pr.tok("t")
        xfree = pr.tok("t")
        for o in range(SLOTS):
            tx = pr.dma("sync", lambda e, o=o: e.dma_start(xq, xo_hbm[o]), wait=[xfree])
            pr.raw_wait("vector", tx)
            tln = ln_inplace(xq, sqb, D, EPS_LN)
            for g in range(4):
                pj = g % 2
                for q in range(4):
                    kc = g * 4 + q
                    w = [tln, psfree_t[pj], tT_war] if q == 0 else ()
                    tt = pr.op("tensor", lambda e, kc=kc, q=q, pj=pj: e.transpose(
                        ps_t[pj][:, q * 128:(q + 1) * 128],
                        xq[:, kc * 128:(kc + 1) * 128], ident),
                        inc=[("t", 1)] if q == 3 else (), wait=w)
                c0 = g * 512
                ths = pr.op("scalar", lambda e, c0=c0, pj=pj: e.activation(
                    tT_h[:, c0:c0 + 512], ps_t[pj], ACT.Copy), inc=[("s", 1)], wait=[tt, tqa])
                pr.raw_wait("vector", ths)
                tev = pr.op("vector", lambda e, c0=c0, pj=pj: e.tensor_tensor(
                    tT_l[:, c0:c0 + 512], ps_t[pj], tT_h[:, c0:c0 + 512], ALU.subtract),
                    inc=[("v", 1)])
                psfree_t[pj] = tev
            xfree = pr.tok("t")
            for kc in range(16):
                _mm3(pr, ps_qa,
                     tT_h[:, kc * 128:(kc + 1) * 128], tT_l[:, kc * 128:(kc + 1) * 128],
                     qa_h[:, kc * QLORA:kc * QLORA + 512], qa_l[:, kc * QLORA:kc * QLORA + 512],
                     start=(kc == 0), stop=(kc == 15),
                     inc=[("t", 1)], wait=[tev, psfree_q] if kc == 0 else ())
                tmm = _mm3(pr, ps_qb,
                           tT_h[:, kc * 128:(kc + 1) * 128], tT_l[:, kc * 128:(kc + 1) * 128],
                           qa_h[:, kc * QLORA + 512:(kc + 1) * QLORA],
                           qa_l[:, kc * QLORA + 512:(kc + 1) * QLORA],
                           start=(kc == 0), stop=(kc == 15), inc=[("t", 1)])
            tT_war = tmm
            xfree = tmm
            pr.raw_wait("vector", tmm)
            pr.raw_wait("vector", pr.tok("s"))
            pr.op("vector", lambda e: e.tensor_copy(qn[:, 0:512], ps_qa))
            psfree_q = pr.op("vector", lambda e: e.tensor_copy(qn[:, 512:QLORA], ps_qb),
                             inc=[("v", 1)])
            pr.op("vector", lambda e: e.tensor_mul(sq2[:, 0:QLORA], qn[:, 0:QLORA], qn[:, 0:QLORA]))
            tv = pr.op("vector", lambda e: e.reduce_sum(s_ssq, sq2[:, 0:QLORA], AX), inc=[("v", 1)])
            ts = pr.op("scalar", lambda e: e.activation(s_sd, s_ssq, ACT.Sqrt,
                                                        bias=c_epsrms, scale=1.0 / QLORA),
                       inc=[("s", 1)], wait=[tv])
            tr = pr.op("vector", lambda e: e.reciprocal(s_rs, s_sd), inc=[("v", 1)], wait=[ts])
            pr.raw_wait("vector", tr)
            tql = pr.op("vector", lambda e: e.tensor_scalar_mul(qn[:, 0:QLORA], qn[:, 0:QLORA], s_rs),
                        inc=[("v", 1)])
            psfree_q = tql
            for qc in range(6):
                wd = min(128, QLORA - qc * 128)
                pj = qc % 2
                tt = pr.op("tensor", lambda e, qc=qc, wd=wd, pj=pj: e.transpose(
                    ps_t[pj][:wd, 0:128], qn[:, qc * 128:qc * 128 + wd], ident),
                    inc=[("t", 1)], wait=[psfree_q, psfree_t[pj]])
                ths = pr.op("scalar", lambda e, qc=qc, wd=wd, pj=pj, o=o: e.activation(
                    qnT_h[:wd, qc * NTOK + o * 128: qc * NTOK + (o + 1) * 128],
                    ps_t[pj][:wd, 0:128], ACT.Copy), inc=[("s", 1)], wait=[tt])
                pr.raw_wait("vector", ths)
                psfree_t[pj] = pr.op("vector", lambda e, qc=qc, wd=wd, pj=pj, o=o: e.tensor_tensor(
                    qnT_l[:wd, qc * NTOK + o * 128: qc * NTOK + (o + 1) * 128], ps_t[pj][:wd, 0:128],
                    qnT_h[:wd, qc * NTOK + o * 128: qc * NTOK + (o + 1) * 128], ALU.subtract),
                    inc=[("v", 1)])
        pr.flush()

    if PHASES < 3:
        emit_taps()
        pr.flush()
        return nc, es
    with ExitStack() as e3:
        B = e3.enter_context
        mask_sb = B(nc.sbuf_tensor("mask_sb", [128, SLOTS * 8 * 128], BF16)).ap()
        kvb_st = B(nc.sbuf_tensor("kvb_st", [128, 384], F32)).ap()
        kvb_h = B(nc.sbuf_tensor("kvb_h", [128, 384], BF16)).ap()
        kvb_l = B(nc.sbuf_tensor("kvb_l", [128, 384], BF16)).ap()
        qb_st = B(nc.sbuf_tensor("qb_st", [128, 768], F32)).ap()
        qb_h = B(nc.sbuf_tensor("qb_h", [128, 768], BF16)).ap()
        qb_l = B(nc.sbuf_tensor("qb_l", [128, 768], BF16)).ap()
        qT_h = B(nc.sbuf_tensor("qT_h", [128, NTOK], BF16)).ap()
        qT_l = B(nc.sbuf_tensor("qT_l", [128, NTOK], BF16)).ap()
        v_h = B(nc.sbuf_tensor("v_h", [128, S], BF16)).ap()
        v_l = B(nc.sbuf_tensor("v_l", [128, S], BF16)).ap()
        outT_h = B(nc.sbuf_tensor("outT_h", [128, NTOK], BF16)).ap()
        outT_l = B(nc.sbuf_tensor("outT_l", [128, NTOK], BF16)).ap()
        rtmp = B(nc.sbuf_tensor("rtmp3", [128, 32], F32)).ap()
        GW = min(4, SLOTS)
        probs = [B(nc.sbuf_tensor(f"probs{j}", [128, 512], F32)).ap() for j in range(GW)]
        qf = probs[0][:, 0:128]
        tp_q = probs[min(1, GW - 1)][:, 0:128]
        pt_h = [B(nc.sbuf_tensor(f"pt_h{j}", [128, 512], BF16)).ap() for j in range(2)]
        pt_l = [B(nc.sbuf_tensor(f"pt_l{j}", [128, 512], BF16)).ap() for j in range(2)]
        o_acc = [B(nc.sbuf_tensor(f"o_acc{j}", [128, 128], F32)).ap() for j in range(GW)]
        o_out = B(nc.sbuf_tensor("o_out", [128, 128], F32)).ap()
        negm = [[B(nc.sbuf_tensor(f"negm{j}_{k}", [128, 1], F32)).ap() for k in range(2)]
                for j in range(GW)]
        negmx = [B(nc.sbuf_tensor(f"negmx{j}", [128, 1], F32)).ap() for j in range(GW)]
        dm = [B(nc.sbuf_tensor(f"dm{j}", [128, 1], F32)).ap() for j in range(GW)]
        fac = [B(nc.sbuf_tensor(f"fac{j}", [128, 1], F32)).ap() for j in range(GW)]
        esum = [B(nc.sbuf_tensor(f"esum{j}", [128, 1], F32)).ap() for j in range(GW)]
        srun = [B(nc.sbuf_tensor(f"srun{j}", [128, 1], F32)).ap() for j in range(GW)]
        rinv = B(nc.sbuf_tensor("rinv", [128, 1], F32)).ap()
        ps_sc = [B(nc.psum_tensor(f"ps_sc{j}", [128, 512], F32)).ap() for j in range(2)]
        ps_pt = [B(nc.psum_tensor(f"ps_pt{j}", [128, 512], F32)).ap() for j in range(2)]
        ps_v = [B(nc.psum_tensor(f"ps_v{j}", [128, 512], F32)).ap() for j in range(2)]
        ps_av = [B(nc.psum_tensor(f"ps_av{j}", [128, 128], F32)).ap() for j in range(2)]

        pr.op("vector", lambda e: e.memset(qb_st, 0.0))
        tmk = pr.dma("sync", lambda e: e.dma_start(mask_sb, mask_hbm))
        pr.raw_wait("vector", tmk)
        psfree_v = [pr.tok("v"), pr.tok("v")]
        kfT_war = pr.tok("t")
        v_war = pr.tok("t")
        qT_war = pr.tok("t")
        stg_war = pr.tok("v")
        out_war = []
        spill_toks = []
        scfree = [pr.tok("s"), pr.tok("s")]
        ptfree = [pr.tok("v"), pr.tok("v")]
        avfree = [pr.tok("v"), pr.tok("v")]
        probs_war = [pr.tok("t")] * GW
        pt_war = [pr.tok("t"), pr.tok("t")]
        for h in range(NH):
            kvb_toks = [pr.dma("scalar", lambda e, ch=ch, h=h: e.dma_start(
                kvb_st[:, ch * 192:(ch + 1) * 192],
                kvb_hbm[ch * 128:(ch + 1) * 128, h * 192:(h + 1) * 192]),
                wait=[stg_war] if ch == 0 else ()) for ch in range(2)]
            for _t in kvb_toks:
                pr.raw_wait("vector", _t)
            for ch in range(2):
                pr.op("vector", lambda e, ch=ch: e.tensor_scalar_mul(
                    kvb_st[:, ch * 192:(ch + 1) * 192], kvb_st[:, ch * 192:(ch + 1) * 192],
                    kvaln_sb[:, ch:ch + 1]))
            _split(pr, kvb_st, kvb_h, kvb_l)
            qb_toks = [pr.dma("scalar", lambda e, qc=qc, wd=min(128, QLORA - qc * 128), h=h:
                e.dma_start(qb_st[:wd, qc * 128:(qc + 1) * 128],
                            qb_hbm[qc * 128:qc * 128 + wd, h * 128:(h + 1) * 128]))
                for qc in range(6)]
            for _t in qb_toks:
                pr.raw_wait("vector", _t)
            for qc in range(6):
                pr.op("vector", lambda e, qc=qc: e.tensor_scalar_mul(
                    qb_st[:, qc * 128:(qc + 1) * 128], qb_st[:, qc * 128:(qc + 1) * 128],
                    qaln_sb[:, qc:qc + 1]))
            stg_war = _split(pr, qb_st, qb_h, qb_l, inc=[("v", 1)])
            tstg = stg_war
            psfree_q2 = pr.tok("v")
            for o in range(SLOTS):
                for qc in range(6):
                    wd = min(128, QLORA - qc * 128)
                    tmm = _mm3(pr, ps_av[0],
                               qnT_h[:wd, qc * NTOK + o * 128:qc * NTOK + (o + 1) * 128],
                               qnT_l[:wd, qc * NTOK + o * 128:qc * NTOK + (o + 1) * 128],
                               qb_h[:wd, qc * 128:(qc + 1) * 128], qb_l[:wd, qc * 128:(qc + 1) * 128],
                               start=(qc == 0), stop=(qc == 5),
                               inc=[("t", 1)],
                               wait=[tstg, qT_war, psfree_q2, avfree[0]] if qc == 0 else ())
                pr.raw_wait("vector", tmm)
                pr.op("vector", lambda e: e.tensor_scalar_mul(qf, ps_av[0], SCALE))
                avfree[0] = pr.op("vector", lambda e: e.engine_nop(), inc=[("v", 1)])
                co = o * 32
                so = SLOTS * 32 + o * 32
                pr.op("vector", lambda e: e.tensor_copy(tp_q[:, 0:64], qf[:, 0:64]))
                pr.op("vector", lambda e, co=co: e.tensor_mul(tp_q[:, 64:96], qf[:, 64:128:2], cs_own[:, co:co + 32]))
                pr.op("vector", lambda e, so=so: e.tensor_mul(rtmp, qf[:, 65:128:2], cs_own[:, so:so + 32]))
                pr.op("vector", lambda e: e.tensor_sub(tp_q[:, 64:96], tp_q[:, 64:96], rtmp))
                pr.op("vector", lambda e, co=co: e.tensor_mul(tp_q[:, 96:128], qf[:, 65:128:2], cs_own[:, co:co + 32]))
                pr.op("vector", lambda e, so=so: e.tensor_mul(rtmp, qf[:, 64:128:2], cs_own[:, so:so + 32]))
                tq2 = pr.op("vector", lambda e: e.tensor_add(tp_q[:, 96:128], tp_q[:, 96:128], rtmp),
                            inc=[("v", 1)])
                tt = pr.op("tensor", lambda e: e.transpose(ps_av[1], tp_q, ident),
                           inc=[("t", 1)], wait=[tq2, avfree[1]])
                pr.raw_wait("scalar", tt)
                ths = pr.op("scalar", lambda e, o=o: e.activation(
                    qT_h[:, o * 128:(o + 1) * 128], ps_av[1], ACT.Copy), inc=[("s", 1)])
                pr.raw_wait("vector", ths)
                psfree_q2 = pr.op("vector", lambda e, o=o: e.tensor_tensor(
                    qT_l[:, o * 128:(o + 1) * 128], ps_av[1], qT_h[:, o * 128:(o + 1) * 128],
                    ALU.subtract), inc=[("v", 1)])
                avfree[1] = psfree_q2
            tqT = pr.tok("v")
            for gi in range(TT // 4):
                kg = gi
                kb = kg % 2
                for ch in range(2):
                    tmm = _mm3(pr, ps_pt[kb][0:64, :],
                               kvb_h[:, ch * 192:ch * 192 + 64],
                               kvb_l[:, ch * 192:ch * 192 + 64],
                               cnT_h[:, ch * S + kg * 512: ch * S + (kg + 1) * 512],
                               cnT_l[:, ch * S + kg * 512: ch * S + (kg + 1) * 512],
                               start=(ch == 0), stop=(ch == 1),
                               inc=[("t", 1)],
                               wait=[stg_war, kfT_war, ptfree[kb]] if ch == 0 else ())
                pr.raw_wait("scalar", tmm)
                ths = pr.op("scalar", lambda e, kg=kg, kb=kb: e.activation(
                    kfT_h[0:64, kg * 512:(kg + 1) * 512], ps_pt[kb][0:64, :], ACT.Copy),
                    inc=[("s", 1)])
                pr.raw_wait("vector", ths)
                ptfree[kb] = pr.op("vector", lambda e, kg=kg, kb=kb: e.tensor_tensor(
                    kfT_l[0:64, kg * 512:(kg + 1) * 512], ps_pt[kb][0:64, :],
                    kfT_h[0:64, kg * 512:(kg + 1) * 512], ALU.subtract), inc=[("v", 1)])
                vg = gi
                vb = vg % 2
                for k4 in range(4):
                    kt = vg * 4 + k4
                    for ch in range(2):
                        w = ()
                        if k4 == 0 and ch == 0:
                            w = [psfree_v[vb]] + ([v_war, tstg] if vg == 0 else [])
                        tmm = _mm3(pr, ps_v[vb][:, k4 * 128:(k4 + 1) * 128],
                                   cnT_h[:, ch * S + kt * 128: ch * S + (kt + 1) * 128],
                                   cnT_l[:, ch * S + kt * 128: ch * S + (kt + 1) * 128],
                                   kvb_h[:, ch * 192 + 64:(ch + 1) * 192],
                                   kvb_l[:, ch * 192 + 64:(ch + 1) * 192],
                                   start=(ch == 0), stop=(ch == 1),
                                   inc=[("t", 1)], wait=w)
                pr.raw_wait("scalar", tmm)
                ths = pr.op("scalar", lambda e, vg=vg, vb=vb: e.activation(
                    v_h[:, vg * 512:(vg + 1) * 512], ps_v[vb], ACT.Copy), inc=[("s", 1)])
                pr.raw_wait("vector", ths)
                psfree_v[vb] = pr.op("vector", lambda e, vg=vg, vb=vb: e.tensor_tensor(
                    v_l[:, vg * 512:(vg + 1) * 512], ps_v[vb],
                    v_h[:, vg * 512:(vg + 1) * 512], ALU.subtract), inc=[("v", 1)])
            tkn = pr.tok("v")
            tvx = pr.tok("v")
            for p in range(SLOTS // GW):
                oo = [GW * p + i for i in range(GW)]
                KGs = [2 * o + 2 for o in oo]
                tini = [None] * GW
                texp = [None] * GW
                tscv = [None] * GW
                fac_war = [pr.tok("v")] * GW
                tsr = [pr.tok("v")] * GW
                for idx in range(GW):
                    pr.op("vector", lambda e, idx=idx: e.memset(negm[idx][1], 1e30))
                    pr.op("vector", lambda e, idx=idx: e.memset(srun[idx], 0.0))
                    tini[idx] = pr.op("vector", lambda e, idx=idx: e.memset(o_acc[idx], 0.0),
                                      inc=[("v", 1)])
                for kg in range(KGs[GW - 1]):
                    live = [i for i in range(GW) if kg < KGs[i]]
                    for idx in live:
                        o = oo[idx]
                        KG = KGs[idx]
                        sb = (kg * GW + idx) % 2
                        tsc = _mm3(pr, ps_sc[sb],
                                   qT_h[:, o * 128:(o + 1) * 128], qT_l[:, o * 128:(o + 1) * 128],
                                   kfT_h[:, kg * 512:(kg + 1) * 512], kfT_l[:, kg * 512:(kg + 1) * 512],
                                   start=True, stop=True,
                                   inc=[("t", 1)],
                                   wait=[tqT, tkn, scfree[sb]] if kg == 0 else [scfree[sb]])
                        kfT_war = tsc
                        qT_war = tsc
                        tscv[idx] = (tsc, sb)
                        pr.raw_wait("vector", tsc)
                        if kg >= KG - 2:
                            mo = (o * 8 + (kg - (KG - 2)) * 4) * 128
                            pr.op("vector", lambda e, sb=sb, mo=mo: e.tensor_add(
                                ps_sc[sb], ps_sc[sb], mask_sb[:, mo:mo + 512]))
                        tmx = pr.op("vector", lambda e, idx=idx, sb=sb: e.reduce_max(
                            negmx[idx], ps_sc[sb], AX, negate=True),
                            inc=[("v", 1)], wait=[tini[idx]] if kg == 0 else ())
                        pr.raw_wait("vector", tmx)
                        pr.op("vector", lambda e, idx=idx, kg=kg: e.tensor_scalar(
                            dm[idx], negmx[idx], negm[idx][1 - kg % 2], negm[idx][1 - kg % 2],
                            ALU.min, ALU.subtract))
                        tvm = pr.op("vector", lambda e, idx=idx, kg=kg: e.tensor_single_scalar(
                            negm[idx][kg % 2], negmx[idx], negm[idx][1 - kg % 2], ALU.min),
                            inc=[("v", 1)])
                        pr.op("scalar", lambda e, idx=idx: e.activation(fac[idx], dm[idx], ACT.Exp),
                              inc=[("s", 1)], wait=[tvm, fac_war[idx]])
                        te = pr.op("scalar", lambda e, idx=idx, kg=kg, sb=sb: e.activation(
                            probs[idx], ps_sc[sb], ACT.Exp, bias=negm[idx][kg % 2],
                            accum_out=esum[idx]),
                            inc=[("s", 1)], wait=[probs_war[idx]])
                        scfree[sb] = te
                        texp[idx] = te
                        tsr[idx] = pr.op("vector", lambda e, idx=idx: e.scalar_tensor_tensor(
                            srun[idx], srun[idx], fac[idx], esum[idx], ALU.mult, ALU.add),
                            inc=[("v", 1)], wait=[te])
                    for idx in live:
                        o = oo[idx]
                        KG = KGs[idx]
                        te = texp[idx]
                        pb = (kg * GW + idx) % 2
                        ab = (kg * GW + idx) % 2
                        for j in range(4):
                            tt = pr.op("tensor", lambda e, j=j, pb=pb, idx=idx: e.transpose(
                                ps_pt[pb][:, j * 128:(j + 1) * 128],
                                probs[idx][:, j * 128:(j + 1) * 128], ident),
                                inc=[("t", 1)] if j == 3 else (),
                                wait=[te, ptfree[pb]] if j == 0 else ())
                        probs_war[idx] = tt
                        ths = pr.op("scalar", lambda e, pb=pb: e.activation(
                            pt_h[pb], ps_pt[pb], ACT.Copy), inc=[("s", 1)],
                            wait=[tt, pt_war[pb]])
                        pr.raw_wait("vector", ths)
                        tpt = pr.op("vector", lambda e, pb=pb: e.tensor_tensor(
                            pt_l[pb], ps_pt[pb], pt_h[pb], ALU.subtract), inc=[("v", 1)])
                        ptfree[pb] = tpt
                        for j in range(4):
                            tav = _mm3(pr, ps_av[ab],
                                       pt_h[pb][:, j * 128:(j + 1) * 128],
                                       pt_l[pb][:, j * 128:(j + 1) * 128],
                                       v_h[:, (kg * 4 + j) * 128:(kg * 4 + j + 1) * 128],
                                       v_l[:, (kg * 4 + j) * 128:(kg * 4 + j + 1) * 128],
                                       start=(j == 0), stop=(j == 3),
                                       inc=[("t", 1)],
                                       wait=[tpt, avfree[ab], tvx] if j == 0 else ())
                        pt_war[pb] = tav
                        v_war = tav
                        pr.raw_wait("vector", tav)
                        tacc = pr.op("vector", lambda e, idx=idx, ab=ab: e.scalar_tensor_tensor(
                            o_acc[idx], o_acc[idx], fac[idx], ps_av[ab], ALU.mult, ALU.add),
                            inc=[("v", 1)])
                        avfree[ab] = tacc
                        fac_war[idx] = tacc
                        if kg == KG - 1:
                            tr = pr.op("vector", lambda e, idx=idx: e.reciprocal(rinv, srun[idx]),
                                       inc=[("v", 1)], wait=[tsr[idx]])
                            pr.raw_wait("vector", tr)
                            tno = pr.op("vector", lambda e, idx=idx: e.tensor_scalar_mul(
                                o_out, o_acc[idx], rinv), inc=[("v", 1)])
                            tt = pr.op("tensor", lambda e, idx=idx, ab=ab: e.transpose(
                                ps_av[ab], o_out, ident), inc=[("t", 1)],
                                wait=[tno, avfree[ab]])
                            pr.raw_wait("scalar", tt)
                            ths = pr.op("scalar", lambda e, o=o, idx=idx, ab=ab: e.activation(
                                outT_h[:, o * 128:(o + 1) * 128], ps_av[ab], ACT.Copy),
                                inc=[("s", 1)], wait=out_war)
                            pr.raw_wait("vector", ths)
                            avfree[ab] = pr.op("vector", lambda e, o=o, idx=idx, ab=ab: e.tensor_tensor(
                                outT_l[:, o * 128:(o + 1) * 128], ps_av[ab],
                                outT_h[:, o * 128:(o + 1) * 128], ALU.subtract), inc=[("v", 1)])
            tev = pr.tok("v")
            sp1 = pr.dma("scalar", lambda e, h=h: e.dma_start(outsp_hbm[h, 0], outT_h),
                         wait=[tev])
            sp2 = pr.dma("scalar", lambda e, h=h: e.dma_start(outsp_hbm[h, 1], outT_l))
            out_war = [sp1, sp2]
            spill_toks.append((sp1, sp2))
            pr.flush()

    if PHASES < 4:
        emit_taps()
        pr.flush()
        es_attn.close()
        return nc, es
    es_attn.close()
    with ExitStack() as e4:
        B = e4.enter_context
        o_accF = B(nc.sbuf_tensor("o_accF", [128, 16 * NTOK], F32)).ap()
        ow_st = B(nc.sbuf_tensor("ow_st", [128, D], F32)).ap()
        ow_h = B(nc.sbuf_tensor("ow_h", [128, D], BF16)).ap()
        ow_l = B(nc.sbuf_tensor("ow_l", [128, D], BF16)).ap()
        rb_h = B(nc.sbuf_tensor("rb_h", [128, NTOK], BF16)).ap()
        rb_l = B(nc.sbuf_tensor("rb_l", [128, NTOK], BF16)).ap()
        otok = B(nc.sbuf_tensor("otok", [128, D], F32)).ap()
        xo = B(nc.sbuf_tensor("xo4", [128, D], F32)).ap()
        sqb = B(nc.sbuf_tensor("sqb4", [128, D], F32)).ap()
        ps_o = [B(nc.psum_tensor(f"ps_o{j}", [128, 512], F32)).ap() for j in range(2)]

        NW = min(512, NTOK)
        NN = NTOK // NW
        stg_war = pr.tok("v")
        rb_war = pr.tok("t")
        psfree_o = [pr.tok("v"), pr.tok("v")]
        for h in range(NH):
            t_ow = pr.dma("scalar", lambda e, h=h: e.dma_start(ow_st, ow_hbm[h * 128:(h + 1) * 128, :]),
                          wait=[stg_war])
            t_r1 = pr.dma("scalar", lambda e, h=h: e.dma_start(rb_h, outsp_hbm[h, 0]),
                          wait=[rb_war, spill_toks[h][0], spill_toks[h][1]])
            t_r2 = pr.dma("scalar", lambda e, h=h: e.dma_start(rb_l, outsp_hbm[h, 1]))
            stg_war = _split(pr, ow_st, ow_h, ow_l, inc=[("v", 1)], wait=[t_ow])
            pr.raw_wait("tensor", t_r1)
            pr.raw_wait("tensor", t_r2)
            for m in range(16):
                for n in range(NN):
                    pj = (m * NN + n) % 2
                    tmm = _mm3(pr, ps_o[pj][:, 0:NW],
                               ow_h[:, m * 128:(m + 1) * 128], ow_l[:, m * 128:(m + 1) * 128],
                               rb_h[:, n * NW:(n + 1) * NW], rb_l[:, n * NW:(n + 1) * NW],
                               start=True, stop=True,
                               inc=[("t", 1)], wait=[stg_war, psfree_o[pj]])
                    c0 = m * NTOK + n * NW
                    pr.raw_wait("vector", tmm)
                    if h == 0:
                        psfree_o[pj] = pr.op("vector", lambda e, c0=c0, pj=pj: e.tensor_copy(
                            o_accF[:, c0:c0 + NW], ps_o[pj][:, 0:NW]), inc=[("v", 1)])
                    else:
                        psfree_o[pj] = pr.op("vector", lambda e, c0=c0, pj=pj: e.tensor_add(
                            o_accF[:, c0:c0 + NW], o_accF[:, c0:c0 + NW], ps_o[pj][:, 0:NW]), inc=[("v", 1)])
            rb_war = tmm
        xo_war = ()
        x1sp_toks = [None] * SLOTS
        t2sp_toks = [None] * SLOTS
        for o in range(SLOTS):
            tx = pr.dma("sync", lambda e, o=o: e.dma_start(xo, xo_hbm[o]), wait=xo_war)
            for g in range(4):
                pj = g % 2
                for q in range(4):
                    m = g * 4 + q
                    tt = pr.op("tensor", lambda e, m=m, q=q, pj=pj, o=o: e.transpose(
                        ps_o[pj][:, q * 128:(q + 1) * 128],
                        o_accF[:, m * NTOK + o * 128: m * NTOK + (o + 1) * 128], ident),
                        inc=[("t", 1)] if q == 3 else (),
                        wait=[psfree_o[0], psfree_o[1]] if q == 0 else ())
                pr.raw_wait("vector", tt)
                psfree_o[pj] = pr.op("vector", lambda e, g=g, pj=pj: e.tensor_copy(
                    otok[:, g * 512:(g + 1) * 512], ps_o[pj]), inc=[("v", 1)])
            ln_inplace(otok, sqb, D, EPS_LN)
            pr.raw_wait("vector", tx)
            tadd = pr.op("vector", lambda e: e.tensor_add(xo, xo, otok), inc=[("v", 1)])
            t1 = pr.dma("scalar", lambda e, o=o: e.dma_start(x1sp_hbm[o], xo), wait=[tadd])
            x1sp_toks[o] = t1
            pr.raw_wait("vector", t1)
            ln_inplace(xo, sqb, D, EPS_LN)
            t2d = pr.dma("scalar", lambda e, o=o: e.dma_start(t2sp_hbm[o], xo),
                         wait=[pr.tok("v")])
            t2sp_toks[o] = t2d
            xo_war = (t2d,)
        pr.flush()

    if PHASES < 5:
        emit_taps()
        pr.flush()
        return nc, es
    with ExitStack() as e5:
        B = e5.enter_context
        t2T_h = B(nc.sbuf_tensor("t2T_h", [128, 16 * THW], BF16)).ap()
        t2T_l = B(nc.sbuf_tensor("t2T_l", [128, 16 * THW], BF16)).ap()
        accF = B(nc.sbuf_tensor("accF", [128, 16 * THW], F32)).ap()
        t2rb = B(nc.sbuf_tensor("t2rb", [128, D], F32)).ap()
        wst = [B(nc.sbuf_tensor(f"wst{j}", [128, CC], F32)).ap() for j in range(2)]
        wg_h = B(nc.sbuf_tensor("wgc_h", [128, 16 * CC], BF16)).ap()
        wg_l = B(nc.sbuf_tensor("wgc_l", [128, 16 * CC], BF16)).ap()
        wdst = B(nc.sbuf_tensor("wdst", [128, D], F32)).ap()
        wd_h = B(nc.sbuf_tensor("wdc_h", [128, 4 * D], BF16)).ap()
        wd_l = B(nc.sbuf_tensor("wdc_l", [128, 4 * D], BF16)).ap()
        gbuf = B(nc.sbuf_tensor("gbuf", [128, HSLOTS * CC], F32)).ap()
        ubuf2 = B(nc.sbuf_tensor("ubuf2", [128, HSLOTS * CC], F32)).ap()
        ubuf = [B(nc.sbuf_tensor(f"ubuf{j}", [128, HSLOTS * CC], F32)).ap() for j in range(2)]
        abuf = B(nc.sbuf_tensor("abuf", [128, CC], F32)).ap()
        cw = B(nc.sbuf_tensor("cw", [128, 10 * (CC // 4)], F32)).ap()
        hsT_h = B(nc.sbuf_tensor("hsT_h", [128, 4 * THW], BF16)).ap()
        hsT_l = B(nc.sbuf_tensor("hsT_l", [128, 4 * THW], BF16)).ap()
        x1rb = B(nc.sbuf_tensor("x1rb", [128, D], F32)).ap()
        if HSLOTS * CC >= D:
            sqb = gbuf[:, 0:D]
        else:
            sqb = B(nc.sbuf_tensor("sqb5", [128, D], F32)).ap()
        ps_g = [B(nc.psum_tensor(f"ps_g{j}", [128, 512], F32)).ap() for j in range(max(2, HSLOTS))]
        ps_p2 = B(nc.psum_tensor("ps_p2", [128, 512], F32)).ap()
        ps_d = [B(nc.psum_tensor(f"ps_d{j}", [128, 512], F32)).ap() for j in range(2)]

        NG = CC // 4
        out_toks = []
        tsilu = [None] * HSLOTS
        tspar = [[pr.tok("v")] * HSLOTS, [pr.tok("v")] * HSLOTS]
        ub_war = [[pr.tok("t")] * HSLOTS, [pr.tok("t")] * HSLOTS]
        t2rb_war = pr.tok("t")
        psfree_g = [pr.tok("s")] * max(2, HSLOTS)
        psfree_p2 = pr.tok("v")
        psfree_d = [pr.tok("v"), pr.tok("v")]
        wst_war = [pr.tok("v"), pr.tok("v")]
        wd_spl = pr.tok("v")
        wd_war = pr.tok("s")
        hsT_war = pr.tok("t")
        tkc_u = None
        x1_war = ()
        for half in range(NHALF if NHALF else HALVES):
            for o4 in range(HSLOTS):
                o = half * HSLOTS + o4
                tx = pr.dma("scalar", lambda e, o=o: e.dma_start(t2rb, t2sp_hbm[o]),
                            wait=[t2rb_war, t2sp_toks[o]])
                for g in range(4):
                    pj = g % 2
                    for q in range(4):
                        kc = g * 4 + q
                        tt = pr.op("tensor", lambda e, kc=kc, q=q, pj=pj: e.transpose(
                            ps_g[pj][:, q * 128:(q + 1) * 128],
                            t2rb[:, kc * 128:(kc + 1) * 128], ident),
                            inc=[("t", 1)] if q == 3 else (),
                            wait=[tx, psfree_g[pj]] if q == 0 else ())
                    pr.raw_wait("scalar", tt)
                    c0 = (g * 4) * THW + o4 * 128
                    hap = bass.AP(t2T_h.tensor, t2T_h.offset + c0, [[t2T_h.ap[0][0], 128], [THW, 4], [1, 128]])
                    lap = bass.AP(t2T_l.tensor, t2T_l.offset + c0, [[t2T_l.ap[0][0], 128], [THW, 4], [1, 128]])
                    ths = pr.op("scalar", lambda e, hap=hap, pj=pj: e.activation(
                        hap, ps_g[pj], ACT.Copy), inc=[("s", 1)])
                    pr.raw_wait("vector", ths)
                    tev = pr.op("vector", lambda e, hap=hap, lap=lap, pj=pj: e.tensor_tensor(
                        lap, ps_g[pj], hap, ALU.subtract), inc=[("v", 1)])
                    psfree_g[pj] = tev
                t2rb_war = tt
            tt2T = pr.tok("v")
            NCC = NCH if NCH else NCHUNK
            def emit_stage_mm(w_hbm, cc, tkc_prev, is_gate):
                tkc = [None] * 16
                for kc in range(16):
                    wb = kc % 2
                    tw = pr.dma("sync", lambda e, kc=kc, cc=cc, wb=wb: e.dma_start(
                        wst[wb], w_hbm[kc * 128:(kc + 1) * 128, cc * CC:(cc + 1) * CC]),
                        wait=[wst_war[wb]] + ([tkc_prev[kc]] if tkc_prev else []))
                    pr.raw_wait("scalar", tw)
                    if tkc_prev:
                        pr.raw_wait("scalar", tkc_prev[kc])
                    th = pr.op("scalar", lambda e, kc=kc, wb=wb: e.activation(
                        wg_h[:, kc * CC:(kc + 1) * CC], wst[wb], ACT.Copy),
                        inc=[("s", 1)])
                    pr.raw_wait("vector", th)
                    wst_war[wb] = pr.op("vector", lambda e, kc=kc, wb=wb: e.tensor_tensor(
                        wg_l[:, kc * CC:(kc + 1) * CC], wst[wb],
                        wg_h[:, kc * CC:(kc + 1) * CC], ALU.subtract), inc=[("v", 1)])
                    tspl = wst_war[wb]
                    for o4 in range(HSLOTS):
                        w = [tspl] if o4 == 0 else []
                        if kc == 0:
                            w = w + [tt2T, psfree_g[o4]]
                        tmm = _mm3(pr, ps_g[o4],
                                   t2T_h[:, kc * THW + o4 * 128: kc * THW + (o4 + 1) * 128],
                                   t2T_l[:, kc * THW + o4 * 128: kc * THW + (o4 + 1) * 128],
                                   wg_h[:, kc * CC:(kc + 1) * CC], wg_l[:, kc * CC:(kc + 1) * CC],
                                   start=(kc == 0), stop=(kc == 15),
                                   inc=[("t", 1)], wait=w)
                    tkc[kc] = tmm
                return tkc

            def emit_tail(cc):
                nonlocal psfree_p2, hsT_war, wd_spl, wd_war
                ub = ubuf[cc % 2]
                for j in range(4):
                    tw = pr.dma("sync", lambda e, j=j, cc=cc: e.dma_start(
                        wdst, wd_hbm[cc * CC + j * 128: cc * CC + (j + 1) * 128, :]),
                        wait=[wd_spl])
                    pr.raw_wait("scalar", tw)
                    pr.raw_wait("scalar", wd_war)
                    th = pr.op("scalar", lambda e, j=j: e.activation(
                        wd_h[:, j * D:(j + 1) * D], wdst, ACT.Copy), inc=[("s", 1)])
                    pr.raw_wait("vector", th)
                    wd_spl = pr.op("vector", lambda e, j=j: e.tensor_tensor(
                        wd_l[:, j * D:(j + 1) * D], wdst, wd_h[:, j * D:(j + 1) * D],
                        ALU.subtract), inc=[("v", 1)])
                for o4 in range(HSLOTS):
                    for j in range(4):
                        tt = pr.op("tensor", lambda e, j=j, o4=o4, ub=ub: e.transpose(
                            ps_p2[:, j * 128:(j + 1) * 128],
                            ub[:, o4 * CC + j * 128: o4 * CC + (j + 1) * 128], ident),
                            inc=[("t", 1)] if j == 3 else (),
                            wait=[tspar[cc % 2][o4], psfree_p2, hsT_war] if j == 0 else ())
                    ub_war[cc % 2][o4] = tt
                    pr.raw_wait("scalar", tt)
                    c0 = o4 * 128
                    hap = bass.AP(hsT_h.tensor, hsT_h.offset + c0, [[hsT_h.ap[0][0], 128], [THW, 4], [1, 128]])
                    lap = bass.AP(hsT_l.tensor, hsT_l.offset + c0, [[hsT_l.ap[0][0], 128], [THW, 4], [1, 128]])
                    ths = pr.op("scalar", lambda e, hap=hap: e.activation(
                        hap, ps_p2, ACT.Copy), inc=[("s", 1)])
                    pr.raw_wait("vector", ths)
                    tev = pr.op("vector", lambda e, hap=hap, lap=lap: e.tensor_tensor(
                        lap, ps_p2, hap, ALU.subtract), inc=[("v", 1)])
                    psfree_p2 = tev
                thsT = pr.tok("v")
                for m in range(16):
                    pj = m % 2
                    for j in range(4):
                        tmm = _mm3(pr, ps_d[pj][:, 0:THW],
                                   wd_h[:, j * D + m * 128: j * D + (m + 1) * 128],
                                   wd_l[:, j * D + m * 128: j * D + (m + 1) * 128],
                                   hsT_h[:, j * THW:(j + 1) * THW], hsT_l[:, j * THW:(j + 1) * THW],
                                   start=(j == 0), stop=(j == 3),
                                   inc=[("t", 1)],
                                   wait=[wd_spl, thsT, psfree_d[pj]] if j == 0 else ())
                    c0 = m * THW
                    pr.raw_wait("vector", tmm)
                    if cc == 0:
                        psfree_d[pj] = pr.op("vector", lambda e, c0=c0, pj=pj: e.tensor_copy(
                            accF[:, c0:c0 + THW], ps_d[pj][:, 0:THW]), inc=[("v", 1)])
                    else:
                        psfree_d[pj] = pr.op("vector", lambda e, c0=c0, pj=pj: e.tensor_add(
                            accF[:, c0:c0 + THW], accF[:, c0:c0 + THW], ps_d[pj][:, 0:THW]),
                            inc=[("v", 1)])
                hsT_war = tmm
                wd_war = tmm

            for cc in range(NCC):
                tkc_g2 = emit_stage_mm(wg_hbm, cc, tkc_u, True)
                for o4 in range(HSLOTS):
                    tsil = pr.op("scalar", lambda e, o4=o4: e.activation(
                        gbuf[:, o4 * CC:(o4 + 1) * CC], ps_g[o4], ACT.Silu),
                        inc=[("s", 1)], wait=[tkc_g2[15]])
                    psfree_g[o4] = tsil
                    tsilu[o4] = tsil
                tkc_u = emit_stage_mm(wu_hbm, cc, tkc_g2, False)
                for o4 in range(HSLOTS):
                    pr.raw_wait("vector", tkc_u[15])
                    psfree_g[o4] = pr.op("vector", lambda e, o4=o4: e.tensor_copy(
                        ubuf2[:, o4 * CC:(o4 + 1) * CC], ps_g[o4]), inc=[("v", 1)])
                ub = ubuf[cc % 2]
                for o4 in range(HSLOTS):
                    g0 = o4 * CC
                    pr.raw_wait("vector", tsilu[o4])
                    pr.raw_wait("vector", ub_war[cc % 2][o4])
                    pr.op("vector", lambda e, g0=g0: e.tensor_mul(
                        gbuf[:, g0:g0 + CC], gbuf[:, g0:g0 + CC], ubuf2[:, g0:g0 + CC]))
                    pr.op("vector", lambda e, g0=g0: e.tensor_mul(
                        abuf, gbuf[:, g0:g0 + CC], gbuf[:, g0:g0 + CC]))
                    prs = [(0, 1), (0, 2), (0, 3), (1, 2), (1, 3), (2, 3)]
                    for k, (a, bb) in enumerate(prs):
                        pr.op("vector", lambda e, a=a, bb=bb, k=k: e.tensor_tensor(
                            cw[:, k * NG:(k + 1) * NG], abuf[:, a::4], abuf[:, bb::4], ALU.is_ge))
                    c01, c02, c03, c12, c13, c23 = [cw[:, k * NG:(k + 1) * NG] for k in range(6)]
                    s0, s1, s2, s3 = [cw[:, (6 + k) * NG:(7 + k) * NG] for k in range(4)]
                    pr.op("vector", lambda e: e.tensor_add(s0, c01, c02))
                    pr.op("vector", lambda e: e.tensor_add(s0, s0, c03))
                    pr.op("vector", lambda e: e.tensor_add(s1, c12, c13))
                    pr.op("vector", lambda e: e.tensor_sub(s1, s1, c01))
                    pr.op("vector", lambda e: e.tensor_scalar_add(s1, s1, 1.0))
                    pr.op("vector", lambda e: e.tensor_sub(s2, c23, c02))
                    pr.op("vector", lambda e: e.tensor_sub(s2, s2, c12))
                    pr.op("vector", lambda e: e.tensor_scalar_add(s2, s2, 2.0))
                    pr.op("vector", lambda e: e.tensor_add(s3, c03, c13))
                    pr.op("vector", lambda e: e.tensor_add(s3, s3, c23))
                    pr.op("vector", lambda e: e.tensor_scalar(s3, s3, -1.0, 3.0, ALU.mult, ALU.add))
                    for k, sk in enumerate((s0, s1, s2, s3)):
                        pr.op("vector", lambda e, sk=sk, k=k, g0=g0, ub=ub: e.scalar_tensor_tensor(
                            ub[:, g0 + k:g0 + CC:4], sk, 2.0, gbuf[:, g0 + k:g0 + CC:4],
                            ALU.is_ge, ALU.mult))
                    tspar[cc % 2][o4] = pr.op("vector", lambda e: e.engine_nop(), inc=[("v", 1)])
                if cc > 0:
                    emit_tail(cc - 1)
            emit_tail(NCC - 1)
            for o4 in range(HSLOTS):
                o = half * HSLOTS + o4
                tx = pr.dma("scalar", lambda e, o=o: e.dma_start(x1rb, x1sp_hbm[o]),
                            wait=list(x1_war) + [x1sp_toks[o]])
                for g in range(4):
                    pj = g % 2
                    for q in range(4):
                        m = g * 4 + q
                        tt = pr.op("tensor", lambda e, m=m, q=q, pj=pj, o4=o4: e.transpose(
                            ps_d[pj][:, q * 128:(q + 1) * 128],
                            accF[:, m * THW + o4 * 128: m * THW + (o4 + 1) * 128], ident),
                            inc=[("t", 1)] if q == 3 else (),
                            wait=[psfree_d[0], psfree_d[1]] if q == 0 else ())
                    pr.raw_wait("vector", tt)
                    psfree_d[pj] = pr.op("vector", lambda e, g=g, pj=pj: e.tensor_copy(
                        t2rb[:, g * 512:(g + 1) * 512], ps_d[pj]), inc=[("v", 1)])
                ln_inplace(t2rb, sqb, D, EPS_LN)
                pr.raw_wait("vector", tx)
                tadd = pr.op("vector", lambda e: e.tensor_add(x1rb, x1rb, t2rb), inc=[("v", 1)])
                tou = pr.dma("sync", lambda e, o=o: e.dma_start(out_hbm[o], x1rb),
                             wait=[tadd])
                out_toks.append(tou)
                x1_war = (tou,)
                t2rb_war = tadd
        for _t in out_toks:
            pr.raw_wait("sync", _t)
    emit_taps()
    pr.flush()

    return nc, es


def kernel(L0_attn_x, L0_attn_q_a_w, L0_attn_q_a_ln_w, L0_attn_q_b_w,
           L0_attn_kv_a_w, L0_attn_kv_a_ln_w, L0_attn_kv_b_w, L0_attn_o_w,
           L0_ffn_w_gate, L0_ffn_w_up, L0_ffn_w_down, _S=None):
    S = _S if _S is not None else 8192
    TT = S // 128
    SLOTS = TT // 8
    x = np.ascontiguousarray(np.asarray(L0_attn_x, dtype=np.float32).reshape(S, D))
    xt = x.reshape(TT, 128, D)

    f32 = lambda a: np.ascontiguousarray(np.asarray(a, dtype=np.float32))
    qa_w = f32(L0_attn_q_a_w)
    qb_w = f32(L0_attn_q_b_w)
    kva_w = f32(L0_attn_kv_a_w)
    kvb_w = f32(L0_attn_kv_b_w)
    o_w = f32(L0_attn_o_w)
    wg = f32(L0_ffn_w_gate)
    wu = f32(L0_ffn_w_up)
    wd = f32(L0_ffn_w_down)
    qa_ln = np.pad(np.asarray(L0_attn_q_a_ln_w, dtype=np.float64), (0, 768 - QLORA),
                   constant_values=1.0).reshape(6, 128).T.astype(np.float32)
    qa_ln = np.ascontiguousarray(qa_ln)
    kva_ln = np.ascontiguousarray(
        np.asarray(L0_attn_kv_a_ln_w, dtype=np.float32).reshape(2, 128).T)

    inv = 1.0 / THETA ** (np.arange(0, ROPE, 2, dtype=np.float64) / ROPE)
    tpos = np.arange(S, dtype=np.float64)
    fr = np.outer(tpos, inv)
    cosf = np.cos(fr).astype(np.float32).reshape(TT, 128, 32)
    sinf = np.sin(fr).astype(np.float32).reshape(TT, 128, 32)
    cosf = np.ascontiguousarray(cosf)
    sinf = np.ascontiguousarray(sinf)

    nc, es = build(S)

    in_maps = []
    for c in range(8):
        tiles = [8 * s + c for s in range(SLOTS)]
        x_own = np.ascontiguousarray(xt[tiles])
        cs_o = np.empty((128, 2 * SLOTS * 32), np.float32)
        for s, ti in enumerate(tiles):
            cs_o[:, s * 32:(s + 1) * 32] = cosf[ti]
            cs_o[:, SLOTS * 32 + s * 32: SLOTS * 32 + (s + 1) * 32] = sinf[ti]
        m = np.zeros((SLOTS, 8, 128, 128), np.float32)
        for s in range(SLOTS):
            for t in range(8):
                if t > c:
                    m[s, t] = NEG
                elif t == c:
                    m[s, t] = np.where(np.arange(128)[:, None] >= np.arange(128)[None, :],
                                       0.0, NEG)
        mask = np.ascontiguousarray(m.transpose(2, 0, 1, 3).reshape(128, SLOTS * 8 * 128)
                                    ).astype(ml_dtypes.bfloat16)
        in_maps.append({
            "x": xt, "x_own": x_own, "cos_f": cosf, "sin_f": sinf,
            "cs_o": np.ascontiguousarray(cs_o), "mask": mask,
            "qa_w": qa_w, "qa_ln": qa_ln, "qb_w": qb_w,
            "kva_w": kva_w, "kva_ln": kva_ln, "kvb_w": kvb_w, "o_w": o_w,
            "w_gate": wg, "w_up": wu, "w_down": wd,
        })

    res = run_bass_kernel_spmd(nc, in_maps, core_ids=list(range(8)))
    es.close()

    y = np.empty((TT, 128, D), np.float32)
    for c in range(8):
        oc = np.asarray(res.results[c]["out"])
        for s in range(SLOTS):
            y[8 * s + c] = oc[s]
    return y.reshape(1, S, D)


# version 14
# speedup vs baseline: 1.0112x; 1.0112x over previous
import numpy as np
import ml_dtypes
from contextlib import ExitStack
import concourse.bass as bass
import concourse.mybir as mybir
from concourse.bass_utils import run_bass_kernel_spmd

F32 = mybir.dt.float32
BF16 = mybir.dt.bfloat16
AX = mybir.AxisListType.X
ALU = mybir.AluOpType
ACT = mybir.ActivationFunctionType

D = 2048
NH = 16
QLORA = 682
KVL = 256
ROPE = 64
SCALE = float(1.0 / np.sqrt(128.0))
THETA = 128000.0
DI = 8192
CC = 512
EPS_LN = 1e-5
EPS_RMS = 1e-6
NEG = -1e9

ENGS = ("sync", "gpsimd", "scalar", "vector", "tensor")


class Pr:

    NP = 16

    def __init__(self, nc, es):
        self.nc = nc
        self.th = {e: [] for e in ENGS}
        self.sems = {}
        self.cnt = {}
        self.waited = {e: {} for e in ENGS}
        self.dma_idx = 0
        names = ["t", "v", "s", "g"] + [f"d{i}" for i in range(self.NP)]
        for name in names:
            h = es.enter_context(nc.semaphore(name))
            self.sems[name] = h
            self.cnt[name] = 0
            for e in ENGS:
                self.waited[e][name] = 0

    def tok(self, name):
        return (name, self.cnt[name])

    def op(self, eng, fn, inc=(), wait=()):
        for (s, v) in wait:
            if v > self.waited[eng][s]:
                self.waited[eng][s] = v
                h = self.sems[s]
                self.th[eng].append(lambda e, h=h, v=v: e.wait_ge(h, v))
        incs = []
        for (s, k) in inc:
            self.cnt[s] += k
            incs.append((self.sems[s], k))

        def thunk(e, fn=fn, incs=incs):
            r = fn(e)
            for (h, k) in incs:
                r = r.then_inc(h, k)
        self.th[eng].append(thunk)
        return self.tok(inc[0][0]) if inc else None

    def dma(self, eng, fn, wait=()):
        s = f"d{self.dma_idx % self.NP}"
        self.dma_idx += 1
        self.raw_wait(eng, (s, self.cnt[s]))
        return self.op(eng, fn, inc=[(s, 16)], wait=wait)

    def dma_cols(self, eng, dst, srcf, ncols, parts, wait=()):
        toks = []
        step = ncols // parts
        for p in range(parts):
            a = p * step
            b = ncols if p == parts - 1 else (p + 1) * step
            toks.append(self.dma(eng, lambda e, a=a, b=b: e.dma_start(
                dst[:, a:b], srcf(a, b)), wait=wait))
        return toks

    def raw_wait(self, eng, tok):
        s, v = tok
        if v > self.waited[eng][s]:
            self.waited[eng][s] = v
            h = self.sems[s]
            self.th[eng].append(lambda e, h=h, v=v: e.wait_ge(h, v))

    def flush(self):
        nonempty = [e for e in ENGS if self.th[e]]
        if not nonempty:
            return
        with self.nc.Block() as blk:
            for e in nonempty:
                getattr(blk, e)(lambda eng, L=list(self.th[e]): [t(eng) for t in L])
        self.th = {e: [] for e in ENGS}


def _split(pr, src, hi, lo, inc=(), wait=()):
    pr.op("vector", lambda e, s=src, h=hi: e.tensor_copy(h, s), wait=wait)
    return pr.op("vector", lambda e, s=src, h=hi, l=lo: e.tensor_tensor(l, s, h, ALU.subtract),
                 inc=inc)


def _mm3(pr, ps, lh, ll, rh, rl, start, stop, inc=(), wait=()):
    pr.op("tensor", lambda e: e.matmul(ps, lh, rh, start=start, stop=False), wait=wait)
    pr.op("tensor", lambda e: e.matmul(ps, lh, rl, start=False, stop=False))
    return pr.op("tensor", lambda e: e.matmul(ps, ll, rh, start=False, stop=stop),
                 inc=inc)


def build(S=8192):
    import os
    PHASES = int(os.environ.get("K_PHASES", "5"))
    NHALF = int(os.environ.get("K_HALF", "0"))
    NCH = int(os.environ.get("K_CHUNKS", "0"))
    TAP = os.environ.get("K_TAP", "")
    TT = S // 128
    SLOTS = TT // 8
    NTOK = SLOTS * 128
    HALVES = 2 if SLOTS >= 2 else 1
    HSLOTS = SLOTS // HALVES
    THW = HSLOTS * 128
    NCHUNK = DI // CC

    nc = bass.Bass("TRN2", target_bir_lowering=False, detect_race_conditions=False)
    es = ExitStack()
    A = es.enter_context

    x_hbm = nc.dram_tensor("x", [TT, 128, D], F32, kind="ExternalInput").ap()
    xo_hbm = nc.dram_tensor("x_own", [SLOTS, 128, D], F32, kind="ExternalInput").ap()
    cosf_hbm = nc.dram_tensor("cos_f", [TT, 128, 32], F32, kind="ExternalInput").ap()
    sinf_hbm = nc.dram_tensor("sin_f", [TT, 128, 32], F32, kind="ExternalInput").ap()
    cso_hbm = nc.dram_tensor("cs_o", [128, 2 * SLOTS * 32], F32, kind="ExternalInput").ap()
    mask_hbm = nc.dram_tensor("mask", [128, SLOTS * 8 * 128], BF16, kind="ExternalInput").ap()
    qa_hbm = nc.dram_tensor("qa_w", [D, QLORA], F32, kind="ExternalInput").ap()
    qaln_hbm = nc.dram_tensor("qa_ln", [128, 6], F32, kind="ExternalInput").ap()
    qb_hbm = nc.dram_tensor("qb_w", [QLORA, D], F32, kind="ExternalInput").ap()
    kva_hbm = nc.dram_tensor("kva_w", [D, KVL + ROPE], F32, kind="ExternalInput").ap()
    kvaln_hbm = nc.dram_tensor("kva_ln", [128, 2], F32, kind="ExternalInput").ap()
    kvb_hbm = nc.dram_tensor("kvb_w", [KVL, NH * 192], F32, kind="ExternalInput").ap()
    ow_hbm = nc.dram_tensor("o_w", [D, D], F32, kind="ExternalInput").ap()
    wg_hbm = nc.dram_tensor("w_gate", [D, DI], F32, kind="ExternalInput").ap()
    wu_hbm = nc.dram_tensor("w_up", [D, DI], F32, kind="ExternalInput").ap()
    wd_hbm = nc.dram_tensor("w_down", [DI, D], F32, kind="ExternalInput").ap()
    out_hbm = nc.dram_tensor("out", [SLOTS, 128, D], F32, kind="ExternalOutput").ap()
    spk = "ExternalOutput" if TAP else "Internal"
    outsp_hbm = nc.dram_tensor("outT_sp", [NH, 2, 128, NTOK], BF16, kind=spk).ap()
    x1sp_hbm = nc.dram_tensor("x1_sp", [SLOTS, 128, D], F32, kind=spk).ap()
    t2sp_hbm = nc.dram_tensor("t2_sp", [SLOTS, 128, D], F32, kind=spk).ap()

    pr = Pr(nc, es)

    ident = A(nc.sbuf_tensor("ident", [128, 128], F32)).ap()
    qaln_sb = A(nc.sbuf_tensor("qaln_sb", [128, 6], F32)).ap()
    kvaln_sb = A(nc.sbuf_tensor("kvaln_sb", [128, 2], F32)).ap()
    s_sum = A(nc.sbuf_tensor("s_sum", [128, 1], F32)).ap()
    s_mean = A(nc.sbuf_tensor("s_mean", [128, 1], F32)).ap()
    s_ssq = A(nc.sbuf_tensor("s_ssq", [128, 1], F32)).ap()
    s_sd = A(nc.sbuf_tensor("s_sd", [128, 1], F32)).ap()
    s_rs = A(nc.sbuf_tensor("s_rs", [128, 1], F32)).ap()
    c_epsln = A(nc.sbuf_tensor("c_epsln", [128, 1], F32)).ap()
    c_epsrms = A(nc.sbuf_tensor("c_epsrms", [128, 1], F32)).ap()
    es_attn = ExitStack()
    AA = es_attn.enter_context
    cnT_h = AA(nc.sbuf_tensor("cnT_h", [128, 2 * S], BF16)).ap()
    cnT_l = AA(nc.sbuf_tensor("cnT_l", [128, 2 * S], BF16)).ap()
    kfT_h = AA(nc.sbuf_tensor("kfT_h", [128, S], BF16)).ap()
    kfT_l = AA(nc.sbuf_tensor("kfT_l", [128, S], BF16)).ap()
    qnT_h = AA(nc.sbuf_tensor("qnT_h", [128, 6 * NTOK], BF16)).ap()
    qnT_l = AA(nc.sbuf_tensor("qnT_l", [128, 6 * NTOK], BF16)).ap()
    cs_own = AA(nc.sbuf_tensor("cs_own", [128, 2 * SLOTS * 32], F32)).ap()

    EPS_AP = {}

    def emit_taps():
        if not TAP:
            return
        taps = {"cnT_h": (cnT_h, BF16, 2 * S), "cnT_l": (cnT_l, BF16, 2 * S),
                "kfT_h": (kfT_h, BF16, S), "kfT_l": (kfT_l, BF16, S),
                "qnT_h": (qnT_h, BF16, 6 * NTOK), "qnT_l": (qnT_l, BF16, 6 * NTOK)}
        for nm in TAP.split(","):
            if nm not in taps:
                continue
            ap, dt, w = taps[nm]
            th = nc.dram_tensor("tap_" + nm, [128, w], dt, kind="ExternalOutput").ap()
            tk = pr.dma("sync", lambda e, ap=ap, th=th: e.dma_start(th, ap))
            pr.raw_wait("sync", tk)

    def ln_inplace(x_ap, sqbuf, width, eps):
        pr.raw_wait("vector", pr.tok("s"))
        t1 = pr.op("vector", lambda e: e.reduce_sum(s_sum, x_ap, AX), inc=[("v", 1)])
        pr.raw_wait("vector", t1)
        t2 = pr.op("vector", lambda e: e.tensor_scalar_mul(s_mean, s_sum, 1.0 / width),
                   inc=[("v", 1)])
        pr.raw_wait("vector", t2)
        pr.op("vector", lambda e: e.tensor_scalar(x_ap, x_ap, s_mean, None, ALU.subtract))
        pr.op("vector", lambda e: e.tensor_mul(sqbuf, x_ap, x_ap))
        tv = pr.op("vector", lambda e: e.reduce_sum(s_ssq, sqbuf, AX), inc=[("v", 1)])
        eps_ap = EPS_AP[eps]
        ts = pr.op("scalar", lambda e: e.activation(s_sd, s_ssq, ACT.Sqrt,
                                                    bias=eps_ap, scale=1.0 / width),
                   inc=[("s", 1)], wait=[tv])
        tr = pr.op("vector", lambda e: e.reciprocal(s_rs, s_sd), inc=[("v", 1)], wait=[ts])
        return pr.op("vector", lambda e: e.tensor_scalar_mul(x_ap, x_ap, s_rs),
                     inc=[("v", 1)], wait=[tr])

    EPS_AP[EPS_LN] = c_epsln
    EPS_AP[EPS_RMS] = c_epsrms
    pr.op("vector", lambda e: e.memset(c_epsln, EPS_LN))
    pr.op("vector", lambda e: e.memset(c_epsrms, EPS_RMS))
    tmz = pr.op("vector", lambda e: e.memset(ident, 0.0), inc=[("v", 1)])
    pr.op("gpsimd", lambda e: e.affine_select(
        out=ident, in_=ident, compare_op=ALU.not_equal, fill=1.0, base=0,
        pattern=[[-1, 128]], channel_multiplier=1), wait=[tmz])
    pr.dma("sync", lambda e: e.dma_start(cs_own, cso_hbm))
    pr.dma("sync", lambda e: e.dma_start(qaln_sb, qaln_hbm))
    pr.dma("sync", lambda e: e.dma_start(kvaln_sb, kvaln_hbm))
    pr.flush()

    with ExitStack() as e1:
        B = e1.enter_context
        kva_st = [B(nc.sbuf_tensor(f"kva_st{j}", [128, 320], F32)).ap() for j in range(2)]
        kva_h = B(nc.sbuf_tensor("kva_h", [128, 16 * 320], BF16)).ap()
        kva_l = B(nc.sbuf_tensor("kva_l", [128, 16 * 320], BF16)).ap()
        xs = [B(nc.sbuf_tensor(f"xs{j}", [128, D], F32)).ap() for j in range(2)]
        sqb = B(nc.sbuf_tensor("sqb", [128, D], F32)).ap()
        tT_h = B(nc.sbuf_tensor("tT_h", [128, 16 * 128], BF16)).ap()
        tT_l = B(nc.sbuf_tensor("tT_l", [128, 16 * 128], BF16)).ap()
        csb = [B(nc.sbuf_tensor(f"csb{j}", [128, 64], F32)).ap() for j in range(2)]
        ctmp = B(nc.sbuf_tensor("ctmp", [128, KVL], F32)).ap()
        ckv_s = B(nc.sbuf_tensor("ckv_s", [128, KVL], F32)).ap()
        sq2 = B(nc.sbuf_tensor("sq2_b1", [128, KVL], F32)).ap()
        kp = B(nc.sbuf_tensor("kp", [128, 64], F32)).ap()
        tp_in = B(nc.sbuf_tensor("tp_in", [128, 128], F32)).ap()
        rtmp = B(nc.sbuf_tensor("rtmp", [128, 32], F32)).ap()
        ps_t = [B(nc.psum_tensor(f"ps_t{j}", [128, 512], F32)).ap() for j in range(2)]
        ps_kv = B(nc.psum_tensor("ps_kv", [128, 320], F32)).ap()
        ps_ct = B(nc.psum_tensor("ps_ct", [128, 384], F32)).ap()

        pr.op("vector", lambda e: e.memset(tp_in[:, 0:64], 0.0))
        kst_war = [pr.tok("v"), pr.tok("v")]
        for kc in range(16):
            wb = kc % 2
            tw = pr.dma("sync", lambda e, kc=kc, wb=wb: e.dma_start(
                kva_st[wb], kva_hbm[kc * 128:(kc + 1) * 128, :]), wait=[kst_war[wb]])
            pr.raw_wait("vector", tw)
            pr.op("vector", lambda e, kc=kc, wb=wb: e.tensor_copy(
                kva_h[:, kc * 320:(kc + 1) * 320], kva_st[wb]))
            kst_war[wb] = pr.op("vector", lambda e, kc=kc, wb=wb: e.tensor_tensor(
                kva_l[:, kc * 320:(kc + 1) * 320], kva_st[wb],
                kva_h[:, kc * 320:(kc + 1) * 320], ALU.subtract), inc=[("v", 1)])
        tkva = pr.tok("v")

        xfree = [pr.tok("t"), pr.tok("t")]
        csfree = [pr.tok("v"), pr.tok("v")]
        psfree_t = [pr.tok("v"), pr.tok("v")]
        psfree_kv = pr.tok("v")
        psfree_ct = pr.tok("v")
        tT_war = pr.tok("t")
        for i in range(TT):
            b = i % 2
            txs = pr.dma_cols("sync", xs[b], (lambda i: (lambda a, bb: x_hbm[i][:, a:bb]))(i),
                              D, 4, wait=[xfree[b]])
            tx = txs[-1]
            tc1 = pr.dma("sync", lambda e, i=i, b=b: e.dma_start(csb[b][:, 0:32], cosf_hbm[i]),
                         wait=[csfree[b]])
            tc2 = pr.dma("sync", lambda e, i=i, b=b: e.dma_start(csb[b][:, 32:64], sinf_hbm[i]))
            tcs = [tc1, tc2]
            for _t in txs:
                pr.raw_wait("vector", _t)
            tln = ln_inplace(xs[b], sqb, D, EPS_LN)
            tev = None
            for g in range(4):
                pj = g % 2
                for q in range(4):
                    kc = g * 4 + q
                    w = [tln, psfree_t[pj], tT_war] if q == 0 else ()
                    tt = pr.op("tensor", lambda e, b=b, kc=kc, q=q, pj=pj: e.transpose(
                        ps_t[pj][:, q * 128:(q + 1) * 128],
                        xs[b][:, kc * 128:(kc + 1) * 128], ident),
                        inc=[("t", 1)] if q == 3 else (), wait=w)
                c0 = g * 512
                ths = pr.op("scalar", lambda e, c0=c0, pj=pj: e.activation(
                    tT_h[:, c0:c0 + 512], ps_t[pj], ACT.Copy), inc=[("s", 1)], wait=[tt, tkva])
                pr.raw_wait("vector", ths)
                tev = pr.op("vector", lambda e, c0=c0, pj=pj: e.tensor_tensor(
                    tT_l[:, c0:c0 + 512], ps_t[pj], tT_h[:, c0:c0 + 512], ALU.subtract),
                    inc=[("v", 1)])
                psfree_t[pj] = tev
            xfree[b] = pr.tok("t")
            for kc in range(16):
                tmm = _mm3(pr, ps_kv,
                           tT_h[:, kc * 128:(kc + 1) * 128], tT_l[:, kc * 128:(kc + 1) * 128],
                           kva_h[:, kc * 320:(kc + 1) * 320], kva_l[:, kc * 320:(kc + 1) * 320],
                           start=(kc == 0), stop=(kc == 15),
                           inc=[("t", 1)], wait=[tev, psfree_kv] if kc == 0 else ())
            tT_war = tmm
            pr.raw_wait("vector", tmm)
            pr.raw_wait("vector", pr.tok("s"))
            pr.op("vector", lambda e: e.tensor_copy(ckv_s, ps_kv[:, 0:KVL]))
            pr.op("vector", lambda e: e.tensor_copy(kp, ps_kv[:, KVL:KVL + 64]))
            psfree_kv = pr.op("vector", lambda e: e.engine_nop(), inc=[("v", 1)])
            pr.op("vector", lambda e: e.tensor_mul(sq2[:, 0:KVL], ckv_s, ckv_s))
            tv = pr.op("vector", lambda e: e.reduce_sum(s_ssq, sq2[:, 0:KVL], AX), inc=[("v", 1)])
            ts = pr.op("scalar", lambda e: e.activation(s_sd, s_ssq, ACT.Sqrt,
                                                        bias=c_epsrms, scale=1.0 / KVL),
                       inc=[("s", 1)], wait=[tv])
            tr = pr.op("vector", lambda e: e.reciprocal(s_rs, s_sd), inc=[("v", 1)], wait=[ts])
            pr.raw_wait("vector", tr)
            pr.op("vector", lambda e: e.tensor_scalar_mul(ctmp, ckv_s, s_rs),
                  wait=[psfree_ct])
            for _t in tcs:
                pr.raw_wait("vector", _t)
            pr.op("vector", lambda e, b=b: e.tensor_mul(tp_in[:, 64:96], kp[:, 0:64:2], csb[b][:, 0:32]))
            pr.op("vector", lambda e, b=b: e.tensor_mul(rtmp, kp[:, 1:64:2], csb[b][:, 32:64]))
            pr.op("vector", lambda e: e.tensor_sub(tp_in[:, 64:96], tp_in[:, 64:96], rtmp))
            pr.op("vector", lambda e, b=b: e.tensor_mul(tp_in[:, 96:128], kp[:, 1:64:2], csb[b][:, 0:32]))
            pr.op("vector", lambda e, b=b: e.tensor_mul(rtmp, kp[:, 0:64:2], csb[b][:, 32:64]))
            tro = pr.op("vector", lambda e: e.tensor_add(tp_in[:, 96:128], tp_in[:, 96:128], rtmp),
                        inc=[("v", 1)])
            csfree[b] = tro
            for j in range(2):
                pr.op("tensor", lambda e, j=j: e.transpose(
                    ps_ct[:, j * 128:(j + 1) * 128], ctmp[:, j * 128:(j + 1) * 128], ident),
                    wait=[tro] if j == 0 else ())
            tct = pr.op("tensor", lambda e: e.transpose(ps_ct[:, 256:384], tp_in, ident),
                        inc=[("t", 1)])
            pr.raw_wait("vector", tct)
            pr.raw_wait("scalar", tct)
            for ch in range(2):
                ths = pr.op("scalar", lambda e, ch=ch, i=i: e.activation(
                    cnT_h[:, ch * S + i * 128: ch * S + (i + 1) * 128],
                    ps_ct[:, ch * 128:(ch + 1) * 128], ACT.Copy), inc=[("s", 1)])
                pr.raw_wait("vector", ths)
                pr.op("vector", lambda e, ch=ch, i=i: e.tensor_tensor(
                    cnT_l[:, ch * S + i * 128: ch * S + (i + 1) * 128],
                    ps_ct[:, ch * 128:(ch + 1) * 128],
                    cnT_h[:, ch * S + i * 128: ch * S + (i + 1) * 128], ALU.subtract))
            pr.raw_wait("vector", pr.tok("s"))
            pr.op("vector", lambda e, i=i: e.tensor_copy(
                kfT_h[64:128, i * 128:(i + 1) * 128], ps_ct[64:128, 256:384]))
            psfree_ct = pr.op("vector", lambda e, i=i: e.tensor_tensor(
                kfT_l[64:128, i * 128:(i + 1) * 128], ps_ct[64:128, 256:384],
                kfT_h[64:128, i * 128:(i + 1) * 128], ALU.subtract), inc=[("v", 1)])
        pr.flush()

    if PHASES < 2:
        emit_taps()
        pr.flush()
        return nc, es
    with ExitStack() as e2:
        B = e2.enter_context
        qa_st = [B(nc.sbuf_tensor(f"qa_st{j}", [128, QLORA], F32)).ap() for j in range(2)]
        qa_h = B(nc.sbuf_tensor("qa_h", [128, 16 * QLORA], BF16)).ap()
        qa_l = B(nc.sbuf_tensor("qa_l", [128, 16 * QLORA], BF16)).ap()
        xq = B(nc.sbuf_tensor("xq", [128, D], F32)).ap()
        sqb = B(nc.sbuf_tensor("sqb2", [128, D], F32)).ap()
        tT_h = B(nc.sbuf_tensor("tT2_h", [128, 16 * 128], BF16)).ap()
        tT_l = B(nc.sbuf_tensor("tT2_l", [128, 16 * 128], BF16)).ap()
        qn = B(nc.sbuf_tensor("qn", [128, QLORA], F32)).ap()
        sq2 = B(nc.sbuf_tensor("sq2_b2", [128, QLORA], F32)).ap()
        ps_t = [B(nc.psum_tensor(f"ps2_t{j}", [128, 512], F32)).ap() for j in range(2)]
        ps_qa = B(nc.psum_tensor("ps_qa", [128, 512], F32)).ap()
        ps_qb = B(nc.psum_tensor("ps_qb", [128, 170], F32)).ap()

        qst_war = [pr.tok("v"), pr.tok("v")]
        for kc in range(16):
            wb = kc % 2
            tws = pr.dma_cols("sync", qa_st[wb],
                              (lambda kc: (lambda a, bb: qa_hbm[kc * 128:(kc + 1) * 128, a:bb]))(kc),
                              QLORA, 2, wait=[qst_war[wb]])
            for _t in tws:
                pr.raw_wait("vector", _t)
            pr.op("vector", lambda e, kc=kc, wb=wb: e.tensor_copy(
                qa_h[:, kc * QLORA:(kc + 1) * QLORA], qa_st[wb]))
            qst_war[wb] = pr.op("vector", lambda e, kc=kc, wb=wb: e.tensor_tensor(
                qa_l[:, kc * QLORA:(kc + 1) * QLORA], qa_st[wb],
                qa_h[:, kc * QLORA:(kc + 1) * QLORA], ALU.subtract), inc=[("v", 1)])
        tqa = pr.tok("v")

        psfree_t = [pr.tok("v"), pr.tok("v")]
        psfree_q = pr.tok("v")
        tT_war = pr.tok("t")
        xfree = pr.tok("t")
        for o in range(SLOTS):
            txs = pr.dma_cols("sync", xq, (lambda o: (lambda a, bb: xo_hbm[o][:, a:bb]))(o),
                              D, 4, wait=[xfree])
            for _t in txs:
                pr.raw_wait("vector", _t)
            tln = ln_inplace(xq, sqb, D, EPS_LN)
            for g in range(4):
                pj = g % 2
                for q in range(4):
                    kc = g * 4 + q
                    w = [tln, psfree_t[pj], tT_war] if q == 0 else ()
                    tt = pr.op("tensor", lambda e, kc=kc, q=q, pj=pj: e.transpose(
                        ps_t[pj][:, q * 128:(q + 1) * 128],
                        xq[:, kc * 128:(kc + 1) * 128], ident),
                        inc=[("t", 1)] if q == 3 else (), wait=w)
                c0 = g * 512
                ths = pr.op("scalar", lambda e, c0=c0, pj=pj: e.activation(
                    tT_h[:, c0:c0 + 512], ps_t[pj], ACT.Copy), inc=[("s", 1)], wait=[tt, tqa])
                pr.raw_wait("vector", ths)
                tev = pr.op("vector", lambda e, c0=c0, pj=pj: e.tensor_tensor(
                    tT_l[:, c0:c0 + 512], ps_t[pj], tT_h[:, c0:c0 + 512], ALU.subtract),
                    inc=[("v", 1)])
                psfree_t[pj] = tev
            xfree = pr.tok("t")
            for kc in range(16):
                _mm3(pr, ps_qa,
                     tT_h[:, kc * 128:(kc + 1) * 128], tT_l[:, kc * 128:(kc + 1) * 128],
                     qa_h[:, kc * QLORA:kc * QLORA + 512], qa_l[:, kc * QLORA:kc * QLORA + 512],
                     start=(kc == 0), stop=(kc == 15),
                     inc=[("t", 1)], wait=[tev, psfree_q] if kc == 0 else ())
                tmm = _mm3(pr, ps_qb,
                           tT_h[:, kc * 128:(kc + 1) * 128], tT_l[:, kc * 128:(kc + 1) * 128],
                           qa_h[:, kc * QLORA + 512:(kc + 1) * QLORA],
                           qa_l[:, kc * QLORA + 512:(kc + 1) * QLORA],
                           start=(kc == 0), stop=(kc == 15), inc=[("t", 1)])
            tT_war = tmm
            xfree = tmm
            pr.raw_wait("vector", tmm)
            pr.raw_wait("vector", pr.tok("s"))
            pr.op("vector", lambda e: e.tensor_copy(qn[:, 0:512], ps_qa))
            psfree_q = pr.op("vector", lambda e: e.tensor_copy(qn[:, 512:QLORA], ps_qb),
                             inc=[("v", 1)])
            pr.op("vector", lambda e: e.tensor_mul(sq2[:, 0:QLORA], qn[:, 0:QLORA], qn[:, 0:QLORA]))
            tv = pr.op("vector", lambda e: e.reduce_sum(s_ssq, sq2[:, 0:QLORA], AX), inc=[("v", 1)])
            ts = pr.op("scalar", lambda e: e.activation(s_sd, s_ssq, ACT.Sqrt,
                                                        bias=c_epsrms, scale=1.0 / QLORA),
                       inc=[("s", 1)], wait=[tv])
            tr = pr.op("vector", lambda e: e.reciprocal(s_rs, s_sd), inc=[("v", 1)], wait=[ts])
            pr.raw_wait("vector", tr)
            tql = pr.op("vector", lambda e: e.tensor_scalar_mul(qn[:, 0:QLORA], qn[:, 0:QLORA], s_rs),
                        inc=[("v", 1)])
            psfree_q = tql
            for qc in range(6):
                wd = min(128, QLORA - qc * 128)
                pj = qc % 2
                tt = pr.op("tensor", lambda e, qc=qc, wd=wd, pj=pj: e.transpose(
                    ps_t[pj][:wd, 0:128], qn[:, qc * 128:qc * 128 + wd], ident),
                    inc=[("t", 1)], wait=[psfree_q, psfree_t[pj]])
                ths = pr.op("scalar", lambda e, qc=qc, wd=wd, pj=pj, o=o: e.activation(
                    qnT_h[:wd, qc * NTOK + o * 128: qc * NTOK + (o + 1) * 128],
                    ps_t[pj][:wd, 0:128], ACT.Copy), inc=[("s", 1)], wait=[tt])
                pr.raw_wait("vector", ths)
                psfree_t[pj] = pr.op("vector", lambda e, qc=qc, wd=wd, pj=pj, o=o: e.tensor_tensor(
                    qnT_l[:wd, qc * NTOK + o * 128: qc * NTOK + (o + 1) * 128], ps_t[pj][:wd, 0:128],
                    qnT_h[:wd, qc * NTOK + o * 128: qc * NTOK + (o + 1) * 128], ALU.subtract),
                    inc=[("v", 1)])
        pr.flush()

    if PHASES < 3:
        emit_taps()
        pr.flush()
        return nc, es
    with ExitStack() as e3:
        B = e3.enter_context
        mask_sb = B(nc.sbuf_tensor("mask_sb", [128, SLOTS * 8 * 128], BF16)).ap()
        kvb_st = B(nc.sbuf_tensor("kvb_st", [128, 384], F32)).ap()
        kvb_h = B(nc.sbuf_tensor("kvb_h", [128, 384], BF16)).ap()
        kvb_l = B(nc.sbuf_tensor("kvb_l", [128, 384], BF16)).ap()
        qb_st = B(nc.sbuf_tensor("qb_st", [128, 768], F32)).ap()
        qb_h = B(nc.sbuf_tensor("qb_h", [128, 768], BF16)).ap()
        qb_l = B(nc.sbuf_tensor("qb_l", [128, 768], BF16)).ap()
        qT_h = B(nc.sbuf_tensor("qT_h", [128, NTOK], BF16)).ap()
        qT_l = B(nc.sbuf_tensor("qT_l", [128, NTOK], BF16)).ap()
        v_h = B(nc.sbuf_tensor("v_h", [128, S], BF16)).ap()
        v_l = B(nc.sbuf_tensor("v_l", [128, S], BF16)).ap()
        outT_h = B(nc.sbuf_tensor("outT_h", [128, NTOK], BF16)).ap()
        outT_l = B(nc.sbuf_tensor("outT_l", [128, NTOK], BF16)).ap()
        rtmp = B(nc.sbuf_tensor("rtmp3", [128, 32], F32)).ap()
        GW = min(4, SLOTS)
        probs = [B(nc.sbuf_tensor(f"probs{j}", [128, 512], F32)).ap() for j in range(GW)]
        qf = probs[0][:, 0:128]
        tp_q = probs[min(1, GW - 1)][:, 0:128]
        pt_h = [B(nc.sbuf_tensor(f"pt_h{j}", [128, 512], BF16)).ap() for j in range(2)]
        pt_l = [B(nc.sbuf_tensor(f"pt_l{j}", [128, 512], BF16)).ap() for j in range(2)]
        o_acc = [B(nc.sbuf_tensor(f"o_acc{j}", [128, 128], F32)).ap() for j in range(GW)]
        o_out = B(nc.sbuf_tensor("o_out", [128, 128], F32)).ap()
        negm = [[B(nc.sbuf_tensor(f"negm{j}_{k}", [128, 1], F32)).ap() for k in range(2)]
                for j in range(GW)]
        negmx = [B(nc.sbuf_tensor(f"negmx{j}", [128, 1], F32)).ap() for j in range(GW)]
        dm = [B(nc.sbuf_tensor(f"dm{j}", [128, 1], F32)).ap() for j in range(GW)]
        fac = [B(nc.sbuf_tensor(f"fac{j}", [128, 1], F32)).ap() for j in range(GW)]
        esum = [B(nc.sbuf_tensor(f"esum{j}", [128, 1], F32)).ap() for j in range(GW)]
        srun = [B(nc.sbuf_tensor(f"srun{j}", [128, 1], F32)).ap() for j in range(GW)]
        rinv = B(nc.sbuf_tensor("rinv", [128, 1], F32)).ap()
        ps_sc = [B(nc.psum_tensor(f"ps_sc{j}", [128, 512], F32)).ap() for j in range(2)]
        ps_pt = [B(nc.psum_tensor(f"ps_pt{j}", [128, 512], F32)).ap() for j in range(2)]
        ps_v = [B(nc.psum_tensor(f"ps_v{j}", [128, 512], F32)).ap() for j in range(2)]
        ps_av = [B(nc.psum_tensor(f"ps_av{j}", [128, 128], F32)).ap() for j in range(2)]

        pr.op("vector", lambda e: e.memset(qb_st, 0.0))
        tmk = pr.dma("sync", lambda e: e.dma_start(mask_sb, mask_hbm))
        pr.raw_wait("vector", tmk)
        psfree_v = [pr.tok("v"), pr.tok("v")]
        kfT_war = pr.tok("t")
        v_war = pr.tok("t")
        qT_war = pr.tok("t")
        stg_war = pr.tok("v")
        out_war = []
        spill_toks = []
        scfree = [pr.tok("s"), pr.tok("s")]
        ptfree = [pr.tok("v"), pr.tok("v")]
        avfree = [pr.tok("v"), pr.tok("v")]
        probs_war = [pr.tok("t")] * GW
        pt_war = [pr.tok("t"), pr.tok("t")]
        for h in range(NH):
            kvb_toks = [pr.dma("scalar", lambda e, ch=ch, h=h: e.dma_start(
                kvb_st[:, ch * 192:(ch + 1) * 192],
                kvb_hbm[ch * 128:(ch + 1) * 128, h * 192:(h + 1) * 192]),
                wait=[stg_war] if ch == 0 else ()) for ch in range(2)]
            for _t in kvb_toks:
                pr.raw_wait("vector", _t)
            for ch in range(2):
                pr.op("vector", lambda e, ch=ch: e.tensor_scalar_mul(
                    kvb_st[:, ch * 192:(ch + 1) * 192], kvb_st[:, ch * 192:(ch + 1) * 192],
                    kvaln_sb[:, ch:ch + 1]))
            _split(pr, kvb_st, kvb_h, kvb_l)
            qb_toks = [pr.dma("scalar", lambda e, qc=qc, wd=min(128, QLORA - qc * 128), h=h:
                e.dma_start(qb_st[:wd, qc * 128:(qc + 1) * 128],
                            qb_hbm[qc * 128:qc * 128 + wd, h * 128:(h + 1) * 128]))
                for qc in range(6)]
            for _t in qb_toks:
                pr.raw_wait("vector", _t)
            for qc in range(6):
                pr.op("vector", lambda e, qc=qc: e.tensor_scalar_mul(
                    qb_st[:, qc * 128:(qc + 1) * 128], qb_st[:, qc * 128:(qc + 1) * 128],
                    qaln_sb[:, qc:qc + 1]))
            stg_war = _split(pr, qb_st, qb_h, qb_l, inc=[("v", 1)])
            tstg = stg_war
            psfree_q2 = pr.tok("v")
            for o in range(SLOTS):
                for qc in range(6):
                    wd = min(128, QLORA - qc * 128)
                    tmm = _mm3(pr, ps_av[0],
                               qnT_h[:wd, qc * NTOK + o * 128:qc * NTOK + (o + 1) * 128],
                               qnT_l[:wd, qc * NTOK + o * 128:qc * NTOK + (o + 1) * 128],
                               qb_h[:wd, qc * 128:(qc + 1) * 128], qb_l[:wd, qc * 128:(qc + 1) * 128],
                               start=(qc == 0), stop=(qc == 5),
                               inc=[("t", 1)],
                               wait=[tstg, qT_war, psfree_q2, avfree[0]] if qc == 0 else ())
                pr.raw_wait("vector", tmm)
                pr.op("vector", lambda e: e.tensor_scalar_mul(qf, ps_av[0], SCALE))
                avfree[0] = pr.op("vector", lambda e: e.engine_nop(), inc=[("v", 1)])
                co = o * 32
                so = SLOTS * 32 + o * 32
                pr.op("vector", lambda e: e.tensor_copy(tp_q[:, 0:64], qf[:, 0:64]))
                pr.op("vector", lambda e, co=co: e.tensor_mul(tp_q[:, 64:96], qf[:, 64:128:2], cs_own[:, co:co + 32]))
                pr.op("vector", lambda e, so=so: e.tensor_mul(rtmp, qf[:, 65:128:2], cs_own[:, so:so + 32]))
                pr.op("vector", lambda e: e.tensor_sub(tp_q[:, 64:96], tp_q[:, 64:96], rtmp))
                pr.op("vector", lambda e, co=co: e.tensor_mul(tp_q[:, 96:128], qf[:, 65:128:2], cs_own[:, co:co + 32]))
                pr.op("vector", lambda e, so=so: e.tensor_mul(rtmp, qf[:, 64:128:2], cs_own[:, so:so + 32]))
                tq2 = pr.op("vector", lambda e: e.tensor_add(tp_q[:, 96:128], tp_q[:, 96:128], rtmp),
                            inc=[("v", 1)])
                tt = pr.op("tensor", lambda e: e.transpose(ps_av[1], tp_q, ident),
                           inc=[("t", 1)], wait=[tq2, avfree[1]])
                pr.raw_wait("scalar", tt)
                ths = pr.op("scalar", lambda e, o=o: e.activation(
                    qT_h[:, o * 128:(o + 1) * 128], ps_av[1], ACT.Copy), inc=[("s", 1)])
                pr.raw_wait("vector", ths)
                psfree_q2 = pr.op("vector", lambda e, o=o: e.tensor_tensor(
                    qT_l[:, o * 128:(o + 1) * 128], ps_av[1], qT_h[:, o * 128:(o + 1) * 128],
                    ALU.subtract), inc=[("v", 1)])
                avfree[1] = psfree_q2
            tqT = pr.tok("v")
            for gi in range(TT // 4):
                kg = gi
                kb = kg % 2
                for ch in range(2):
                    tmm = _mm3(pr, ps_pt[kb][0:64, :],
                               kvb_h[:, ch * 192:ch * 192 + 64],
                               kvb_l[:, ch * 192:ch * 192 + 64],
                               cnT_h[:, ch * S + kg * 512: ch * S + (kg + 1) * 512],
                               cnT_l[:, ch * S + kg * 512: ch * S + (kg + 1) * 512],
                               start=(ch == 0), stop=(ch == 1),
                               inc=[("t", 1)],
                               wait=[stg_war, kfT_war, ptfree[kb]] if ch == 0 else ())
                pr.raw_wait("scalar", tmm)
                ths = pr.op("scalar", lambda e, kg=kg, kb=kb: e.activation(
                    kfT_h[0:64, kg * 512:(kg + 1) * 512], ps_pt[kb][0:64, :], ACT.Copy),
                    inc=[("s", 1)])
                pr.raw_wait("vector", ths)
                ptfree[kb] = pr.op("vector", lambda e, kg=kg, kb=kb: e.tensor_tensor(
                    kfT_l[0:64, kg * 512:(kg + 1) * 512], ps_pt[kb][0:64, :],
                    kfT_h[0:64, kg * 512:(kg + 1) * 512], ALU.subtract), inc=[("v", 1)])
                vg = gi
                vb = vg % 2
                for k4 in range(4):
                    kt = vg * 4 + k4
                    for ch in range(2):
                        w = ()
                        if k4 == 0 and ch == 0:
                            w = [psfree_v[vb]] + ([v_war, tstg] if vg == 0 else [])
                        tmm = _mm3(pr, ps_v[vb][:, k4 * 128:(k4 + 1) * 128],
                                   cnT_h[:, ch * S + kt * 128: ch * S + (kt + 1) * 128],
                                   cnT_l[:, ch * S + kt * 128: ch * S + (kt + 1) * 128],
                                   kvb_h[:, ch * 192 + 64:(ch + 1) * 192],
                                   kvb_l[:, ch * 192 + 64:(ch + 1) * 192],
                                   start=(ch == 0), stop=(ch == 1),
                                   inc=[("t", 1)], wait=w)
                pr.raw_wait("scalar", tmm)
                ths = pr.op("scalar", lambda e, vg=vg, vb=vb: e.activation(
                    v_h[:, vg * 512:(vg + 1) * 512], ps_v[vb], ACT.Copy), inc=[("s", 1)])
                pr.raw_wait("vector", ths)
                psfree_v[vb] = pr.op("vector", lambda e, vg=vg, vb=vb: e.tensor_tensor(
                    v_l[:, vg * 512:(vg + 1) * 512], ps_v[vb],
                    v_h[:, vg * 512:(vg + 1) * 512], ALU.subtract), inc=[("v", 1)])
            tkn = pr.tok("v")
            tvx = pr.tok("v")
            for p in range(SLOTS // GW):
                oo = [GW * p + i for i in range(GW)]
                KGs = [2 * o + 2 for o in oo]
                tini = [None] * GW
                texp = [None] * GW
                tscv = [None] * GW
                fac_war = [pr.tok("v")] * GW
                tsr = [pr.tok("v")] * GW
                negmx_war = [[] for _ in range(GW)]
                for idx in range(GW):
                    pr.op("vector", lambda e, idx=idx: e.memset(negm[idx][1], 1e30),
                          wait=[pr.tok("g")])
                    pr.op("vector", lambda e, idx=idx: e.memset(srun[idx], 0.0),
                          wait=[tsr[idx]])
                    tini[idx] = pr.op("vector", lambda e, idx=idx: e.memset(o_acc[idx], 0.0),
                                      inc=[("v", 1)])
                for kg in range(KGs[GW - 1]):
                    live = [i for i in range(GW) if kg < KGs[i]]
                    for idx in live:
                        o = oo[idx]
                        KG = KGs[idx]
                        sb = (kg * GW + idx) % 2
                        tsc = _mm3(pr, ps_sc[sb],
                                   qT_h[:, o * 128:(o + 1) * 128], qT_l[:, o * 128:(o + 1) * 128],
                                   kfT_h[:, kg * 512:(kg + 1) * 512], kfT_l[:, kg * 512:(kg + 1) * 512],
                                   start=True, stop=True,
                                   inc=[("t", 1)],
                                   wait=[tqT, tkn, scfree[sb]] if kg == 0 else [scfree[sb]])
                        kfT_war = tsc
                        qT_war = tsc
                        tscv[idx] = (tsc, sb)
                        pr.raw_wait("vector", tsc)
                        if kg >= KG - 2:
                            mo = (o * 8 + (kg - (KG - 2)) * 4) * 128
                            pr.op("vector", lambda e, sb=sb, mo=mo: e.tensor_add(
                                ps_sc[sb], ps_sc[sb], mask_sb[:, mo:mo + 512]))
                        tmx = pr.op("vector", lambda e, idx=idx, sb=sb: e.reduce_max(
                            negmx[idx], ps_sc[sb], AX, negate=True),
                            inc=[("v", 1)],
                            wait=([tini[idx]] if kg == 0 else []) + negmx_war[idx])
                        pr.raw_wait("vector", tmx)
                        pr.op("vector", lambda e, idx=idx, kg=kg: e.tensor_scalar(
                            dm[idx], negmx[idx], negm[idx][1 - kg % 2], negm[idx][1 - kg % 2],
                            ALU.min, ALU.subtract))
                        tvm = pr.op("vector", lambda e, idx=idx, kg=kg: e.tensor_single_scalar(
                            negm[idx][kg % 2], negmx[idx], negm[idx][1 - kg % 2], ALU.min),
                            inc=[("v", 1)])
                        pr.op("scalar", lambda e, idx=idx: e.activation(fac[idx], dm[idx], ACT.Exp),
                              inc=[("s", 1)], wait=[tvm, fac_war[idx]])
                        te = pr.op("scalar", lambda e, idx=idx, kg=kg, sb=sb: e.activation(
                            probs[idx], ps_sc[sb], ACT.Exp, bias=negm[idx][kg % 2],
                            accum_out=esum[idx]),
                            inc=[("s", 1)], wait=[probs_war[idx]])
                        scfree[sb] = te
                        texp[idx] = te
                        tsr[idx] = pr.op("vector", lambda e, idx=idx: e.scalar_tensor_tensor(
                            srun[idx], srun[idx], fac[idx], esum[idx], ALU.mult, ALU.add),
                            inc=[("v", 1)], wait=[te])
                    for idx in live:
                        o = oo[idx]
                        KG = KGs[idx]
                        te = texp[idx]
                        pb = (kg * GW + idx) % 2
                        ab = (kg * GW + idx) % 2
                        for j in range(4):
                            tt = pr.op("tensor", lambda e, j=j, pb=pb, idx=idx: e.transpose(
                                ps_pt[pb][:, j * 128:(j + 1) * 128],
                                probs[idx][:, j * 128:(j + 1) * 128], ident),
                                inc=[("t", 1)] if j == 3 else (),
                                wait=[te, ptfree[pb]] if j == 0 else ())
                        probs_war[idx] = tt
                        ths = pr.op("scalar", lambda e, pb=pb: e.activation(
                            pt_h[pb], ps_pt[pb], ACT.Copy), inc=[("s", 1)],
                            wait=[tt, pt_war[pb]])
                        pr.raw_wait("vector", ths)
                        tpt = pr.op("vector", lambda e, pb=pb: e.tensor_tensor(
                            pt_l[pb], ps_pt[pb], pt_h[pb], ALU.subtract), inc=[("v", 1)])
                        ptfree[pb] = tpt
                        for j in range(4):
                            tav = _mm3(pr, ps_av[ab],
                                       pt_h[pb][:, j * 128:(j + 1) * 128],
                                       pt_l[pb][:, j * 128:(j + 1) * 128],
                                       v_h[:, (kg * 4 + j) * 128:(kg * 4 + j + 1) * 128],
                                       v_l[:, (kg * 4 + j) * 128:(kg * 4 + j + 1) * 128],
                                       start=(j == 0), stop=(j == 3),
                                       inc=[("t", 1)],
                                       wait=[tpt, avfree[ab], tvx] if j == 0 else ())
                        pt_war[pb] = tav
                        v_war = tav
                        pr.raw_wait("vector", tav)
                        tacc = pr.op("vector", lambda e, idx=idx, ab=ab: e.scalar_tensor_tensor(
                            o_acc[idx], o_acc[idx], fac[idx], ps_av[ab], ALU.mult, ALU.add),
                            inc=[("v", 1)])
                        avfree[ab] = tacc
                        fac_war[idx] = tacc
                        if kg == KG - 1:
                            tr = pr.op("vector", lambda e, idx=idx: e.reciprocal(rinv, srun[idx]),
                                       inc=[("v", 1)], wait=[tsr[idx]])
                            pr.raw_wait("vector", tr)
                            tno = pr.op("vector", lambda e, idx=idx: e.tensor_scalar_mul(
                                o_out, o_acc[idx], rinv), inc=[("v", 1)])
                            tt = pr.op("tensor", lambda e, idx=idx, ab=ab: e.transpose(
                                ps_av[ab], o_out, ident), inc=[("t", 1)],
                                wait=[tno, avfree[ab]])
                            pr.raw_wait("scalar", tt)
                            ths = pr.op("scalar", lambda e, o=o, idx=idx, ab=ab: e.activation(
                                outT_h[:, o * 128:(o + 1) * 128], ps_av[ab], ACT.Copy),
                                inc=[("s", 1)], wait=out_war)
                            pr.raw_wait("vector", ths)
                            avfree[ab] = pr.op("vector", lambda e, o=o, idx=idx, ab=ab: e.tensor_tensor(
                                outT_l[:, o * 128:(o + 1) * 128], ps_av[ab],
                                outT_h[:, o * 128:(o + 1) * 128], ALU.subtract), inc=[("v", 1)])
            tev = pr.tok("v")
            sp1 = pr.dma("scalar", lambda e, h=h: e.dma_start(outsp_hbm[h, 0], outT_h),
                         wait=[tev])
            sp2 = pr.dma("scalar", lambda e, h=h: e.dma_start(outsp_hbm[h, 1], outT_l))
            out_war = [sp1, sp2]
            spill_toks.append((sp1, sp2))
            pr.flush()

    if PHASES < 4:
        emit_taps()
        pr.flush()
        es_attn.close()
        return nc, es
    es_attn.close()
    with ExitStack() as e4:
        B = e4.enter_context
        o_accF = B(nc.sbuf_tensor("o_accF", [128, 16 * NTOK], F32)).ap()
        ow_st = B(nc.sbuf_tensor("ow_st", [128, D], F32)).ap()
        ow_h = B(nc.sbuf_tensor("ow_h", [128, D], BF16)).ap()
        ow_l = B(nc.sbuf_tensor("ow_l", [128, D], BF16)).ap()
        rb_h = B(nc.sbuf_tensor("rb_h", [128, NTOK], BF16)).ap()
        rb_l = B(nc.sbuf_tensor("rb_l", [128, NTOK], BF16)).ap()
        otok = B(nc.sbuf_tensor("otok", [128, D], F32)).ap()
        xo = B(nc.sbuf_tensor("xo4", [128, D], F32)).ap()
        sqb = B(nc.sbuf_tensor("sqb4", [128, D], F32)).ap()
        ps_o = [B(nc.psum_tensor(f"ps_o{j}", [128, 512], F32)).ap() for j in range(2)]

        NW = min(512, NTOK)
        NN = NTOK // NW
        stg_war = pr.tok("v")
        rb_war = pr.tok("t")
        psfree_o = [pr.tok("v"), pr.tok("v")]
        for h in range(NH):
            t_ows = pr.dma_cols("scalar", ow_st,
                                (lambda h: (lambda a, bb: ow_hbm[h * 128:(h + 1) * 128, a:bb]))(h),
                                D, 4, wait=[stg_war])
            t_ow = t_ows[-1]
            t_r1 = pr.dma("scalar", lambda e, h=h: e.dma_start(rb_h, outsp_hbm[h, 0]),
                          wait=[rb_war, spill_toks[h][0], spill_toks[h][1]])
            t_r2 = pr.dma("scalar", lambda e, h=h: e.dma_start(rb_l, outsp_hbm[h, 1]))
            stg_war = _split(pr, ow_st, ow_h, ow_l, inc=[("v", 1)], wait=t_ows)
            pr.raw_wait("tensor", t_r1)
            pr.raw_wait("tensor", t_r2)
            for m in range(16):
                for n in range(NN):
                    pj = (m * NN + n) % 2
                    tmm = _mm3(pr, ps_o[pj][:, 0:NW],
                               ow_h[:, m * 128:(m + 1) * 128], ow_l[:, m * 128:(m + 1) * 128],
                               rb_h[:, n * NW:(n + 1) * NW], rb_l[:, n * NW:(n + 1) * NW],
                               start=True, stop=True,
                               inc=[("t", 1)], wait=[stg_war, psfree_o[pj]])
                    c0 = m * NTOK + n * NW
                    pr.raw_wait("vector", tmm)
                    if h == 0:
                        psfree_o[pj] = pr.op("vector", lambda e, c0=c0, pj=pj: e.tensor_copy(
                            o_accF[:, c0:c0 + NW], ps_o[pj][:, 0:NW]), inc=[("v", 1)])
                    else:
                        psfree_o[pj] = pr.op("vector", lambda e, c0=c0, pj=pj: e.tensor_add(
                            o_accF[:, c0:c0 + NW], o_accF[:, c0:c0 + NW], ps_o[pj][:, 0:NW]), inc=[("v", 1)])
            rb_war = tmm
        xo_war = ()
        x1sp_toks = [None] * SLOTS
        t2sp_toks = [None] * SLOTS
        for o in range(SLOTS):
            txs4 = pr.dma_cols("sync", xo, (lambda o: (lambda a, bb: xo_hbm[o][:, a:bb]))(o),
                               D, 4, wait=xo_war)
            tx = txs4[-1]
            for g in range(4):
                pj = g % 2
                for q in range(4):
                    m = g * 4 + q
                    tt = pr.op("tensor", lambda e, m=m, q=q, pj=pj, o=o: e.transpose(
                        ps_o[pj][:, q * 128:(q + 1) * 128],
                        o_accF[:, m * NTOK + o * 128: m * NTOK + (o + 1) * 128], ident),
                        inc=[("t", 1)] if q == 3 else (),
                        wait=[psfree_o[0], psfree_o[1]] if q == 0 else ())
                pr.raw_wait("vector", tt)
                psfree_o[pj] = pr.op("vector", lambda e, g=g, pj=pj: e.tensor_copy(
                    otok[:, g * 512:(g + 1) * 512], ps_o[pj]), inc=[("v", 1)])
            ln_inplace(otok, sqb, D, EPS_LN)
            pr.raw_wait("vector", tx)
            tadd = pr.op("vector", lambda e: e.tensor_add(xo, xo, otok), inc=[("v", 1)])
            t1 = pr.dma("scalar", lambda e, o=o: e.dma_start(x1sp_hbm[o], xo), wait=[tadd])
            x1sp_toks[o] = t1
            pr.raw_wait("vector", t1)
            ln_inplace(xo, sqb, D, EPS_LN)
            t2d = pr.dma("scalar", lambda e, o=o: e.dma_start(t2sp_hbm[o], xo),
                         wait=[pr.tok("v")])
            t2sp_toks[o] = t2d
            xo_war = (t2d,)
        pr.flush()

    if PHASES < 5:
        emit_taps()
        pr.flush()
        return nc, es
    with ExitStack() as e5:
        B = e5.enter_context
        t2T_h = B(nc.sbuf_tensor("t2T_h", [128, 16 * THW], BF16)).ap()
        t2T_l = B(nc.sbuf_tensor("t2T_l", [128, 16 * THW], BF16)).ap()
        accF = B(nc.sbuf_tensor("accF", [128, 16 * THW], F32)).ap()
        t2rb = B(nc.sbuf_tensor("t2rb", [128, D], F32)).ap()
        wst = [B(nc.sbuf_tensor(f"wst{j}", [128, CC], F32)).ap() for j in range(4)]
        wg_h = B(nc.sbuf_tensor("wgc_h", [128, 16 * CC], BF16)).ap()
        wg_l = B(nc.sbuf_tensor("wgc_l", [128, 16 * CC], BF16)).ap()
        wdst = B(nc.sbuf_tensor("wdst", [128, D], F32)).ap()
        wd_h = B(nc.sbuf_tensor("wdc_h", [128, 4 * D], BF16)).ap()
        wd_l = B(nc.sbuf_tensor("wdc_l", [128, 4 * D], BF16)).ap()
        gbuf = B(nc.sbuf_tensor("gbuf", [128, HSLOTS * CC], F32)).ap()
        ubuf2 = B(nc.sbuf_tensor("ubuf2", [128, HSLOTS * CC], F32)).ap()
        ubuf = [B(nc.sbuf_tensor(f"ubuf{j}", [128, HSLOTS * CC], F32)).ap() for j in range(2)]
        abuf = B(nc.sbuf_tensor("abuf", [128, CC], F32)).ap()
        cw = B(nc.sbuf_tensor("cw", [128, 10 * (CC // 4)], F32)).ap()
        hsT_h = B(nc.sbuf_tensor("hsT_h", [128, 4 * THW], BF16)).ap()
        hsT_l = B(nc.sbuf_tensor("hsT_l", [128, 4 * THW], BF16)).ap()
        x1rb = B(nc.sbuf_tensor("x1rb", [128, D], F32)).ap()
        if HSLOTS * CC >= D:
            sqb = gbuf[:, 0:D]
        else:
            sqb = B(nc.sbuf_tensor("sqb5", [128, D], F32)).ap()
        ps_g = [B(nc.psum_tensor(f"ps_g{j}", [128, 512], F32)).ap() for j in range(max(2, HSLOTS))]
        ps_p2 = B(nc.psum_tensor("ps_p2", [128, 512], F32)).ap()
        ps_d = [B(nc.psum_tensor(f"ps_d{j}", [128, 512], F32)).ap() for j in range(2)]

        NG = CC // 4
        out_toks = []
        tsilu = [None] * HSLOTS
        tspar = [[pr.tok("v")] * HSLOTS, [pr.tok("v")] * HSLOTS]
        ub_war = [[pr.tok("t")] * HSLOTS, [pr.tok("t")] * HSLOTS]
        t2rb_war = pr.tok("t")
        psfree_g = [pr.tok("s")] * max(2, HSLOTS)
        psfree_p2 = pr.tok("v")
        psfree_d = [pr.tok("v"), pr.tok("v")]
        wst_war = [pr.tok("v")] * 4
        wd_spl = pr.tok("v")
        wd_war = pr.tok("s")
        hsT_war = pr.tok("t")
        tkc_u = None
        x1_war = ()
        for half in range(NHALF if NHALF else HALVES):
            for o4 in range(HSLOTS):
                o = half * HSLOTS + o4
                txs5 = pr.dma_cols("scalar", t2rb,
                                   (lambda o: (lambda a, bb: t2sp_hbm[o][:, a:bb]))(o),
                                   D, 4, wait=[t2rb_war, t2sp_toks[o]])
                tx = txs5[-1]
                for g in range(4):
                    pj = g % 2
                    for q in range(4):
                        kc = g * 4 + q
                        tt = pr.op("tensor", lambda e, kc=kc, q=q, pj=pj: e.transpose(
                            ps_g[pj][:, q * 128:(q + 1) * 128],
                            t2rb[:, kc * 128:(kc + 1) * 128], ident),
                            inc=[("t", 1)] if q == 3 else (),
                            wait=(txs5 + [psfree_g[pj]]) if q == 0 else ())
                    pr.raw_wait("scalar", tt)
                    c0 = (g * 4) * THW + o4 * 128
                    hap = bass.AP(t2T_h.tensor, t2T_h.offset + c0, [[t2T_h.ap[0][0], 128], [THW, 4], [1, 128]])
                    lap = bass.AP(t2T_l.tensor, t2T_l.offset + c0, [[t2T_l.ap[0][0], 128], [THW, 4], [1, 128]])
                    ths = pr.op("scalar", lambda e, hap=hap, pj=pj: e.activation(
                        hap, ps_g[pj], ACT.Copy), inc=[("s", 1)])
                    pr.raw_wait("vector", ths)
                    tev = pr.op("vector", lambda e, hap=hap, lap=lap, pj=pj: e.tensor_tensor(
                        lap, ps_g[pj], hap, ALU.subtract), inc=[("v", 1)])
                    psfree_g[pj] = tev
                t2rb_war = tt
            tt2T = pr.tok("v")
            NCC = NCH if NCH else NCHUNK
            def emit_stage_mm(w_hbm, cc, tkc_prev, is_gate):
                tkc = [None] * 16
                for kc in range(16):
                    wb = kc % 4
                    tws = pr.dma_cols("sync", wst[wb],
                                      (lambda kc, cc: (lambda a, bb: w_hbm[
                                          kc * 128:(kc + 1) * 128, cc * CC + a: cc * CC + bb]))(kc, cc),
                                      CC, 2,
                                      wait=[wst_war[wb]] + ([tkc_prev[kc]] if tkc_prev else []))
                    for _t in tws:
                        pr.raw_wait("scalar", _t)
                    if tkc_prev:
                        pr.raw_wait("scalar", tkc_prev[kc])
                    th = pr.op("scalar", lambda e, kc=kc, wb=wb: e.activation(
                        wg_h[:, kc * CC:(kc + 1) * CC], wst[wb], ACT.Copy),
                        inc=[("s", 1)])
                    pr.raw_wait("vector", th)
                    wst_war[wb] = pr.op("vector", lambda e, kc=kc, wb=wb: e.tensor_tensor(
                        wg_l[:, kc * CC:(kc + 1) * CC], wst[wb],
                        wg_h[:, kc * CC:(kc + 1) * CC], ALU.subtract), inc=[("v", 1)])
                    tspl = wst_war[wb]
                    for o4 in range(HSLOTS):
                        w = [tspl] if o4 == 0 else []
                        if kc == 0:
                            w = w + [tt2T, psfree_g[o4]]
                        tmm = _mm3(pr, ps_g[o4],
                                   t2T_h[:, kc * THW + o4 * 128: kc * THW + (o4 + 1) * 128],
                                   t2T_l[:, kc * THW + o4 * 128: kc * THW + (o4 + 1) * 128],
                                   wg_h[:, kc * CC:(kc + 1) * CC], wg_l[:, kc * CC:(kc + 1) * CC],
                                   start=(kc == 0), stop=(kc == 15),
                                   inc=[("t", 1)], wait=w)
                    tkc[kc] = tmm
                return tkc

            def emit_tail(cc):
                nonlocal psfree_p2, hsT_war, wd_spl, wd_war
                ub = ubuf[cc % 2]
                for j in range(4):
                    tws = pr.dma_cols("sync", wdst,
                                      (lambda j, cc: (lambda a, bb: wd_hbm[
                                          cc * CC + j * 128: cc * CC + (j + 1) * 128, a:bb]))(j, cc),
                                      D, 4, wait=[wd_spl])
                    for _t in tws:
                        pr.raw_wait("scalar", _t)
                    pr.raw_wait("scalar", wd_war)
                    th = pr.op("scalar", lambda e, j=j: e.activation(
                        wd_h[:, j * D:(j + 1) * D], wdst, ACT.Copy), inc=[("s", 1)])
                    pr.raw_wait("vector", th)
                    wd_spl = pr.op("vector", lambda e, j=j: e.tensor_tensor(
                        wd_l[:, j * D:(j + 1) * D], wdst, wd_h[:, j * D:(j + 1) * D],
                        ALU.subtract), inc=[("v", 1)])
                for o4 in range(HSLOTS):
                    for j in range(4):
                        tt = pr.op("tensor", lambda e, j=j, o4=o4, ub=ub: e.transpose(
                            ps_p2[:, j * 128:(j + 1) * 128],
                            ub[:, o4 * CC + j * 128: o4 * CC + (j + 1) * 128], ident),
                            inc=[("t", 1)] if j == 3 else (),
                            wait=[tspar[cc % 2][o4], psfree_p2, hsT_war] if j == 0 else ())
                    ub_war[cc % 2][o4] = tt
                    pr.raw_wait("scalar", tt)
                    c0 = o4 * 128
                    hap = bass.AP(hsT_h.tensor, hsT_h.offset + c0, [[hsT_h.ap[0][0], 128], [THW, 4], [1, 128]])
                    lap = bass.AP(hsT_l.tensor, hsT_l.offset + c0, [[hsT_l.ap[0][0], 128], [THW, 4], [1, 128]])
                    ths = pr.op("scalar", lambda e, hap=hap: e.activation(
                        hap, ps_p2, ACT.Copy), inc=[("s", 1)])
                    pr.raw_wait("vector", ths)
                    tev = pr.op("vector", lambda e, hap=hap, lap=lap: e.tensor_tensor(
                        lap, ps_p2, hap, ALU.subtract), inc=[("v", 1)])
                    psfree_p2 = tev
                thsT = pr.tok("v")
                for m in range(16):
                    pj = m % 2
                    for j in range(4):
                        tmm = _mm3(pr, ps_d[pj][:, 0:THW],
                                   wd_h[:, j * D + m * 128: j * D + (m + 1) * 128],
                                   wd_l[:, j * D + m * 128: j * D + (m + 1) * 128],
                                   hsT_h[:, j * THW:(j + 1) * THW], hsT_l[:, j * THW:(j + 1) * THW],
                                   start=(j == 0), stop=(j == 3),
                                   inc=[("t", 1)],
                                   wait=[wd_spl, thsT, psfree_d[pj]] if j == 0 else ())
                    c0 = m * THW
                    pr.raw_wait("vector", tmm)
                    if cc == 0:
                        psfree_d[pj] = pr.op("vector", lambda e, c0=c0, pj=pj: e.tensor_copy(
                            accF[:, c0:c0 + THW], ps_d[pj][:, 0:THW]), inc=[("v", 1)])
                    else:
                        psfree_d[pj] = pr.op("vector", lambda e, c0=c0, pj=pj: e.tensor_add(
                            accF[:, c0:c0 + THW], accF[:, c0:c0 + THW], ps_d[pj][:, 0:THW]),
                            inc=[("v", 1)])
                hsT_war = tmm
                wd_war = tmm

            for cc in range(NCC):
                tkc_g2 = emit_stage_mm(wg_hbm, cc, tkc_u, True)
                for o4 in range(HSLOTS):
                    tsil = pr.op("scalar", lambda e, o4=o4: e.activation(
                        gbuf[:, o4 * CC:(o4 + 1) * CC], ps_g[o4], ACT.Silu),
                        inc=[("s", 1)], wait=[tkc_g2[15]])
                    psfree_g[o4] = tsil
                    tsilu[o4] = tsil
                tkc_u = emit_stage_mm(wu_hbm, cc, tkc_g2, False)
                for o4 in range(HSLOTS):
                    pr.raw_wait("vector", tkc_u[15])
                    psfree_g[o4] = pr.op("vector", lambda e, o4=o4: e.tensor_copy(
                        ubuf2[:, o4 * CC:(o4 + 1) * CC], ps_g[o4]), inc=[("v", 1)])
                ub = ubuf[cc % 2]
                for o4 in range(HSLOTS):
                    g0 = o4 * CC
                    pr.raw_wait("vector", tsilu[o4])
                    pr.raw_wait("vector", ub_war[cc % 2][o4])
                    pr.op("vector", lambda e, g0=g0: e.tensor_mul(
                        gbuf[:, g0:g0 + CC], gbuf[:, g0:g0 + CC], ubuf2[:, g0:g0 + CC]))
                    pr.op("vector", lambda e, g0=g0: e.tensor_mul(
                        abuf, gbuf[:, g0:g0 + CC], gbuf[:, g0:g0 + CC]))
                    prs = [(0, 1), (0, 2), (0, 3), (1, 2), (1, 3), (2, 3)]
                    for k, (a, bb) in enumerate(prs):
                        pr.op("vector", lambda e, a=a, bb=bb, k=k: e.tensor_tensor(
                            cw[:, k * NG:(k + 1) * NG], abuf[:, a::4], abuf[:, bb::4], ALU.is_ge))
                    c01, c02, c03, c12, c13, c23 = [cw[:, k * NG:(k + 1) * NG] for k in range(6)]
                    s0, s1, s2, s3 = [cw[:, (6 + k) * NG:(7 + k) * NG] for k in range(4)]
                    pr.op("vector", lambda e: e.tensor_add(s0, c01, c02))
                    pr.op("vector", lambda e: e.tensor_add(s0, s0, c03))
                    pr.op("vector", lambda e: e.tensor_add(s1, c12, c13))
                    pr.op("vector", lambda e: e.tensor_sub(s1, s1, c01))
                    pr.op("vector", lambda e: e.tensor_scalar_add(s1, s1, 1.0))
                    pr.op("vector", lambda e: e.tensor_sub(s2, c23, c02))
                    pr.op("vector", lambda e: e.tensor_sub(s2, s2, c12))
                    pr.op("vector", lambda e: e.tensor_scalar_add(s2, s2, 2.0))
                    pr.op("vector", lambda e: e.tensor_add(s3, c03, c13))
                    pr.op("vector", lambda e: e.tensor_add(s3, s3, c23))
                    pr.op("vector", lambda e: e.tensor_scalar(s3, s3, -1.0, 3.0, ALU.mult, ALU.add))
                    for k, sk in enumerate((s0, s1, s2, s3)):
                        pr.op("vector", lambda e, sk=sk, k=k, g0=g0, ub=ub: e.scalar_tensor_tensor(
                            ub[:, g0 + k:g0 + CC:4], sk, 2.0, gbuf[:, g0 + k:g0 + CC:4],
                            ALU.is_ge, ALU.mult))
                    tspar[cc % 2][o4] = pr.op("vector", lambda e: e.engine_nop(), inc=[("v", 1)])
                if cc > 0:
                    emit_tail(cc - 1)
            emit_tail(NCC - 1)
            for o4 in range(HSLOTS):
                o = half * HSLOTS + o4
                txs6 = pr.dma_cols("scalar", x1rb,
                                   (lambda o: (lambda a, bb: x1sp_hbm[o][:, a:bb]))(o),
                                   D, 4, wait=list(x1_war) + [x1sp_toks[o]])
                tx = txs6[-1]
                for g in range(4):
                    pj = g % 2
                    for q in range(4):
                        m = g * 4 + q
                        tt = pr.op("tensor", lambda e, m=m, q=q, pj=pj, o4=o4: e.transpose(
                            ps_d[pj][:, q * 128:(q + 1) * 128],
                            accF[:, m * THW + o4 * 128: m * THW + (o4 + 1) * 128], ident),
                            inc=[("t", 1)] if q == 3 else (),
                            wait=[psfree_d[0], psfree_d[1]] if q == 0 else ())
                    pr.raw_wait("vector", tt)
                    psfree_d[pj] = pr.op("vector", lambda e, g=g, pj=pj: e.tensor_copy(
                        t2rb[:, g * 512:(g + 1) * 512], ps_d[pj]), inc=[("v", 1)])
                ln_inplace(t2rb, sqb, D, EPS_LN)
                pr.raw_wait("vector", tx)
                tadd = pr.op("vector", lambda e: e.tensor_add(x1rb, x1rb, t2rb), inc=[("v", 1)])
                tou = pr.dma("sync", lambda e, o=o: e.dma_start(out_hbm[o], x1rb),
                             wait=[tadd])
                out_toks.append(tou)
                x1_war = (tou,)
                t2rb_war = tadd
        for _t in out_toks:
            pr.raw_wait("sync", _t)
    emit_taps()
    pr.flush()

    return nc, es


def kernel(L0_attn_x, L0_attn_q_a_w, L0_attn_q_a_ln_w, L0_attn_q_b_w,
           L0_attn_kv_a_w, L0_attn_kv_a_ln_w, L0_attn_kv_b_w, L0_attn_o_w,
           L0_ffn_w_gate, L0_ffn_w_up, L0_ffn_w_down, _S=None):
    S = _S if _S is not None else 8192
    TT = S // 128
    SLOTS = TT // 8
    x = np.ascontiguousarray(np.asarray(L0_attn_x, dtype=np.float32).reshape(S, D))
    xt = x.reshape(TT, 128, D)

    f32 = lambda a: np.ascontiguousarray(np.asarray(a, dtype=np.float32))
    qa_w = f32(L0_attn_q_a_w)
    qb_w = f32(L0_attn_q_b_w)
    kva_w = f32(L0_attn_kv_a_w)
    kvb_w = f32(L0_attn_kv_b_w)
    o_w = f32(L0_attn_o_w)
    wg = f32(L0_ffn_w_gate)
    wu = f32(L0_ffn_w_up)
    wd = f32(L0_ffn_w_down)
    qa_ln = np.pad(np.asarray(L0_attn_q_a_ln_w, dtype=np.float64), (0, 768 - QLORA),
                   constant_values=1.0).reshape(6, 128).T.astype(np.float32)
    qa_ln = np.ascontiguousarray(qa_ln)
    kva_ln = np.ascontiguousarray(
        np.asarray(L0_attn_kv_a_ln_w, dtype=np.float32).reshape(2, 128).T)

    inv = 1.0 / THETA ** (np.arange(0, ROPE, 2, dtype=np.float64) / ROPE)
    tpos = np.arange(S, dtype=np.float64)
    fr = np.outer(tpos, inv)
    cosf = np.cos(fr).astype(np.float32).reshape(TT, 128, 32)
    sinf = np.sin(fr).astype(np.float32).reshape(TT, 128, 32)
    cosf = np.ascontiguousarray(cosf)
    sinf = np.ascontiguousarray(sinf)

    nc, es = build(S)

    in_maps = []
    for c in range(8):
        tiles = [8 * s + c for s in range(SLOTS)]
        x_own = np.ascontiguousarray(xt[tiles])
        cs_o = np.empty((128, 2 * SLOTS * 32), np.float32)
        for s, ti in enumerate(tiles):
            cs_o[:, s * 32:(s + 1) * 32] = cosf[ti]
            cs_o[:, SLOTS * 32 + s * 32: SLOTS * 32 + (s + 1) * 32] = sinf[ti]
        m = np.zeros((SLOTS, 8, 128, 128), np.float32)
        for s in range(SLOTS):
            for t in range(8):
                if t > c:
                    m[s, t] = NEG
                elif t == c:
                    m[s, t] = np.where(np.arange(128)[:, None] >= np.arange(128)[None, :],
                                       0.0, NEG)
        mask = np.ascontiguousarray(m.transpose(2, 0, 1, 3).reshape(128, SLOTS * 8 * 128)
                                    ).astype(ml_dtypes.bfloat16)
        in_maps.append({
            "x": xt, "x_own": x_own, "cos_f": cosf, "sin_f": sinf,
            "cs_o": np.ascontiguousarray(cs_o), "mask": mask,
            "qa_w": qa_w, "qa_ln": qa_ln, "qb_w": qb_w,
            "kva_w": kva_w, "kva_ln": kva_ln, "kvb_w": kvb_w, "o_w": o_w,
            "w_gate": wg, "w_up": wu, "w_down": wd,
        })

    res = run_bass_kernel_spmd(nc, in_maps, core_ids=list(range(8)))
    es.close()

    y = np.empty((TT, 128, D), np.float32)
    for c in range(8):
        oc = np.asarray(res.results[c]["out"])
        for s in range(SLOTS):
            y[8 * s + c] = oc[s]
    return y.reshape(1, S, D)


# version 15 (final)
# speedup vs baseline: 1.0162x; 1.0049x over previous
import numpy as np
import ml_dtypes
from contextlib import ExitStack
import concourse.bass as bass
import concourse.mybir as mybir
from concourse.bass_utils import run_bass_kernel_spmd

F32 = mybir.dt.float32
BF16 = mybir.dt.bfloat16
AX = mybir.AxisListType.X
ALU = mybir.AluOpType
ACT = mybir.ActivationFunctionType

D = 2048
NH = 16
QLORA = 682
KVL = 256
ROPE = 64
SCALE = float(1.0 / np.sqrt(128.0))
THETA = 128000.0
DI = 8192
CC = 512
EPS_LN = 1e-5
EPS_RMS = 1e-6
NEG = -1e9

ENGS = ("sync", "gpsimd", "scalar", "vector", "tensor")


class Pr:

    NP = 16

    def __init__(self, nc, es):
        self.nc = nc
        self.th = {e: [] for e in ENGS}
        self.sems = {}
        self.cnt = {}
        self.waited = {e: {} for e in ENGS}
        self.dma_idx = 0
        names = ["t", "v", "s", "g"] + [f"d{i}" for i in range(self.NP)]
        for name in names:
            h = es.enter_context(nc.semaphore(name))
            self.sems[name] = h
            self.cnt[name] = 0
            for e in ENGS:
                self.waited[e][name] = 0

    def tok(self, name):
        return (name, self.cnt[name])

    def op(self, eng, fn, inc=(), wait=()):
        for (s, v) in wait:
            if v > self.waited[eng][s]:
                self.waited[eng][s] = v
                h = self.sems[s]
                self.th[eng].append(lambda e, h=h, v=v: e.wait_ge(h, v))
        incs = []
        for (s, k) in inc:
            self.cnt[s] += k
            incs.append((self.sems[s], k))

        def thunk(e, fn=fn, incs=incs):
            r = fn(e)
            for (h, k) in incs:
                r = r.then_inc(h, k)
        self.th[eng].append(thunk)
        return self.tok(inc[0][0]) if inc else None

    def dma(self, eng, fn, wait=()):
        s = f"d{self.dma_idx % self.NP}"
        self.dma_idx += 1
        self.raw_wait(eng, (s, self.cnt[s]))
        return self.op(eng, fn, inc=[(s, 16)], wait=wait)

    def dma_cols(self, eng, dst, srcf, ncols, parts, wait=()):
        toks = []
        step = ncols // parts
        for p in range(parts):
            a = p * step
            b = ncols if p == parts - 1 else (p + 1) * step
            toks.append(self.dma(eng, lambda e, a=a, b=b: e.dma_start(
                dst[:, a:b], srcf(a, b)), wait=wait))
        return toks

    def raw_wait(self, eng, tok):
        s, v = tok
        if v > self.waited[eng][s]:
            self.waited[eng][s] = v
            h = self.sems[s]
            self.th[eng].append(lambda e, h=h, v=v: e.wait_ge(h, v))

    def flush(self):
        nonempty = [e for e in ENGS if self.th[e]]
        if not nonempty:
            return
        with self.nc.Block() as blk:
            for e in nonempty:
                getattr(blk, e)(lambda eng, L=list(self.th[e]): [t(eng) for t in L])
        self.th = {e: [] for e in ENGS}


def _split(pr, src, hi, lo, inc=(), wait=()):
    pr.op("vector", lambda e, s=src, h=hi: e.tensor_copy(h, s), wait=wait)
    return pr.op("vector", lambda e, s=src, h=hi, l=lo: e.tensor_tensor(l, s, h, ALU.subtract),
                 inc=inc)


def _mm3(pr, ps, lh, ll, rh, rl, start, stop, inc=(), wait=(), force_inc=False):
    pr.op("tensor", lambda e: e.matmul(ps, lh, rh, start=start, stop=False), wait=wait)
    pr.op("tensor", lambda e: e.matmul(ps, lh, rl, start=False, stop=False))
    return pr.op("tensor", lambda e: e.matmul(ps, ll, rh, start=False, stop=stop),
                 inc=inc if (stop or force_inc) else ())


def build(S=8192):
    import os
    PHASES = int(os.environ.get("K_PHASES", "5"))
    NHALF = int(os.environ.get("K_HALF", "0"))
    NCH = int(os.environ.get("K_CHUNKS", "0"))
    TAP = os.environ.get("K_TAP", "")
    TT = S // 128
    SLOTS = TT // 8
    NTOK = SLOTS * 128
    HALVES = 2 if SLOTS >= 2 else 1
    HSLOTS = SLOTS // HALVES
    THW = HSLOTS * 128
    NCHUNK = DI // CC

    nc = bass.Bass("TRN2", target_bir_lowering=False, detect_race_conditions=False)
    es = ExitStack()
    A = es.enter_context

    x_hbm = nc.dram_tensor("x", [TT, 128, D], F32, kind="ExternalInput").ap()
    xo_hbm = nc.dram_tensor("x_own", [SLOTS, 128, D], F32, kind="ExternalInput").ap()
    cosf_hbm = nc.dram_tensor("cos_f", [TT, 128, 32], F32, kind="ExternalInput").ap()
    sinf_hbm = nc.dram_tensor("sin_f", [TT, 128, 32], F32, kind="ExternalInput").ap()
    cso_hbm = nc.dram_tensor("cs_o", [128, 2 * SLOTS * 32], F32, kind="ExternalInput").ap()
    mask_hbm = nc.dram_tensor("mask", [128, SLOTS * 8 * 128], BF16, kind="ExternalInput").ap()
    qa_hbm = nc.dram_tensor("qa_w", [D, QLORA], F32, kind="ExternalInput").ap()
    qaln_hbm = nc.dram_tensor("qa_ln", [128, 6], F32, kind="ExternalInput").ap()
    qb_hbm = nc.dram_tensor("qb_w", [QLORA, D], F32, kind="ExternalInput").ap()
    kva_hbm = nc.dram_tensor("kva_w", [D, KVL + ROPE], F32, kind="ExternalInput").ap()
    kvaln_hbm = nc.dram_tensor("kva_ln", [128, 2], F32, kind="ExternalInput").ap()
    kvb_hbm = nc.dram_tensor("kvb_w", [KVL, NH * 192], F32, kind="ExternalInput").ap()
    ow_hbm = nc.dram_tensor("o_w", [D, D], F32, kind="ExternalInput").ap()
    wg_hbm = nc.dram_tensor("w_gate", [D, DI], F32, kind="ExternalInput").ap()
    wu_hbm = nc.dram_tensor("w_up", [D, DI], F32, kind="ExternalInput").ap()
    wd_hbm = nc.dram_tensor("w_down", [DI, D], F32, kind="ExternalInput").ap()
    out_hbm = nc.dram_tensor("out", [SLOTS, 128, D], F32, kind="ExternalOutput").ap()
    spk = "ExternalOutput" if TAP else "Internal"
    outsp_hbm = nc.dram_tensor("outT_sp", [NH, 2, 128, NTOK], BF16, kind=spk).ap()
    x1sp_hbm = nc.dram_tensor("x1_sp", [SLOTS, 128, D], F32, kind=spk).ap()
    t2sp_hbm = nc.dram_tensor("t2_sp", [SLOTS, 128, D], F32, kind=spk).ap()

    pr = Pr(nc, es)

    ident = A(nc.sbuf_tensor("ident", [128, 128], F32)).ap()
    qaln_sb = A(nc.sbuf_tensor("qaln_sb", [128, 6], F32)).ap()
    kvaln_sb = A(nc.sbuf_tensor("kvaln_sb", [128, 2], F32)).ap()
    s_sum = A(nc.sbuf_tensor("s_sum", [128, 1], F32)).ap()
    s_mean = A(nc.sbuf_tensor("s_mean", [128, 1], F32)).ap()
    s_ssq = A(nc.sbuf_tensor("s_ssq", [128, 1], F32)).ap()
    s_sd = A(nc.sbuf_tensor("s_sd", [128, 1], F32)).ap()
    s_rs = A(nc.sbuf_tensor("s_rs", [128, 1], F32)).ap()
    c_epsln = A(nc.sbuf_tensor("c_epsln", [128, 1], F32)).ap()
    c_epsrms = A(nc.sbuf_tensor("c_epsrms", [128, 1], F32)).ap()
    es_attn = ExitStack()
    AA = es_attn.enter_context
    cnT_h = AA(nc.sbuf_tensor("cnT_h", [128, 2 * S], BF16)).ap()
    cnT_l = AA(nc.sbuf_tensor("cnT_l", [128, 2 * S], BF16)).ap()
    kfT_h = AA(nc.sbuf_tensor("kfT_h", [128, S], BF16)).ap()
    kfT_l = AA(nc.sbuf_tensor("kfT_l", [128, S], BF16)).ap()
    qnT_h = AA(nc.sbuf_tensor("qnT_h", [128, 6 * NTOK], BF16)).ap()
    qnT_l = AA(nc.sbuf_tensor("qnT_l", [128, 6 * NTOK], BF16)).ap()
    cs_own = AA(nc.sbuf_tensor("cs_own", [128, 2 * SLOTS * 32], F32)).ap()

    EPS_AP = {}

    def emit_taps():
        if not TAP:
            return
        taps = {"cnT_h": (cnT_h, BF16, 2 * S), "cnT_l": (cnT_l, BF16, 2 * S),
                "kfT_h": (kfT_h, BF16, S), "kfT_l": (kfT_l, BF16, S),
                "qnT_h": (qnT_h, BF16, 6 * NTOK), "qnT_l": (qnT_l, BF16, 6 * NTOK)}
        for nm in TAP.split(","):
            if nm not in taps:
                continue
            ap, dt, w = taps[nm]
            th = nc.dram_tensor("tap_" + nm, [128, w], dt, kind="ExternalOutput").ap()
            tk = pr.dma("sync", lambda e, ap=ap, th=th: e.dma_start(th, ap))
            pr.raw_wait("sync", tk)

    def ln_inplace(x_ap, sqbuf, width, eps):
        pr.raw_wait("vector", pr.tok("s"))
        t1 = pr.op("vector", lambda e: e.reduce_sum(s_sum, x_ap, AX), inc=[("v", 1)])
        pr.raw_wait("vector", t1)
        t2 = pr.op("vector", lambda e: e.tensor_scalar_mul(s_mean, s_sum, 1.0 / width),
                   inc=[("v", 1)])
        pr.raw_wait("vector", t2)
        pr.op("vector", lambda e: e.tensor_scalar(x_ap, x_ap, s_mean, None, ALU.subtract))
        pr.op("vector", lambda e: e.tensor_mul(sqbuf, x_ap, x_ap))
        tv = pr.op("vector", lambda e: e.reduce_sum(s_ssq, sqbuf, AX), inc=[("v", 1)])
        eps_ap = EPS_AP[eps]
        ts = pr.op("scalar", lambda e: e.activation(s_sd, s_ssq, ACT.Sqrt,
                                                    bias=eps_ap, scale=1.0 / width),
                   inc=[("s", 1)], wait=[tv])
        tr = pr.op("vector", lambda e: e.reciprocal(s_rs, s_sd), inc=[("v", 1)], wait=[ts])
        return pr.op("vector", lambda e: e.tensor_scalar_mul(x_ap, x_ap, s_rs),
                     inc=[("v", 1)], wait=[tr])

    EPS_AP[EPS_LN] = c_epsln
    EPS_AP[EPS_RMS] = c_epsrms
    pr.op("vector", lambda e: e.memset(c_epsln, EPS_LN))
    pr.op("vector", lambda e: e.memset(c_epsrms, EPS_RMS))
    tmz = pr.op("vector", lambda e: e.memset(ident, 0.0), inc=[("v", 1)])
    pr.op("gpsimd", lambda e: e.affine_select(
        out=ident, in_=ident, compare_op=ALU.not_equal, fill=1.0, base=0,
        pattern=[[-1, 128]], channel_multiplier=1), wait=[tmz])
    pr.dma("sync", lambda e: e.dma_start(cs_own, cso_hbm))
    pr.dma("sync", lambda e: e.dma_start(qaln_sb, qaln_hbm))
    pr.dma("sync", lambda e: e.dma_start(kvaln_sb, kvaln_hbm))
    pr.flush()

    with ExitStack() as e1:
        B = e1.enter_context
        kva_st = [B(nc.sbuf_tensor(f"kva_st{j}", [128, 320], F32)).ap() for j in range(2)]
        kva_h = B(nc.sbuf_tensor("kva_h", [128, 16 * 320], BF16)).ap()
        kva_l = B(nc.sbuf_tensor("kva_l", [128, 16 * 320], BF16)).ap()
        xs = [B(nc.sbuf_tensor(f"xs{j}", [128, D], F32)).ap() for j in range(3)]
        sqb = B(nc.sbuf_tensor("sqb", [128, D], F32)).ap()
        tT_h = B(nc.sbuf_tensor("tT_h", [128, 16 * 128], BF16)).ap()
        tT_l = B(nc.sbuf_tensor("tT_l", [128, 16 * 128], BF16)).ap()
        csb = [B(nc.sbuf_tensor(f"csb{j}", [128, 64], F32)).ap() for j in range(3)]
        ctmp = B(nc.sbuf_tensor("ctmp", [128, KVL], F32)).ap()
        ckv_s = B(nc.sbuf_tensor("ckv_s", [128, KVL], F32)).ap()
        sq2 = B(nc.sbuf_tensor("sq2_b1", [128, KVL], F32)).ap()
        kp = B(nc.sbuf_tensor("kp", [128, 64], F32)).ap()
        tp_in = B(nc.sbuf_tensor("tp_in", [128, 128], F32)).ap()
        rtmp = B(nc.sbuf_tensor("rtmp", [128, 32], F32)).ap()
        ps_t = [B(nc.psum_tensor(f"ps_t{j}", [128, 512], F32)).ap() for j in range(2)]
        ps_kv_ = [B(nc.psum_tensor(f"ps_kv{j}", [128, 320], F32)).ap() for j in range(2)]
        ps_ct_ = [B(nc.psum_tensor(f"ps_ct{j}", [128, 384], F32)).ap() for j in range(2)]

        pr.op("vector", lambda e: e.memset(tp_in[:, 0:64], 0.0))
        kst_war = [pr.tok("v"), pr.tok("v")]
        for kc in range(16):
            wb = kc % 2
            tw = pr.dma("sync", lambda e, kc=kc, wb=wb: e.dma_start(
                kva_st[wb], kva_hbm[kc * 128:(kc + 1) * 128, :]), wait=[kst_war[wb]])
            pr.raw_wait("vector", tw)
            pr.op("vector", lambda e, kc=kc, wb=wb: e.tensor_copy(
                kva_h[:, kc * 320:(kc + 1) * 320], kva_st[wb]))
            kst_war[wb] = pr.op("vector", lambda e, kc=kc, wb=wb: e.tensor_tensor(
                kva_l[:, kc * 320:(kc + 1) * 320], kva_st[wb],
                kva_h[:, kc * 320:(kc + 1) * 320], ALU.subtract), inc=[("v", 1)])
        tkva = pr.tok("v")

        xfree = [pr.tok("t")] * 3
        csfree = [pr.tok("v")] * 3
        psfree_t = [pr.tok("v"), pr.tok("v")]
        psfree_kv = [pr.tok("v"), pr.tok("v")]
        psfree_ct = [pr.tok("v"), pr.tok("v")]
        tT_war = pr.tok("t")
        for i in range(TT):
            b = i % 3
            pb1 = i % 2
            ps_kv = ps_kv_[pb1]
            ps_ct = ps_ct_[pb1]
            txs = pr.dma_cols("sync", xs[b], (lambda i: (lambda a, bb: x_hbm[i][:, a:bb]))(i),
                              D, 4, wait=[xfree[b]])
            tx = txs[-1]
            tc1 = pr.dma("sync", lambda e, i=i, b=b: e.dma_start(csb[b][:, 0:32], cosf_hbm[i]),
                         wait=[csfree[b]])
            tc2 = pr.dma("sync", lambda e, i=i, b=b: e.dma_start(csb[b][:, 32:64], sinf_hbm[i]))
            tcs = [tc1, tc2]
            for _t in txs:
                pr.raw_wait("vector", _t)
            tln = ln_inplace(xs[b], sqb, D, EPS_LN)
            tev = None
            for g in range(4):
                pj = g % 2
                for q in range(4):
                    kc = g * 4 + q
                    w = [tln, psfree_t[pj], tT_war] if q == 0 else ()
                    tt = pr.op("tensor", lambda e, b=b, kc=kc, q=q, pj=pj: e.transpose(
                        ps_t[pj][:, q * 128:(q + 1) * 128],
                        xs[b][:, kc * 128:(kc + 1) * 128], ident),
                        inc=[("t", 1)] if q == 3 else (), wait=w)
                c0 = g * 512
                ths = pr.op("scalar", lambda e, c0=c0, pj=pj: e.activation(
                    tT_h[:, c0:c0 + 512], ps_t[pj], ACT.Copy), inc=[("s", 1)], wait=[tt, tkva])
                pr.raw_wait("vector", ths)
                tev = pr.op("vector", lambda e, c0=c0, pj=pj: e.tensor_tensor(
                    tT_l[:, c0:c0 + 512], ps_t[pj], tT_h[:, c0:c0 + 512], ALU.subtract),
                    inc=[("v", 1)])
                psfree_t[pj] = tev
            xfree[b] = pr.tok("t")
            for kc in range(16):
                tmm = _mm3(pr, ps_kv,
                           tT_h[:, kc * 128:(kc + 1) * 128], tT_l[:, kc * 128:(kc + 1) * 128],
                           kva_h[:, kc * 320:(kc + 1) * 320], kva_l[:, kc * 320:(kc + 1) * 320],
                           start=(kc == 0), stop=(kc == 15),
                           inc=[("t", 1)],
                           wait=[tev, psfree_kv[pb1]] if kc == 0 else ())
            tT_war = tmm
            pr.raw_wait("vector", tmm)
            pr.raw_wait("vector", pr.tok("s"))
            pr.op("vector", lambda e, ps_kv=ps_kv: e.tensor_copy(ckv_s, ps_kv[:, 0:KVL]))
            pr.op("vector", lambda e, ps_kv=ps_kv: e.tensor_copy(kp, ps_kv[:, KVL:KVL + 64]))
            psfree_kv[pb1] = pr.op("vector", lambda e: e.engine_nop(), inc=[("v", 1)])
            pr.op("vector", lambda e: e.tensor_mul(sq2[:, 0:KVL], ckv_s, ckv_s))
            tv = pr.op("vector", lambda e: e.reduce_sum(s_ssq, sq2[:, 0:KVL], AX), inc=[("v", 1)])
            ts = pr.op("scalar", lambda e: e.activation(s_sd, s_ssq, ACT.Sqrt,
                                                        bias=c_epsrms, scale=1.0 / KVL),
                       inc=[("s", 1)], wait=[tv])
            tr = pr.op("vector", lambda e: e.reciprocal(s_rs, s_sd), inc=[("v", 1)], wait=[ts])
            pr.raw_wait("vector", tr)
            pr.op("vector", lambda e: e.tensor_scalar_mul(ctmp, ckv_s, s_rs),
                  wait=[psfree_ct[pb1]])
            for _t in tcs:
                pr.raw_wait("vector", _t)
            pr.op("vector", lambda e, b=b: e.tensor_mul(tp_in[:, 64:96], kp[:, 0:64:2], csb[b][:, 0:32]))
            pr.op("vector", lambda e, b=b: e.tensor_mul(rtmp, kp[:, 1:64:2], csb[b][:, 32:64]))
            pr.op("vector", lambda e: e.tensor_sub(tp_in[:, 64:96], tp_in[:, 64:96], rtmp))
            pr.op("vector", lambda e, b=b: e.tensor_mul(tp_in[:, 96:128], kp[:, 1:64:2], csb[b][:, 0:32]))
            pr.op("vector", lambda e, b=b: e.tensor_mul(rtmp, kp[:, 0:64:2], csb[b][:, 32:64]))
            tro = pr.op("vector", lambda e: e.tensor_add(tp_in[:, 96:128], tp_in[:, 96:128], rtmp),
                        inc=[("v", 1)])
            csfree[b] = tro
            for j in range(2):
                pr.op("tensor", lambda e, j=j, ps_ct=ps_ct: e.transpose(
                    ps_ct[:, j * 128:(j + 1) * 128], ctmp[:, j * 128:(j + 1) * 128], ident),
                    wait=[tro] if j == 0 else ())
            tct = pr.op("tensor", lambda e, ps_ct=ps_ct: e.transpose(ps_ct[:, 256:384], tp_in, ident),
                        inc=[("t", 1)])
            pr.raw_wait("vector", tct)
            pr.raw_wait("scalar", tct)
            for ch in range(2):
                ths = pr.op("scalar", lambda e, ch=ch, i=i, ps_ct=ps_ct: e.activation(
                    cnT_h[:, ch * S + i * 128: ch * S + (i + 1) * 128],
                    ps_ct[:, ch * 128:(ch + 1) * 128], ACT.Copy), inc=[("s", 1)])
                pr.raw_wait("vector", ths)
                pr.op("vector", lambda e, ch=ch, i=i, ps_ct=ps_ct: e.tensor_tensor(
                    cnT_l[:, ch * S + i * 128: ch * S + (i + 1) * 128],
                    ps_ct[:, ch * 128:(ch + 1) * 128],
                    cnT_h[:, ch * S + i * 128: ch * S + (i + 1) * 128], ALU.subtract))
            pr.raw_wait("vector", pr.tok("s"))
            pr.op("vector", lambda e, i=i, ps_ct=ps_ct: e.tensor_copy(
                kfT_h[64:128, i * 128:(i + 1) * 128], ps_ct[64:128, 256:384]))
            psfree_ct[pb1] = pr.op("vector", lambda e, i=i, ps_ct=ps_ct: e.tensor_tensor(
                kfT_l[64:128, i * 128:(i + 1) * 128], ps_ct[64:128, 256:384],
                kfT_h[64:128, i * 128:(i + 1) * 128], ALU.subtract), inc=[("v", 1)])
        pr.flush()

    if PHASES < 2:
        emit_taps()
        pr.flush()
        return nc, es
    with ExitStack() as e2:
        B = e2.enter_context
        qa_st = [B(nc.sbuf_tensor(f"qa_st{j}", [128, QLORA], F32)).ap() for j in range(2)]
        qa_h = B(nc.sbuf_tensor("qa_h", [128, 16 * QLORA], BF16)).ap()
        qa_l = B(nc.sbuf_tensor("qa_l", [128, 16 * QLORA], BF16)).ap()
        xq = B(nc.sbuf_tensor("xq", [128, D], F32)).ap()
        sqb = B(nc.sbuf_tensor("sqb2", [128, D], F32)).ap()
        tT_h = B(nc.sbuf_tensor("tT2_h", [128, 16 * 128], BF16)).ap()
        tT_l = B(nc.sbuf_tensor("tT2_l", [128, 16 * 128], BF16)).ap()
        qn = B(nc.sbuf_tensor("qn", [128, QLORA], F32)).ap()
        sq2 = B(nc.sbuf_tensor("sq2_b2", [128, QLORA], F32)).ap()
        ps_t = [B(nc.psum_tensor(f"ps2_t{j}", [128, 512], F32)).ap() for j in range(2)]
        ps_qa = B(nc.psum_tensor("ps_qa", [128, 512], F32)).ap()
        ps_qb = B(nc.psum_tensor("ps_qb", [128, 170], F32)).ap()

        qst_war = [pr.tok("v"), pr.tok("v")]
        for kc in range(16):
            wb = kc % 2
            tws = pr.dma_cols("sync", qa_st[wb],
                              (lambda kc: (lambda a, bb: qa_hbm[kc * 128:(kc + 1) * 128, a:bb]))(kc),
                              QLORA, 2, wait=[qst_war[wb]])
            for _t in tws:
                pr.raw_wait("vector", _t)
            pr.op("vector", lambda e, kc=kc, wb=wb: e.tensor_copy(
                qa_h[:, kc * QLORA:(kc + 1) * QLORA], qa_st[wb]))
            qst_war[wb] = pr.op("vector", lambda e, kc=kc, wb=wb: e.tensor_tensor(
                qa_l[:, kc * QLORA:(kc + 1) * QLORA], qa_st[wb],
                qa_h[:, kc * QLORA:(kc + 1) * QLORA], ALU.subtract), inc=[("v", 1)])
        tqa = pr.tok("v")

        psfree_t = [pr.tok("v"), pr.tok("v")]
        psfree_q = pr.tok("v")
        tT_war = pr.tok("t")
        xfree = pr.tok("t")
        for o in range(SLOTS):
            txs = pr.dma_cols("sync", xq, (lambda o: (lambda a, bb: xo_hbm[o][:, a:bb]))(o),
                              D, 4, wait=[xfree])
            for _t in txs:
                pr.raw_wait("vector", _t)
            tln = ln_inplace(xq, sqb, D, EPS_LN)
            for g in range(4):
                pj = g % 2
                for q in range(4):
                    kc = g * 4 + q
                    w = [tln, psfree_t[pj], tT_war] if q == 0 else ()
                    tt = pr.op("tensor", lambda e, kc=kc, q=q, pj=pj: e.transpose(
                        ps_t[pj][:, q * 128:(q + 1) * 128],
                        xq[:, kc * 128:(kc + 1) * 128], ident),
                        inc=[("t", 1)] if q == 3 else (), wait=w)
                c0 = g * 512
                ths = pr.op("scalar", lambda e, c0=c0, pj=pj: e.activation(
                    tT_h[:, c0:c0 + 512], ps_t[pj], ACT.Copy), inc=[("s", 1)], wait=[tt, tqa])
                pr.raw_wait("vector", ths)
                tev = pr.op("vector", lambda e, c0=c0, pj=pj: e.tensor_tensor(
                    tT_l[:, c0:c0 + 512], ps_t[pj], tT_h[:, c0:c0 + 512], ALU.subtract),
                    inc=[("v", 1)])
                psfree_t[pj] = tev
            xfree = pr.tok("t")
            for kc in range(16):
                _mm3(pr, ps_qa,
                     tT_h[:, kc * 128:(kc + 1) * 128], tT_l[:, kc * 128:(kc + 1) * 128],
                     qa_h[:, kc * QLORA:kc * QLORA + 512], qa_l[:, kc * QLORA:kc * QLORA + 512],
                     start=(kc == 0), stop=(kc == 15),
                     inc=[("t", 1)], wait=[tev, psfree_q] if kc == 0 else ())
                tmm = _mm3(pr, ps_qb,
                           tT_h[:, kc * 128:(kc + 1) * 128], tT_l[:, kc * 128:(kc + 1) * 128],
                           qa_h[:, kc * QLORA + 512:(kc + 1) * QLORA],
                           qa_l[:, kc * QLORA + 512:(kc + 1) * QLORA],
                           start=(kc == 0), stop=(kc == 15), inc=[("t", 1)])
            tT_war = tmm
            xfree = tmm
            pr.raw_wait("vector", tmm)
            pr.raw_wait("vector", pr.tok("s"))
            pr.op("vector", lambda e: e.tensor_copy(qn[:, 0:512], ps_qa))
            psfree_q = pr.op("vector", lambda e: e.tensor_copy(qn[:, 512:QLORA], ps_qb),
                             inc=[("v", 1)])
            pr.op("vector", lambda e: e.tensor_mul(sq2[:, 0:QLORA], qn[:, 0:QLORA], qn[:, 0:QLORA]))
            tv = pr.op("vector", lambda e: e.reduce_sum(s_ssq, sq2[:, 0:QLORA], AX), inc=[("v", 1)])
            ts = pr.op("scalar", lambda e: e.activation(s_sd, s_ssq, ACT.Sqrt,
                                                        bias=c_epsrms, scale=1.0 / QLORA),
                       inc=[("s", 1)], wait=[tv])
            tr = pr.op("vector", lambda e: e.reciprocal(s_rs, s_sd), inc=[("v", 1)], wait=[ts])
            pr.raw_wait("vector", tr)
            tql = pr.op("vector", lambda e: e.tensor_scalar_mul(qn[:, 0:QLORA], qn[:, 0:QLORA], s_rs),
                        inc=[("v", 1)])
            psfree_q = tql
            for qc in range(6):
                wd = min(128, QLORA - qc * 128)
                pj = qc % 2
                tt = pr.op("tensor", lambda e, qc=qc, wd=wd, pj=pj: e.transpose(
                    ps_t[pj][:wd, 0:128], qn[:, qc * 128:qc * 128 + wd], ident),
                    inc=[("t", 1)], wait=[psfree_q, psfree_t[pj]])
                ths = pr.op("scalar", lambda e, qc=qc, wd=wd, pj=pj, o=o: e.activation(
                    qnT_h[:wd, qc * NTOK + o * 128: qc * NTOK + (o + 1) * 128],
                    ps_t[pj][:wd, 0:128], ACT.Copy), inc=[("s", 1)], wait=[tt])
                pr.raw_wait("vector", ths)
                psfree_t[pj] = pr.op("vector", lambda e, qc=qc, wd=wd, pj=pj, o=o: e.tensor_tensor(
                    qnT_l[:wd, qc * NTOK + o * 128: qc * NTOK + (o + 1) * 128], ps_t[pj][:wd, 0:128],
                    qnT_h[:wd, qc * NTOK + o * 128: qc * NTOK + (o + 1) * 128], ALU.subtract),
                    inc=[("v", 1)])
        pr.flush()

    if PHASES < 3:
        emit_taps()
        pr.flush()
        return nc, es
    with ExitStack() as e3:
        B = e3.enter_context
        mask_sb = B(nc.sbuf_tensor("mask_sb", [128, SLOTS * 8 * 128], BF16)).ap()
        kvb_st = B(nc.sbuf_tensor("kvb_st", [128, 384], F32)).ap()
        kvb_h = B(nc.sbuf_tensor("kvb_h", [128, 384], BF16)).ap()
        kvb_l = B(nc.sbuf_tensor("kvb_l", [128, 384], BF16)).ap()
        qb_st = B(nc.sbuf_tensor("qb_st", [128, 768], F32)).ap()
        qb_h = B(nc.sbuf_tensor("qb_h", [128, 768], BF16)).ap()
        qb_l = B(nc.sbuf_tensor("qb_l", [128, 768], BF16)).ap()
        qT_h = B(nc.sbuf_tensor("qT_h", [128, NTOK], BF16)).ap()
        qT_l = B(nc.sbuf_tensor("qT_l", [128, NTOK], BF16)).ap()
        v_h = B(nc.sbuf_tensor("v_h", [128, S], BF16)).ap()
        v_l = B(nc.sbuf_tensor("v_l", [128, S], BF16)).ap()
        outT_h = B(nc.sbuf_tensor("outT_h", [128, NTOK], BF16)).ap()
        outT_l = B(nc.sbuf_tensor("outT_l", [128, NTOK], BF16)).ap()
        rtmp = B(nc.sbuf_tensor("rtmp3", [128, 32], F32)).ap()
        GW = min(4, SLOTS)
        probs = [B(nc.sbuf_tensor(f"probs{j}", [128, 512], F32)).ap() for j in range(GW)]
        qf = probs[0][:, 0:128]
        tp_q = probs[min(1, GW - 1)][:, 0:128]
        pt_h = [B(nc.sbuf_tensor(f"pt_h{j}", [128, 512], BF16)).ap() for j in range(2)]
        pt_l = [B(nc.sbuf_tensor(f"pt_l{j}", [128, 512], BF16)).ap() for j in range(2)]
        o_acc = [B(nc.sbuf_tensor(f"o_acc{j}", [128, 128], F32)).ap() for j in range(GW)]
        o_out = B(nc.sbuf_tensor("o_out", [128, 128], F32)).ap()
        negm = [[B(nc.sbuf_tensor(f"negm{j}_{k}", [128, 1], F32)).ap() for k in range(2)]
                for j in range(GW)]
        negmx = [B(nc.sbuf_tensor(f"negmx{j}", [128, 1], F32)).ap() for j in range(GW)]
        dm = [B(nc.sbuf_tensor(f"dm{j}", [128, 1], F32)).ap() for j in range(GW)]
        fac = [B(nc.sbuf_tensor(f"fac{j}", [128, 1], F32)).ap() for j in range(GW)]
        esum = [B(nc.sbuf_tensor(f"esum{j}", [128, 1], F32)).ap() for j in range(GW)]
        srun = [B(nc.sbuf_tensor(f"srun{j}", [128, 1], F32)).ap() for j in range(GW)]
        rinv = B(nc.sbuf_tensor("rinv", [128, 1], F32)).ap()
        ps_sc = [B(nc.psum_tensor(f"ps_sc{j}", [128, 512], F32)).ap() for j in range(2)]
        ps_pt = [B(nc.psum_tensor(f"ps_pt{j}", [128, 512], F32)).ap() for j in range(2)]
        ps_v = [B(nc.psum_tensor(f"ps_v{j}", [128, 512], F32)).ap() for j in range(2)]
        ps_av = [B(nc.psum_tensor(f"ps_av{j}", [128, 128], F32)).ap() for j in range(2)]

        pr.op("vector", lambda e: e.memset(qb_st, 0.0))
        tmk = pr.dma("sync", lambda e: e.dma_start(mask_sb, mask_hbm))
        pr.raw_wait("vector", tmk)
        psfree_v = [pr.tok("v"), pr.tok("v")]
        kfT_war = pr.tok("t")
        v_war = pr.tok("t")
        qT_war = pr.tok("t")
        stg_war = pr.tok("v")
        out_war = []
        spill_toks = []
        scfree = [pr.tok("s"), pr.tok("s")]
        ptfree = [pr.tok("v"), pr.tok("v")]
        avfree = [pr.tok("v"), pr.tok("v")]
        probs_war = [pr.tok("t")] * GW
        pt_war = [pr.tok("t"), pr.tok("t")]
        for h in range(NH):
            kvb_toks = [pr.dma("scalar", lambda e, ch=ch, h=h: e.dma_start(
                kvb_st[:, ch * 192:(ch + 1) * 192],
                kvb_hbm[ch * 128:(ch + 1) * 128, h * 192:(h + 1) * 192]),
                wait=[stg_war] if ch == 0 else ()) for ch in range(2)]
            for _t in kvb_toks:
                pr.raw_wait("vector", _t)
            for ch in range(2):
                pr.op("vector", lambda e, ch=ch: e.tensor_scalar_mul(
                    kvb_st[:, ch * 192:(ch + 1) * 192], kvb_st[:, ch * 192:(ch + 1) * 192],
                    kvaln_sb[:, ch:ch + 1]))
            _split(pr, kvb_st, kvb_h, kvb_l)
            qb_toks = [pr.dma("scalar", lambda e, qc=qc, wd=min(128, QLORA - qc * 128), h=h:
                e.dma_start(qb_st[:wd, qc * 128:(qc + 1) * 128],
                            qb_hbm[qc * 128:qc * 128 + wd, h * 128:(h + 1) * 128]))
                for qc in range(6)]
            for _t in qb_toks:
                pr.raw_wait("vector", _t)
            for qc in range(6):
                pr.op("vector", lambda e, qc=qc: e.tensor_scalar_mul(
                    qb_st[:, qc * 128:(qc + 1) * 128], qb_st[:, qc * 128:(qc + 1) * 128],
                    qaln_sb[:, qc:qc + 1]))
            stg_war = _split(pr, qb_st, qb_h, qb_l, inc=[("v", 1)])
            tstg = stg_war
            psfree_q2 = pr.tok("v")
            for o in range(SLOTS):
                for qc in range(6):
                    wd = min(128, QLORA - qc * 128)
                    tmm = _mm3(pr, ps_av[0],
                               qnT_h[:wd, qc * NTOK + o * 128:qc * NTOK + (o + 1) * 128],
                               qnT_l[:wd, qc * NTOK + o * 128:qc * NTOK + (o + 1) * 128],
                               qb_h[:wd, qc * 128:(qc + 1) * 128], qb_l[:wd, qc * 128:(qc + 1) * 128],
                               start=(qc == 0), stop=(qc == 5),
                               inc=[("t", 1)],
                               wait=[tstg, qT_war, psfree_q2, avfree[0]] if qc == 0 else ())
                pr.raw_wait("vector", tmm)
                pr.op("vector", lambda e: e.tensor_scalar_mul(qf, ps_av[0], SCALE))
                avfree[0] = pr.op("vector", lambda e: e.engine_nop(), inc=[("v", 1)])
                co = o * 32
                so = SLOTS * 32 + o * 32
                pr.op("vector", lambda e: e.tensor_copy(tp_q[:, 0:64], qf[:, 0:64]))
                pr.op("vector", lambda e, co=co: e.tensor_mul(tp_q[:, 64:96], qf[:, 64:128:2], cs_own[:, co:co + 32]))
                pr.op("vector", lambda e, so=so: e.tensor_mul(rtmp, qf[:, 65:128:2], cs_own[:, so:so + 32]))
                pr.op("vector", lambda e: e.tensor_sub(tp_q[:, 64:96], tp_q[:, 64:96], rtmp))
                pr.op("vector", lambda e, co=co: e.tensor_mul(tp_q[:, 96:128], qf[:, 65:128:2], cs_own[:, co:co + 32]))
                pr.op("vector", lambda e, so=so: e.tensor_mul(rtmp, qf[:, 64:128:2], cs_own[:, so:so + 32]))
                tq2 = pr.op("vector", lambda e: e.tensor_add(tp_q[:, 96:128], tp_q[:, 96:128], rtmp),
                            inc=[("v", 1)])
                tt = pr.op("tensor", lambda e: e.transpose(ps_av[1], tp_q, ident),
                           inc=[("t", 1)], wait=[tq2, avfree[1]])
                pr.raw_wait("scalar", tt)
                ths = pr.op("scalar", lambda e, o=o: e.activation(
                    qT_h[:, o * 128:(o + 1) * 128], ps_av[1], ACT.Copy), inc=[("s", 1)])
                pr.raw_wait("vector", ths)
                psfree_q2 = pr.op("vector", lambda e, o=o: e.tensor_tensor(
                    qT_l[:, o * 128:(o + 1) * 128], ps_av[1], qT_h[:, o * 128:(o + 1) * 128],
                    ALU.subtract), inc=[("v", 1)])
                avfree[1] = psfree_q2
            tqT = pr.tok("v")
            for gi in range(TT // 4):
                kg = gi
                kb = kg % 2
                for ch in range(2):
                    tmm = _mm3(pr, ps_pt[kb][0:64, :],
                               kvb_h[:, ch * 192:ch * 192 + 64],
                               kvb_l[:, ch * 192:ch * 192 + 64],
                               cnT_h[:, ch * S + kg * 512: ch * S + (kg + 1) * 512],
                               cnT_l[:, ch * S + kg * 512: ch * S + (kg + 1) * 512],
                               start=(ch == 0), stop=(ch == 1),
                               inc=[("t", 1)],
                               wait=[stg_war, kfT_war, ptfree[kb]] if ch == 0 else ())
                pr.raw_wait("scalar", tmm)
                ths = pr.op("scalar", lambda e, kg=kg, kb=kb: e.activation(
                    kfT_h[0:64, kg * 512:(kg + 1) * 512], ps_pt[kb][0:64, :], ACT.Copy),
                    inc=[("s", 1)])
                pr.raw_wait("vector", ths)
                ptfree[kb] = pr.op("vector", lambda e, kg=kg, kb=kb: e.tensor_tensor(
                    kfT_l[0:64, kg * 512:(kg + 1) * 512], ps_pt[kb][0:64, :],
                    kfT_h[0:64, kg * 512:(kg + 1) * 512], ALU.subtract), inc=[("v", 1)])
                vg = gi
                vb = vg % 2
                for k4 in range(4):
                    kt = vg * 4 + k4
                    for ch in range(2):
                        w = ()
                        if k4 == 0 and ch == 0:
                            w = [psfree_v[vb]] + ([v_war, tstg] if vg == 0 else [])
                        tmm = _mm3(pr, ps_v[vb][:, k4 * 128:(k4 + 1) * 128],
                                   cnT_h[:, ch * S + kt * 128: ch * S + (kt + 1) * 128],
                                   cnT_l[:, ch * S + kt * 128: ch * S + (kt + 1) * 128],
                                   kvb_h[:, ch * 192 + 64:(ch + 1) * 192],
                                   kvb_l[:, ch * 192 + 64:(ch + 1) * 192],
                                   start=(ch == 0), stop=(ch == 1),
                                   inc=[("t", 1)], wait=w)
                pr.raw_wait("scalar", tmm)
                ths = pr.op("scalar", lambda e, vg=vg, vb=vb: e.activation(
                    v_h[:, vg * 512:(vg + 1) * 512], ps_v[vb], ACT.Copy), inc=[("s", 1)])
                pr.raw_wait("vector", ths)
                psfree_v[vb] = pr.op("vector", lambda e, vg=vg, vb=vb: e.tensor_tensor(
                    v_l[:, vg * 512:(vg + 1) * 512], ps_v[vb],
                    v_h[:, vg * 512:(vg + 1) * 512], ALU.subtract), inc=[("v", 1)])
            tkn = pr.tok("v")
            tvx = pr.tok("v")
            for p in range(SLOTS // GW):
                oo = [GW * p + i for i in range(GW)]
                KGs = [2 * o + 2 for o in oo]
                tini = [None] * GW
                texp = [None] * GW
                tscv = [None] * GW
                fac_war = [pr.tok("v")] * GW
                tsr = [pr.tok("v")] * GW
                negmx_war = [[] for _ in range(GW)]
                for idx in range(GW):
                    pr.op("vector", lambda e, idx=idx: e.memset(negm[idx][1], 1e30),
                          wait=[pr.tok("g")])
                    pr.op("vector", lambda e, idx=idx: e.memset(srun[idx], 0.0),
                          wait=[tsr[idx]])
                    tini[idx] = pr.op("vector", lambda e, idx=idx: e.memset(o_acc[idx], 0.0),
                                      inc=[("v", 1)])
                for kg in range(KGs[GW - 1]):
                    live = [i for i in range(GW) if kg < KGs[i]]
                    for idx in live:
                        o = oo[idx]
                        KG = KGs[idx]
                        sb = (kg * GW + idx) % 2
                        tsc = _mm3(pr, ps_sc[sb],
                                   qT_h[:, o * 128:(o + 1) * 128], qT_l[:, o * 128:(o + 1) * 128],
                                   kfT_h[:, kg * 512:(kg + 1) * 512], kfT_l[:, kg * 512:(kg + 1) * 512],
                                   start=True, stop=True,
                                   inc=[("t", 1)],
                                   wait=[tqT, tkn, scfree[sb]] if kg == 0 else [scfree[sb]])
                        kfT_war = tsc
                        qT_war = tsc
                        tscv[idx] = (tsc, sb)
                        pr.raw_wait("vector", tsc)
                        if kg >= KG - 2:
                            mo = (o * 8 + (kg - (KG - 2)) * 4) * 128
                            pr.op("vector", lambda e, sb=sb, mo=mo: e.tensor_add(
                                ps_sc[sb], ps_sc[sb], mask_sb[:, mo:mo + 512]))
                        tmx = pr.op("vector", lambda e, idx=idx, sb=sb: e.reduce_max(
                            negmx[idx], ps_sc[sb], AX, negate=True),
                            inc=[("v", 1)],
                            wait=([tini[idx]] if kg == 0 else []) + negmx_war[idx])
                        pr.raw_wait("vector", tmx)
                        pr.op("vector", lambda e, idx=idx, kg=kg: e.tensor_scalar(
                            dm[idx], negmx[idx], negm[idx][1 - kg % 2], negm[idx][1 - kg % 2],
                            ALU.min, ALU.subtract))
                        tvm = pr.op("vector", lambda e, idx=idx, kg=kg: e.tensor_single_scalar(
                            negm[idx][kg % 2], negmx[idx], negm[idx][1 - kg % 2], ALU.min),
                            inc=[("v", 1)])
                        pr.op("scalar", lambda e, idx=idx: e.activation(fac[idx], dm[idx], ACT.Exp),
                              inc=[("s", 1)], wait=[tvm, fac_war[idx]])
                        te = pr.op("scalar", lambda e, idx=idx, kg=kg, sb=sb: e.activation(
                            probs[idx], ps_sc[sb], ACT.Exp, bias=negm[idx][kg % 2],
                            accum_out=esum[idx]),
                            inc=[("s", 1)], wait=[probs_war[idx]])
                        scfree[sb] = te
                        texp[idx] = te
                        tsr[idx] = pr.op("vector", lambda e, idx=idx: e.scalar_tensor_tensor(
                            srun[idx], srun[idx], fac[idx], esum[idx], ALU.mult, ALU.add),
                            inc=[("v", 1)], wait=[te])
                    for idx in live:
                        o = oo[idx]
                        KG = KGs[idx]
                        te = texp[idx]
                        pb = (kg * GW + idx) % 2
                        ab = (kg * GW + idx) % 2
                        for j in range(4):
                            tt = pr.op("tensor", lambda e, j=j, pb=pb, idx=idx: e.transpose(
                                ps_pt[pb][:, j * 128:(j + 1) * 128],
                                probs[idx][:, j * 128:(j + 1) * 128], ident),
                                inc=[("t", 1)] if j == 3 else (),
                                wait=[te, ptfree[pb]] if j == 0 else ())
                        probs_war[idx] = tt
                        ths = pr.op("scalar", lambda e, pb=pb: e.activation(
                            pt_h[pb], ps_pt[pb], ACT.Copy), inc=[("s", 1)],
                            wait=[tt, pt_war[pb]])
                        pr.raw_wait("vector", ths)
                        tpt = pr.op("vector", lambda e, pb=pb: e.tensor_tensor(
                            pt_l[pb], ps_pt[pb], pt_h[pb], ALU.subtract), inc=[("v", 1)])
                        ptfree[pb] = tpt
                        for j in range(4):
                            tav = _mm3(pr, ps_av[ab],
                                       pt_h[pb][:, j * 128:(j + 1) * 128],
                                       pt_l[pb][:, j * 128:(j + 1) * 128],
                                       v_h[:, (kg * 4 + j) * 128:(kg * 4 + j + 1) * 128],
                                       v_l[:, (kg * 4 + j) * 128:(kg * 4 + j + 1) * 128],
                                       start=(j == 0), stop=(j == 3),
                                       inc=[("t", 1)],
                                       wait=[tpt, avfree[ab], tvx] if j == 0 else ())
                        pt_war[pb] = tav
                        v_war = tav
                        pr.raw_wait("vector", tav)
                        tacc = pr.op("vector", lambda e, idx=idx, ab=ab: e.scalar_tensor_tensor(
                            o_acc[idx], o_acc[idx], fac[idx], ps_av[ab], ALU.mult, ALU.add),
                            inc=[("v", 1)])
                        avfree[ab] = tacc
                        fac_war[idx] = tacc
                        if kg == KG - 1:
                            tr = pr.op("vector", lambda e, idx=idx: e.reciprocal(rinv, srun[idx]),
                                       inc=[("v", 1)], wait=[tsr[idx]])
                            pr.raw_wait("vector", tr)
                            tno = pr.op("vector", lambda e, idx=idx: e.tensor_scalar_mul(
                                o_out, o_acc[idx], rinv), inc=[("v", 1)])
                            tt = pr.op("tensor", lambda e, idx=idx, ab=ab: e.transpose(
                                ps_av[ab], o_out, ident), inc=[("t", 1)],
                                wait=[tno, avfree[ab]])
                            pr.raw_wait("scalar", tt)
                            ths = pr.op("scalar", lambda e, o=o, idx=idx, ab=ab: e.activation(
                                outT_h[:, o * 128:(o + 1) * 128], ps_av[ab], ACT.Copy),
                                inc=[("s", 1)], wait=out_war)
                            pr.raw_wait("vector", ths)
                            avfree[ab] = pr.op("vector", lambda e, o=o, idx=idx, ab=ab: e.tensor_tensor(
                                outT_l[:, o * 128:(o + 1) * 128], ps_av[ab],
                                outT_h[:, o * 128:(o + 1) * 128], ALU.subtract), inc=[("v", 1)])
            tev = pr.tok("v")
            sp1 = pr.dma("scalar", lambda e, h=h: e.dma_start(outsp_hbm[h, 0], outT_h),
                         wait=[tev])
            sp2 = pr.dma("scalar", lambda e, h=h: e.dma_start(outsp_hbm[h, 1], outT_l))
            out_war = [sp1, sp2]
            spill_toks.append((sp1, sp2))
            pr.flush()

    if PHASES < 4:
        emit_taps()
        pr.flush()
        es_attn.close()
        return nc, es
    es_attn.close()
    with ExitStack() as e4:
        B = e4.enter_context
        o_accF = B(nc.sbuf_tensor("o_accF", [128, 16 * NTOK], F32)).ap()
        ow_st = B(nc.sbuf_tensor("ow_st", [128, D], F32)).ap()
        ow_h = B(nc.sbuf_tensor("ow_h", [128, D], BF16)).ap()
        ow_l = B(nc.sbuf_tensor("ow_l", [128, D], BF16)).ap()
        rb_h = B(nc.sbuf_tensor("rb_h", [128, NTOK], BF16)).ap()
        rb_l = B(nc.sbuf_tensor("rb_l", [128, NTOK], BF16)).ap()
        otok = B(nc.sbuf_tensor("otok", [128, D], F32)).ap()
        xo = B(nc.sbuf_tensor("xo4", [128, D], F32)).ap()
        sqb = B(nc.sbuf_tensor("sqb4", [128, D], F32)).ap()
        ps_o = [B(nc.psum_tensor(f"ps_o{j}", [128, 512], F32)).ap() for j in range(2)]

        NW = min(512, NTOK)
        NN = NTOK // NW
        stg_war = pr.tok("v")
        rb_war = pr.tok("t")
        psfree_o = [pr.tok("v"), pr.tok("v")]
        for h in range(NH):
            t_ows = pr.dma_cols("scalar", ow_st,
                                (lambda h: (lambda a, bb: ow_hbm[h * 128:(h + 1) * 128, a:bb]))(h),
                                D, 4, wait=[stg_war])
            t_ow = t_ows[-1]
            t_r1 = pr.dma("scalar", lambda e, h=h: e.dma_start(rb_h, outsp_hbm[h, 0]),
                          wait=[rb_war, spill_toks[h][0], spill_toks[h][1]])
            t_r2 = pr.dma("scalar", lambda e, h=h: e.dma_start(rb_l, outsp_hbm[h, 1]))
            stg_war = _split(pr, ow_st, ow_h, ow_l, inc=[("v", 1)], wait=t_ows)
            pr.raw_wait("tensor", t_r1)
            pr.raw_wait("tensor", t_r2)
            for m in range(16):
                for n in range(NN):
                    pj = (m * NN + n) % 2
                    tmm = _mm3(pr, ps_o[pj][:, 0:NW],
                               ow_h[:, m * 128:(m + 1) * 128], ow_l[:, m * 128:(m + 1) * 128],
                               rb_h[:, n * NW:(n + 1) * NW], rb_l[:, n * NW:(n + 1) * NW],
                               start=True, stop=True,
                               inc=[("t", 1)], wait=[stg_war, psfree_o[pj]])
                    c0 = m * NTOK + n * NW
                    pr.raw_wait("vector", tmm)
                    if h == 0:
                        psfree_o[pj] = pr.op("vector", lambda e, c0=c0, pj=pj: e.tensor_copy(
                            o_accF[:, c0:c0 + NW], ps_o[pj][:, 0:NW]), inc=[("v", 1)])
                    else:
                        psfree_o[pj] = pr.op("vector", lambda e, c0=c0, pj=pj: e.tensor_add(
                            o_accF[:, c0:c0 + NW], o_accF[:, c0:c0 + NW], ps_o[pj][:, 0:NW]), inc=[("v", 1)])
            rb_war = tmm
        xo_war = ()
        x1sp_toks = [None] * SLOTS
        t2sp_toks = [None] * SLOTS
        for o in range(SLOTS):
            txs4 = pr.dma_cols("sync", xo, (lambda o: (lambda a, bb: xo_hbm[o][:, a:bb]))(o),
                               D, 4, wait=xo_war)
            tx = txs4[-1]
            for g in range(4):
                pj = g % 2
                for q in range(4):
                    m = g * 4 + q
                    tt = pr.op("tensor", lambda e, m=m, q=q, pj=pj, o=o: e.transpose(
                        ps_o[pj][:, q * 128:(q + 1) * 128],
                        o_accF[:, m * NTOK + o * 128: m * NTOK + (o + 1) * 128], ident),
                        inc=[("t", 1)] if q == 3 else (),
                        wait=[psfree_o[0], psfree_o[1]] if q == 0 else ())
                pr.raw_wait("vector", tt)
                psfree_o[pj] = pr.op("vector", lambda e, g=g, pj=pj: e.tensor_copy(
                    otok[:, g * 512:(g + 1) * 512], ps_o[pj]), inc=[("v", 1)])
            ln_inplace(otok, sqb, D, EPS_LN)
            pr.raw_wait("vector", tx)
            tadd = pr.op("vector", lambda e: e.tensor_add(xo, xo, otok), inc=[("v", 1)])
            t1 = pr.dma("scalar", lambda e, o=o: e.dma_start(x1sp_hbm[o], xo), wait=[tadd])
            x1sp_toks[o] = t1
            pr.raw_wait("vector", t1)
            ln_inplace(xo, sqb, D, EPS_LN)
            t2d = pr.dma("scalar", lambda e, o=o: e.dma_start(t2sp_hbm[o], xo),
                         wait=[pr.tok("v")])
            t2sp_toks[o] = t2d
            xo_war = (t2d,)
        pr.flush()

    if PHASES < 5:
        emit_taps()
        pr.flush()
        return nc, es
    with ExitStack() as e5:
        B = e5.enter_context
        t2T_h = B(nc.sbuf_tensor("t2T_h", [128, 16 * THW], BF16)).ap()
        t2T_l = B(nc.sbuf_tensor("t2T_l", [128, 16 * THW], BF16)).ap()
        accF = B(nc.sbuf_tensor("accF", [128, 16 * THW], F32)).ap()
        t2rb = B(nc.sbuf_tensor("t2rb", [128, D], F32)).ap()
        wst = [B(nc.sbuf_tensor(f"wst{j}", [128, CC], F32)).ap() for j in range(4)]
        wg_h = B(nc.sbuf_tensor("wgc_h", [128, 16 * CC], BF16)).ap()
        wg_l = B(nc.sbuf_tensor("wgc_l", [128, 16 * CC], BF16)).ap()
        wdst = B(nc.sbuf_tensor("wdst", [128, D], F32)).ap()
        wd_h = B(nc.sbuf_tensor("wdc_h", [128, 4 * D], BF16)).ap()
        wd_l = B(nc.sbuf_tensor("wdc_l", [128, 4 * D], BF16)).ap()
        gbuf = B(nc.sbuf_tensor("gbuf", [128, HSLOTS * CC], F32)).ap()
        ubuf2 = B(nc.sbuf_tensor("ubuf2", [128, HSLOTS * CC], F32)).ap()
        ubuf = [B(nc.sbuf_tensor(f"ubuf{j}", [128, HSLOTS * CC], F32)).ap() for j in range(2)]
        abuf = B(nc.sbuf_tensor("abuf", [128, CC], F32)).ap()
        cw = B(nc.sbuf_tensor("cw", [128, 10 * (CC // 4)], F32)).ap()
        hsT_h = B(nc.sbuf_tensor("hsT_h", [128, 4 * THW], BF16)).ap()
        hsT_l = B(nc.sbuf_tensor("hsT_l", [128, 4 * THW], BF16)).ap()
        x1rb = B(nc.sbuf_tensor("x1rb", [128, D], F32)).ap()
        if HSLOTS * CC >= D:
            sqb = gbuf[:, 0:D]
        else:
            sqb = B(nc.sbuf_tensor("sqb5", [128, D], F32)).ap()
        ps_g = [B(nc.psum_tensor(f"ps_g{j}", [128, 512], F32)).ap() for j in range(max(2, HSLOTS))]
        ps_p2 = B(nc.psum_tensor("ps_p2", [128, 512], F32)).ap()
        ps_d = [B(nc.psum_tensor(f"ps_d{j}", [128, 512], F32)).ap() for j in range(2)]

        NG = CC // 4
        out_toks = []
        tsilu = [None] * HSLOTS
        tspar = [[pr.tok("v")] * HSLOTS, [pr.tok("v")] * HSLOTS]
        ub_war = [[pr.tok("t")] * HSLOTS, [pr.tok("t")] * HSLOTS]
        t2rb_war = pr.tok("t")
        psfree_g = [pr.tok("s")] * max(2, HSLOTS)
        psfree_p2 = pr.tok("v")
        psfree_d = [pr.tok("v"), pr.tok("v")]
        wst_war = [pr.tok("v")] * 4
        wd_spl = pr.tok("v")
        wd_war = pr.tok("s")
        hsT_war = pr.tok("t")
        tkc_u = None
        x1_war = ()
        for half in range(NHALF if NHALF else HALVES):
            for o4 in range(HSLOTS):
                o = half * HSLOTS + o4
                txs5 = pr.dma_cols("scalar", t2rb,
                                   (lambda o: (lambda a, bb: t2sp_hbm[o][:, a:bb]))(o),
                                   D, 4, wait=[t2rb_war, t2sp_toks[o]])
                tx = txs5[-1]
                for g in range(4):
                    pj = g % 2
                    for q in range(4):
                        kc = g * 4 + q
                        tt = pr.op("tensor", lambda e, kc=kc, q=q, pj=pj: e.transpose(
                            ps_g[pj][:, q * 128:(q + 1) * 128],
                            t2rb[:, kc * 128:(kc + 1) * 128], ident),
                            inc=[("t", 1)] if q == 3 else (),
                            wait=(txs5 + [psfree_g[pj]]) if q == 0 else ())
                    pr.raw_wait("scalar", tt)
                    c0 = (g * 4) * THW + o4 * 128
                    hap = bass.AP(t2T_h.tensor, t2T_h.offset + c0, [[t2T_h.ap[0][0], 128], [THW, 4], [1, 128]])
                    lap = bass.AP(t2T_l.tensor, t2T_l.offset + c0, [[t2T_l.ap[0][0], 128], [THW, 4], [1, 128]])
                    ths = pr.op("scalar", lambda e, hap=hap, pj=pj: e.activation(
                        hap, ps_g[pj], ACT.Copy), inc=[("s", 1)])
                    pr.raw_wait("vector", ths)
                    tev = pr.op("vector", lambda e, hap=hap, lap=lap, pj=pj: e.tensor_tensor(
                        lap, ps_g[pj], hap, ALU.subtract), inc=[("v", 1)])
                    psfree_g[pj] = tev
                t2rb_war = tt
            tt2T = pr.tok("v")
            NCC = NCH if NCH else NCHUNK
            def emit_stage_mm(w_hbm, cc, tkc_prev, is_gate):
                tkc = [None] * 16
                for kc in range(16):
                    wb = kc % 4
                    tws = pr.dma_cols("sync", wst[wb],
                                      (lambda kc, cc: (lambda a, bb: w_hbm[
                                          kc * 128:(kc + 1) * 128, cc * CC + a: cc * CC + bb]))(kc, cc),
                                      CC, 2,
                                      wait=[wst_war[wb]] + ([tkc_prev[kc]] if tkc_prev else []))
                    for _t in tws:
                        pr.raw_wait("scalar", _t)
                    if tkc_prev:
                        pr.raw_wait("scalar", tkc_prev[kc])
                    th = pr.op("scalar", lambda e, kc=kc, wb=wb: e.activation(
                        wg_h[:, kc * CC:(kc + 1) * CC], wst[wb], ACT.Copy),
                        inc=[("s", 1)])
                    pr.raw_wait("vector", th)
                    wst_war[wb] = pr.op("vector", lambda e, kc=kc, wb=wb: e.tensor_tensor(
                        wg_l[:, kc * CC:(kc + 1) * CC], wst[wb],
                        wg_h[:, kc * CC:(kc + 1) * CC], ALU.subtract), inc=[("v", 1)])
                    tspl = wst_war[wb]
                    for o4 in range(HSLOTS):
                        w = [tspl] if o4 == 0 else []
                        if kc == 0:
                            w = w + [tt2T, psfree_g[o4]]
                        tmm = _mm3(pr, ps_g[o4],
                                   t2T_h[:, kc * THW + o4 * 128: kc * THW + (o4 + 1) * 128],
                                   t2T_l[:, kc * THW + o4 * 128: kc * THW + (o4 + 1) * 128],
                                   wg_h[:, kc * CC:(kc + 1) * CC], wg_l[:, kc * CC:(kc + 1) * CC],
                                   start=(kc == 0), stop=(kc == 15),
                                   inc=[("t", 1)], wait=w,
                                   force_inc=(o4 == HSLOTS - 1))
                    tkc[kc] = tmm
                return tkc

            def emit_tail(cc):
                nonlocal psfree_p2, hsT_war, wd_spl, wd_war
                ub = ubuf[cc % 2]
                for j in range(4):
                    tws = pr.dma_cols("sync", wdst,
                                      (lambda j, cc: (lambda a, bb: wd_hbm[
                                          cc * CC + j * 128: cc * CC + (j + 1) * 128, a:bb]))(j, cc),
                                      D, 4, wait=[wd_spl])
                    for _t in tws:
                        pr.raw_wait("scalar", _t)
                    pr.raw_wait("scalar", wd_war)
                    th = pr.op("scalar", lambda e, j=j: e.activation(
                        wd_h[:, j * D:(j + 1) * D], wdst, ACT.Copy), inc=[("s", 1)])
                    pr.raw_wait("vector", th)
                    wd_spl = pr.op("vector", lambda e, j=j: e.tensor_tensor(
                        wd_l[:, j * D:(j + 1) * D], wdst, wd_h[:, j * D:(j + 1) * D],
                        ALU.subtract), inc=[("v", 1)])
                for o4 in range(HSLOTS):
                    for j in range(4):
                        tt = pr.op("tensor", lambda e, j=j, o4=o4, ub=ub: e.transpose(
                            ps_p2[:, j * 128:(j + 1) * 128],
                            ub[:, o4 * CC + j * 128: o4 * CC + (j + 1) * 128], ident),
                            inc=[("t", 1)] if j == 3 else (),
                            wait=[tspar[cc % 2][o4], psfree_p2, hsT_war] if j == 0 else ())
                    ub_war[cc % 2][o4] = tt
                    pr.raw_wait("scalar", tt)
                    c0 = o4 * 128
                    hap = bass.AP(hsT_h.tensor, hsT_h.offset + c0, [[hsT_h.ap[0][0], 128], [THW, 4], [1, 128]])
                    lap = bass.AP(hsT_l.tensor, hsT_l.offset + c0, [[hsT_l.ap[0][0], 128], [THW, 4], [1, 128]])
                    ths = pr.op("scalar", lambda e, hap=hap: e.activation(
                        hap, ps_p2, ACT.Copy), inc=[("s", 1)])
                    pr.raw_wait("vector", ths)
                    tev = pr.op("vector", lambda e, hap=hap, lap=lap: e.tensor_tensor(
                        lap, ps_p2, hap, ALU.subtract), inc=[("v", 1)])
                    psfree_p2 = tev
                thsT = pr.tok("v")
                for m in range(16):
                    pj = m % 2
                    for j in range(4):
                        tmm = _mm3(pr, ps_d[pj][:, 0:THW],
                                   wd_h[:, j * D + m * 128: j * D + (m + 1) * 128],
                                   wd_l[:, j * D + m * 128: j * D + (m + 1) * 128],
                                   hsT_h[:, j * THW:(j + 1) * THW], hsT_l[:, j * THW:(j + 1) * THW],
                                   start=(j == 0), stop=(j == 3),
                                   inc=[("t", 1)],
                                   wait=[wd_spl, thsT, psfree_d[pj]] if j == 0 else ())
                    c0 = m * THW
                    pr.raw_wait("vector", tmm)
                    if cc == 0:
                        psfree_d[pj] = pr.op("vector", lambda e, c0=c0, pj=pj: e.tensor_copy(
                            accF[:, c0:c0 + THW], ps_d[pj][:, 0:THW]), inc=[("v", 1)])
                    else:
                        psfree_d[pj] = pr.op("vector", lambda e, c0=c0, pj=pj: e.tensor_add(
                            accF[:, c0:c0 + THW], accF[:, c0:c0 + THW], ps_d[pj][:, 0:THW]),
                            inc=[("v", 1)])
                hsT_war = tmm
                wd_war = tmm

            for cc in range(NCC):
                tkc_g2 = emit_stage_mm(wg_hbm, cc, tkc_u, True)
                for o4 in range(HSLOTS):
                    tsil = pr.op("scalar", lambda e, o4=o4: e.activation(
                        gbuf[:, o4 * CC:(o4 + 1) * CC], ps_g[o4], ACT.Silu),
                        inc=[("s", 1)], wait=[tkc_g2[15]])
                    psfree_g[o4] = tsil
                    tsilu[o4] = tsil
                tkc_u = emit_stage_mm(wu_hbm, cc, tkc_g2, False)
                for o4 in range(HSLOTS):
                    pr.raw_wait("vector", tkc_u[15])
                    psfree_g[o4] = pr.op("vector", lambda e, o4=o4: e.tensor_copy(
                        ubuf2[:, o4 * CC:(o4 + 1) * CC], ps_g[o4]), inc=[("v", 1)])
                ub = ubuf[cc % 2]
                for o4 in range(HSLOTS):
                    g0 = o4 * CC
                    pr.raw_wait("vector", tsilu[o4])
                    pr.raw_wait("vector", ub_war[cc % 2][o4])
                    pr.op("vector", lambda e, g0=g0: e.tensor_mul(
                        gbuf[:, g0:g0 + CC], gbuf[:, g0:g0 + CC], ubuf2[:, g0:g0 + CC]))
                    pr.op("vector", lambda e, g0=g0: e.tensor_mul(
                        abuf, gbuf[:, g0:g0 + CC], gbuf[:, g0:g0 + CC]))
                    prs = [(0, 1), (0, 2), (0, 3), (1, 2), (1, 3), (2, 3)]
                    for k, (a, bb) in enumerate(prs):
                        pr.op("vector", lambda e, a=a, bb=bb, k=k: e.tensor_tensor(
                            cw[:, k * NG:(k + 1) * NG], abuf[:, a::4], abuf[:, bb::4], ALU.is_ge))
                    c01, c02, c03, c12, c13, c23 = [cw[:, k * NG:(k + 1) * NG] for k in range(6)]
                    s0, s1, s2, s3 = [cw[:, (6 + k) * NG:(7 + k) * NG] for k in range(4)]
                    pr.op("vector", lambda e: e.tensor_add(s0, c01, c02))
                    pr.op("vector", lambda e: e.tensor_add(s0, s0, c03))
                    pr.op("vector", lambda e: e.tensor_add(s1, c12, c13))
                    pr.op("vector", lambda e: e.tensor_sub(s1, s1, c01))
                    pr.op("vector", lambda e: e.tensor_scalar_add(s1, s1, 1.0))
                    pr.op("vector", lambda e: e.tensor_sub(s2, c23, c02))
                    pr.op("vector", lambda e: e.tensor_sub(s2, s2, c12))
                    pr.op("vector", lambda e: e.tensor_scalar_add(s2, s2, 2.0))
                    pr.op("vector", lambda e: e.tensor_add(s3, c03, c13))
                    pr.op("vector", lambda e: e.tensor_add(s3, s3, c23))
                    pr.op("vector", lambda e: e.tensor_scalar(s3, s3, -1.0, 3.0, ALU.mult, ALU.add))
                    for k, sk in enumerate((s0, s1, s2, s3)):
                        pr.op("vector", lambda e, sk=sk, k=k, g0=g0, ub=ub: e.scalar_tensor_tensor(
                            ub[:, g0 + k:g0 + CC:4], sk, 2.0, gbuf[:, g0 + k:g0 + CC:4],
                            ALU.is_ge, ALU.mult))
                    tspar[cc % 2][o4] = pr.op("vector", lambda e: e.engine_nop(), inc=[("v", 1)])
                if cc > 0:
                    emit_tail(cc - 1)
            emit_tail(NCC - 1)
            for o4 in range(HSLOTS):
                o = half * HSLOTS + o4
                txs6 = pr.dma_cols("scalar", x1rb,
                                   (lambda o: (lambda a, bb: x1sp_hbm[o][:, a:bb]))(o),
                                   D, 4, wait=list(x1_war) + [x1sp_toks[o]])
                tx = txs6[-1]
                for g in range(4):
                    pj = g % 2
                    for q in range(4):
                        m = g * 4 + q
                        tt = pr.op("tensor", lambda e, m=m, q=q, pj=pj, o4=o4: e.transpose(
                            ps_d[pj][:, q * 128:(q + 1) * 128],
                            accF[:, m * THW + o4 * 128: m * THW + (o4 + 1) * 128], ident),
                            inc=[("t", 1)] if q == 3 else (),
                            wait=[psfree_d[0], psfree_d[1]] if q == 0 else ())
                    pr.raw_wait("vector", tt)
                    psfree_d[pj] = pr.op("vector", lambda e, g=g, pj=pj: e.tensor_copy(
                        t2rb[:, g * 512:(g + 1) * 512], ps_d[pj]), inc=[("v", 1)])
                ln_inplace(t2rb, sqb, D, EPS_LN)
                pr.raw_wait("vector", tx)
                tadd = pr.op("vector", lambda e: e.tensor_add(x1rb, x1rb, t2rb), inc=[("v", 1)])
                tou = pr.dma("sync", lambda e, o=o: e.dma_start(out_hbm[o], x1rb),
                             wait=[tadd])
                out_toks.append(tou)
                x1_war = (tou,)
                t2rb_war = tadd
        for _t in out_toks:
            pr.raw_wait("sync", _t)
    emit_taps()
    pr.flush()

    return nc, es


def kernel(L0_attn_x, L0_attn_q_a_w, L0_attn_q_a_ln_w, L0_attn_q_b_w,
           L0_attn_kv_a_w, L0_attn_kv_a_ln_w, L0_attn_kv_b_w, L0_attn_o_w,
           L0_ffn_w_gate, L0_ffn_w_up, L0_ffn_w_down, _S=None):
    S = _S if _S is not None else 8192
    TT = S // 128
    SLOTS = TT // 8
    x = np.ascontiguousarray(np.asarray(L0_attn_x, dtype=np.float32).reshape(S, D))
    xt = x.reshape(TT, 128, D)

    f32 = lambda a: np.ascontiguousarray(np.asarray(a, dtype=np.float32))
    qa_w = f32(L0_attn_q_a_w)
    qb_w = f32(L0_attn_q_b_w)
    kva_w = f32(L0_attn_kv_a_w)
    kvb_w = f32(L0_attn_kv_b_w)
    o_w = f32(L0_attn_o_w)
    wg = f32(L0_ffn_w_gate)
    wu = f32(L0_ffn_w_up)
    wd = f32(L0_ffn_w_down)
    qa_ln = np.pad(np.asarray(L0_attn_q_a_ln_w, dtype=np.float64), (0, 768 - QLORA),
                   constant_values=1.0).reshape(6, 128).T.astype(np.float32)
    qa_ln = np.ascontiguousarray(qa_ln)
    kva_ln = np.ascontiguousarray(
        np.asarray(L0_attn_kv_a_ln_w, dtype=np.float32).reshape(2, 128).T)

    inv = 1.0 / THETA ** (np.arange(0, ROPE, 2, dtype=np.float64) / ROPE)
    tpos = np.arange(S, dtype=np.float64)
    fr = np.outer(tpos, inv)
    cosf = np.cos(fr).astype(np.float32).reshape(TT, 128, 32)
    sinf = np.sin(fr).astype(np.float32).reshape(TT, 128, 32)
    cosf = np.ascontiguousarray(cosf)
    sinf = np.ascontiguousarray(sinf)

    nc, es = build(S)

    in_maps = []
    for c in range(8):
        tiles = [8 * s + c for s in range(SLOTS)]
        x_own = np.ascontiguousarray(xt[tiles])
        cs_o = np.empty((128, 2 * SLOTS * 32), np.float32)
        for s, ti in enumerate(tiles):
            cs_o[:, s * 32:(s + 1) * 32] = cosf[ti]
            cs_o[:, SLOTS * 32 + s * 32: SLOTS * 32 + (s + 1) * 32] = sinf[ti]
        m = np.zeros((SLOTS, 8, 128, 128), np.float32)
        for s in range(SLOTS):
            for t in range(8):
                if t > c:
                    m[s, t] = NEG
                elif t == c:
                    m[s, t] = np.where(np.arange(128)[:, None] >= np.arange(128)[None, :],
                                       0.0, NEG)
        mask = np.ascontiguousarray(m.transpose(2, 0, 1, 3).reshape(128, SLOTS * 8 * 128)
                                    ).astype(ml_dtypes.bfloat16)
        in_maps.append({
            "x": xt, "x_own": x_own, "cos_f": cosf, "sin_f": sinf,
            "cs_o": np.ascontiguousarray(cs_o), "mask": mask,
            "qa_w": qa_w, "qa_ln": qa_ln, "qb_w": qb_w,
            "kva_w": kva_w, "kva_ln": kva_ln, "kvb_w": kvb_w, "o_w": o_w,
            "w_gate": wg, "w_up": wu, "w_down": wd,
        })

    res = run_bass_kernel_spmd(nc, in_maps, core_ids=list(range(8)))
    es.close()

    y = np.empty((TT, 128, D), np.float32)
    for c in range(8):
        oc = np.asarray(res.results[c]["out"])
        for s in range(SLOTS):
            y[8 * s + c] = oc[s]
    return y.reshape(1, S, D)
